# version 10
# baseline (speedup 1.0000x reference)
import numpy as np
import ml_dtypes
from contextlib import ExitStack

import concourse.bass as bass
import concourse.mybir as mybir
from concourse.bass_utils import run_bass_kernel_spmd

F32 = mybir.dt.float32
BF16 = mybir.dt.bfloat16
AF = mybir.ActivationFunctionType
ALU = mybir.AluOpType
AX = mybir.AxisListType
AP = bass.AP
bf16 = ml_dtypes.bfloat16

T = 4096
D = 1024
NT = 32
NH = 16
HD = 64
CSHIFT = 30.0
EPS_LN = 1e-5
EPS_RMS = 1e-6



class Prog:

    ENGINES = ("sync", "tensor", "vector", "scalar", "gpsimd")

    def __init__(self):
        self.counts = {}
        self.ticks = {}
        self.mode = "plan"
        self.funcs = []
        self.pfx = ""

    def body(self, engine, fn):
        self.funcs.append((engine, fn, self.pfx))

    def run_plan(self):
        self.mode = "plan"
        self.counts = {}
        for name, fn, pfx in self.funcs:
            fn(_PlanEng(self, name, pfx))

    def run_emit(self, nc, block, chans):
        self.mode = "emit"
        for name, fn, pfx in self.funcs:
            def make(fn=fn, name=name, pfx=pfx):
                def body(eng):
                    fn(_EmitEng(self, name, eng, chans, pfx))
                return body
            getattr(block, name)(make())

    def final(self, ch):
        return self.counts[ch]


class _PlanEng:
    def __init__(self, prog, name, pfx=""):
        self.prog = prog
        self.name = name
        self.pfx = pfx

    def op(self, fn, ch=None, key=None, k=1):
        if ch is not None:
            ch = self.pfx + ch
            c = self.prog.counts.get(ch, 0) + k
            self.prog.counts[ch] = c
            assert (ch, key) not in self.prog.ticks, f"dup tick {ch} {key}"
            self.prog.ticks[(ch, key)] = c

    def wait(self, ch, key=None, value=None):
        pass


class _EmitEng:
    def __init__(self, prog, name, eng, chans, pfx=""):
        self.prog = prog
        self.name = name
        self.eng = eng
        self.chans = chans
        self.pfx = pfx

    def op(self, fn, ch=None, key=None, k=1):
        ins = fn(self.eng)
        if ch is not None:
            ins.then_inc(self.chans[self.pfx + ch], k)

    def wait(self, ch, key=None, value=None):
        ch = self.pfx + ch
        if value is None:
            value = self.prog.ticks[(ch, key)]
        self.eng.wait_ge(self.chans[ch], value)


class Arena:

    def __init__(self, nc, ctx, size_bytes):
        import concourse.mybir as _mb
        self.nc = nc
        self.size = size_bytes
        self.handle = ctx.enter_context(
            nc.sbuf_tensor("arena", [nc.NUM_PARTITIONS, size_bytes], mybir.dt.uint8))
        self.base = nc.lookup_mloc(self.handle).addr
        self.off = 0
        self.n = 0

    def reset(self):
        self.off = 0

    def sb(self, name, shape, dtype=F32):
        per_part = 1
        for s in shape[1:]:
            per_part *= s
        per_part *= mybir.dt.size(dtype)
        addr = self.base + self.off
        self.off = (self.off + per_part + 31) // 32 * 32
        assert self.off <= self.size, f"arena overflow at {name}: {self.off}"
        self.n += 1
        return self.nc.alloc_sbuf_tensor_at(f"{name}_p{self.n}", shape, dtype,
                                            offset=addr)


def tick(prog, ch, key):
    return prog.ticks[(ch, key)]



def _l0_attn_masks():
    a = np.arange(128)
    A = a[:, None]
    B = a[None, :]
    s_far = (A % 64 == B % 64)
    s_d1 = (A - B > 64) | (A % 64 == B % 64)
    s_d0 = (A <= B) & ((B - A < 64) | (A % 64 == B % 64))
    f_far = np.broadcast_to(A % 64 == 63, (128, 128))
    f_d0 = (A <= B) & ((A // 64 == B // 64) | (A % 64 == 63))
    zero = np.zeros((128, 128), bool)

    def combos(far, d1, d0):
        return [
            [far, far, far, far],
            [d1, far, far, far],
            [d0, d1, far, far],
            [zero, d0, d1, far],
            [zero, zero, d0, d1],
            [zero, zero, zero, d0],
        ]
    out = []
    for far, d1, d0 in ((s_far, s_d1, s_d0), (f_far, f_far, f_d0)):
        for c in combos(far, d1, d0):
            out.append(np.concatenate(c, axis=1))
    return np.stack(out).astype(np.float32).astype(bf16)


def _mask_id(kt, qg, headclass):
    base = 4 * qg
    c = 0 if kt <= base - 2 else kt - base + 2
    return headclass * 6 + c


def prep_l0_attn(ins):
    return {
        "wqT": np.ascontiguousarray(ins["L0_attn_Wq"].T).astype(bf16),
        "wkT": np.ascontiguousarray(ins["L0_attn_Wk"].T).astype(bf16),
        "wvT": np.ascontiguousarray(ins["L0_attn_Wv"].T).astype(bf16),
        "woT": np.ascontiguousarray(ins["L0_attn_Wo"].T).astype(bf16),
        "ident": np.eye(128, dtype=np.float32).astype(bf16),
        "masks": _l0_attn_masks(),
        "epsv": np.full((128, 1), EPS_LN, np.float32),
    }



def emit_l0_attn(nc, prog, ar, aps, pool):
    x = aps["x"]
    wqT = aps["A_wqT"]
    wkT = aps["A_wkT"]
    wvT = aps["A_wvT"]
    woT = aps["A_woT"]
    ident = aps["A_ident"]
    masks = aps["A_masks"]
    epsv = aps["A_epsv"]
    y = aps["interA"]
    xnT_scr = aps["xnT_scr"]

    if True:
        sb = ar.sb

        s_masks = sb("s_masks", [128, 12 * 512], BF16)
        s_id = sb("s_id", [128, 128], BF16)
        s_wo = sb("s_wo", [128, 8 * D], BF16)
        s_wh = [sb(f"s_wh{j}", [128, 3 * 512], BF16) for j in range(2)]
        s_eps = sb("s_eps", [128, 1])
        s_x = [sb(f"s_x{j}", [128, D]) for j in range(2)]
        s_sq = sb("s_sq", [128, D])
        s_mean = [sb(f"s_mean{j}", [128, 1]) for j in range(2)]
        s_ss = sb("s_ss", [128, 1])
        s_nvar = sb("s_nvar", [128, 1])
        s_std = sb("s_std", [128, 1])
        s_rstd = [sb(f"s_rstd{j}", [128, 1]) for j in range(2)]
        s_negmb = [sb(f"s_negmb{j}", [128, 1]) for j in range(2)]
        s_xn = [sb(f"s_xn{j}", [128, D], BF16) for j in range(2)]
        s_xnT = [sb(f"s_xnT{j}", [128, D], BF16) for j in range(2)]
        s_xc = [sb(f"s_xc{j}", [128, 8 * 512], BF16) for j in range(2)]
        s_kT = sb("s_kT", [65, T], BF16)
        s_qT = sb("s_qT", [65, T], BF16)
        s_vT = sb("s_vT", [64, T], BF16)
        s_vtok = sb("s_vtok", [128, 32 * 65], BF16)
        s_pex = [sb(f"s_pex{j}", [128, 1024], BF16) for j in range(2)]
        s_pt = [sb(f"s_pt{j}", [128, 1024], BF16) for j in range(2)]
        s_oun = sb("s_oun", [65, 512])
        s_rden = sb("s_rden", [1, 512])
        s_ones65 = sb("s_ones65", [1, 65])
        o_all = [sb(f"o_all{c}", [128, T], BF16) for c in range(8)]
        s_out = [sb(f"s_out{j}", [128, D]) for j in range(2)]

        p_tr = pool["tr"]
        p_pair = [pool["w"][:, :], pool["w2"][:, :]]
        p_av = pool["f0"][0:65, :]
        p_bc = pool["f1"][0:65, 0:512]
        p_qkv = pool["f0"][0:64, :]
        p_proj = [pool["w"][:, 0:512], pool["w"][:, 512:1024]]

        def kt_list(qg):
            return list(range(min(NT, 4 * qg + 4)))

        def body_const(E):
            E.op(lambda e: e.dma_start(out=s_id[:, :], in_=ident),
                 ch="const", key="c0", k=16)
            E.op(lambda e: e.dma_start(out=s_eps[:, :], in_=epsv),
                 ch="const", key="c1", k=16)
            E.op(lambda e: e.dma_start(
                out=s_masks[:, :],
                in_=AP(masks.tensor, 0, [[512, 128], [128 * 512, 12], [1, 512]])),
                 ch="const", key="c2", k=16)
            E.op(lambda e: e.dma_start(
                out=s_wo[:, :],
                in_=AP(woT.tensor, 0, [[D, 128], [128 * D, 8], [1, D]])),
                 ch="const", key="loads", k=16)
        prog.body("sync", body_const)

        def body_const_g(E):
            E.wait("const", "loads")
            E.op(lambda e: e.memset(s_ones65[:, :], 1.0))
            E.op(lambda e: e.memset(AP(s_vtok, 64, [[2080, 128], [65, 32]]), 1.0))
            E.op(lambda e: e.memset(s_kT[64:65, :], 1.0))
            E.op(lambda e: e.memset(s_qT[64:65, :], -CSHIFT), ch="const", key="memsets")
        prog.body("gpsimd", body_const_g)

        def p1_sync(E):
            for i in range(NT):
                j = i % 2
                if i >= 2:
                    E.wait("s", ("xn", i - 2))
                E.op(lambda e, i=i, j=j: e.dma_start(
                    out=s_x[j][:, :], in_=x[i * 128:(i + 1) * 128, :]),
                    ch="x", key=("x", i), k=16)
        prog.body("sync", p1_sync)

        def p1_scalar(E):
            E.wait("const", "loads")
            for i in range(NT):
                j = i % 2
                E.wait("x", ("x", i))
                if i >= 1:
                    E.wait("v", ("ss", i - 1))
                E.op(lambda e, j=j: e.activation(s_sq[:, :], s_x[j][:, :], AF.Square),
                     ch="s", key=("sq", i))
                E.wait("v", ("nvar", i))
                if i >= 1:
                    E.wait("v", ("rstd", i - 1))
                E.op(lambda e: e.activation(s_std[:, :], s_nvar[:, :], AF.Sqrt,
                                            bias=s_eps[:, 0:1], scale=-1.0),
                     ch="s", key=("std", i))
                E.wait("v", ("negmb", i))
                if i >= 2:
                    E.wait("t", ("tr", 8 * (i - 1) - 1))
                E.op(lambda e, j=j: e.activation(s_xn[j][:, :], s_x[j][:, :], AF.Identity,
                                                 bias=s_negmb[j][:, 0:1],
                                                 scale=s_rstd[j][:, 0:1]),
                     ch="s", key=("xn", i))
        prog.body("scalar", p1_scalar)

        def p1_vector(E):
            for i in range(NT):
                j = i % 2
                E.wait("x", ("x", i))
                E.op(lambda e, j=j: e.reduce_sum(s_mean[j][:, :], s_x[j][:, :], axis=AX.X),
                     ch="v", key=("meansum", i))
                E.wait("v", ("meansum", i))
                E.op(lambda e, j=j: e.tensor_scalar_mul(s_mean[j][:, :], s_mean[j][:, :], 1.0 / D),
                     ch="v", key=("mean", i))
                E.wait("s", ("sq", i))
                E.op(lambda e: e.reduce_sum(s_ss[:, :], s_sq[:, :], axis=AX.X),
                     ch="v", key=("ss", i))
                E.wait("v", ("ss", i))
                E.op(lambda e: e.tensor_scalar_mul(s_ss[:, :], s_ss[:, :], 1.0 / D),
                     ch="v", key=("ssm", i))
                E.wait("v", ("mean", i))
                E.wait("v", ("ssm", i))
                E.op(lambda e, j=j: e.scalar_tensor_tensor(
                    s_nvar[:, :], s_mean[j][:, :], s_mean[j][:, 0:1], s_ss[:, :],
                    ALU.mult, ALU.subtract), ch="v", key=("nvar", i))
                E.wait("s", ("std", i))
                E.op(lambda e, j=j: e.reciprocal(s_rstd[j][:, :], s_std[:, :]),
                     ch="v", key=("rstd", i))
                E.wait("v", ("rstd", i))
                E.op(lambda e, j=j: e.tensor_scalar(
                    s_negmb[j][:, :], s_mean[j][:, :], s_rstd[j][:, 0:1], -1.0,
                    ALU.mult, ALU.mult), ch="v", key=("negmb", i))
                for c in range(8):
                    k = 8 * i + c
                    E.wait("t", ("tr", k))
                    if c == 0 and i >= 2:
                        E.wait("scr", ("w", i - 2))
                    E.op(lambda e, j=j, c=c: e.tensor_copy(
                        s_xnT[j][:, c * 128:(c + 1) * 128], p_tr[c % 2][:, :]),
                        ch="v", key=("ev", k))
        prog.body("vector", p1_vector)

        def p1_tensor(E):
            for i in range(NT):
                j = i % 2
                E.wait("s", ("xn", i))
                for c in range(8):
                    k = 8 * i + c
                    if k >= 2:
                        E.wait("v", ("ev", k - 2))
                    E.op(lambda e, j=j, c=c: e.transpose(
                        p_tr[c % 2][:, :], s_xn[j][:, c * 128:(c + 1) * 128], s_id[:, :]),
                        ch="t", key=("tr", k))
        prog.body("tensor", p1_tensor)

        def p1_gpsimd(E):
            E.wait("const", "memsets")
            for i in range(NT):
                j = i % 2
                E.wait("v", ("ev", 8 * i + 7))
                E.op(lambda e, i=i, j=j: e.dma_start(
                    out=AP(xnT_scr.tensor, i * 128, [[T, 128], [128 * T, 8], [1, 128]]),
                    in_=s_xnT[j][:, :]), ch="scr", key=("w", i), k=16)
        prog.body("gpsimd", p1_gpsimd)

        for h in range(NH):
            def p2_sync(E, h=h):
                jw = h % 2
                if h >= 2:
                    E.wait("t", ("qkv", h - 1, 7, "V"))
                for w_d, poff in ((wkT, 0), (wqT, 512), (wvT, 1024)):
                    E.op(lambda e, w_d=w_d, poff=poff, jw=jw: e.dma_start(
                        out=s_wh[jw][:, poff:poff + 512],
                        in_=AP(w_d.tensor, h * HD, [[D, 128], [128 * D, 8], [1, HD]])),
                        ch="xc", key=("wh", h, poff), k=16)
                for tg in range(8):
                    jb = tg % 2
                    if h == 0 and tg == 0:
                        E.wait("scr", ("w", NT - 1))
                    prev = (h, tg - 2) if tg >= 2 else ((h - 1, 6 + tg) if h >= 1 else None)
                    if prev is not None:
                        E.wait("t", ("qkv", prev[0], prev[1], "V"))
                    E.op(lambda e, tg=tg, jb=jb: e.dma_start(
                        out=s_xc[jb][:, :],
                        in_=AP(xnT_scr.tensor, tg * 512, [[T, 128], [128 * T, 8], [1, 512]])),
                        ch="xc", key=("xc", h, tg), k=16)
            prog.body("sync", p2_sync)

            def p2_tensor_qkv(E, h=h):
                for tg in range(8):
                    jb = tg % 2
                    E.wait("xc", ("xc", h, tg))
                    if h > 0 and tg == 0:
                        lq, lk = _last_qgkt(kt_list)
                        E.wait("t", ("av", h - 1, lq, lk))
                        E.wait("s", ("oun", h - 1, 7))
                    if tg == 0:
                        E.wait("xc", ("wh", h, 1024))
                    for proj, poff in (("K", 0), ("Q", 512), ("V", 1024)):
                        prev = _prev_proj(h, tg, proj)
                        if prev is not None:
                            E.wait("s", ("qkve",) + prev)
                        for c in range(8):
                            last = (c == 7)
                            E.op(lambda e, c=c, poff=poff, jb=jb, jw=h % 2: e.matmul(
                                p_qkv,
                                s_wh[jw][:, poff + c * HD: poff + (c + 1) * HD],
                                s_xc[jb][:, c * 512:(c + 1) * 512],
                                start=(c == 0), stop=(c == 7), skip_group_check=True),
                                ch=("t" if last else None),
                                key=(("qkv", h, tg, proj) if last else None))
            prog.body("tensor", p2_tensor_qkv)

            def p2_scalar_qkv(E, h=h):
                for tg in range(8):
                    for proj in ("K", "Q", "V"):
                        E.wait("t", ("qkv", h, tg, proj))
                        if proj == "K":
                            E.op(lambda e, tg=tg: e.copy(
                                s_kT[0:64, tg * 512:(tg + 1) * 512], p_qkv),
                                ch="s", key=("qkve", h, tg, "K"))
                        elif proj == "Q":
                            E.op(lambda e, tg=tg: e.activation(
                                s_qT[0:64, tg * 512:(tg + 1) * 512], p_qkv,
                                AF.Identity, scale=0.125),
                                ch="s", key=("qkve", h, tg, "Q"))
                        else:
                            E.op(lambda e, tg=tg: e.copy(
                                s_vT[0:64, tg * 512:(tg + 1) * 512], p_qkv),
                                ch="s", key=("qkve", h, tg, "V"))
            prog.body("scalar", p2_scalar_qkv)

            def p2_tensor_vtr(E, h=h):
                E.wait("s", ("qkve", h, 7, "V"))
                for i in range(NT):
                    if i >= 2:
                        E.wait("v", ("vtok", h, i - 2))
                    if h == 0 and i < 2:
                        E.wait("v", ("ev", 8 * (NT - 1) + 6 + i))
                    elif h >= 1 and i < 2:
                        E.wait("v", ("vtok", h - 1, NT - 2 + i))
                    E.op(lambda e, i=i: e.transpose(
                        p_tr[i % 2][:, 0:64], s_vT[0:64, i * 128:(i + 1) * 128],
                        s_id[0:64, 0:64]), ch="t", key=("vtr", h, i))
            prog.body("tensor", p2_tensor_vtr)

            def p2_vector_vtok(E, h=h):
                for i in range(NT):
                    E.wait("t", ("vtr", h, i))
                    E.op(lambda e, i=i: e.tensor_copy(
                        s_vtok[:, 65 * i:65 * i + 64], p_tr[i % 2][:, 0:64]),
                        ch="v", key=("vtok", h, i))
            prog.body("vector", p2_vector_vtok)

            seq = []
            for qg in range(8):
                for kt in kt_list(qg):
                    seq.append((qg, kt))

            def p2_tensor_att(E, h=h, seq=seq):
                pairs = [(seq[k], seq[k + 1]) for k in range(0, len(seq), 2)]
                n = len(pairs)
                for step in range(n + 1):
                    if step < n:
                        p = step
                        for half in range(2):
                            qg, kt = pairs[p][half]
                            if p >= 2 and half == 0:
                                E.wait("s", ("exp", h, p - 2))
                                E.wait("v", ("pt", h, p - 2))
                            if p == 0 and half == 0:
                                E.wait("s", ("qkve", h, 7, "Q"))
                                E.wait("v", ("vtok", h, NT - 1))
                            E.op(lambda e, p=p, half=half, kt=kt, qg=qg: e.matmul(
                                p_pair[p % 2][:, half * 512:(half + 1) * 512],
                                s_kT[:, kt * 128:(kt + 1) * 128],
                                s_qT[:, qg * 512:(qg + 1) * 512],
                                start=True, stop=True, skip_group_check=True),
                                ch=("t" if half == 1 else None),
                                key=(("st", h, p) if half == 1 else None))
                    if step >= 1:
                        p = step - 1
                        for half in range(2):
                            qg, kt = pairs[p][half]
                            kts = kt_list(qg)
                            n_kt = kts.index(kt)
                            E.wait("v", ("pt", h, p))
                            if n_kt == 0:
                                E.wait("s", ("qkve", h, 7, "V"))
                                if h > 0 or qg > 0:
                                    E.wait("s", ("oun",) + _prev_qg(h, qg))
                            E.op(lambda e, p=p, half=half, kt=kt,
                                 n_kt=n_kt, last=(n_kt == len(kts) - 1): e.matmul(
                                p_av[:, :], s_vtok[:, 65 * kt:65 * (kt + 1)],
                                s_pt[p % 2][:, half * 512:(half + 1) * 512],
                                start=(n_kt == 0), stop=last, skip_group_check=True),
                                ch="t", key=("av", h, qg, kt))
                            if kt == kts[-1]:
                                E.wait("v", ("rden", h, qg))
                                if (h, qg) != (0, 0):
                                    E.wait("v", ("onorm",) + _prev_qg(h, qg))
                                E.op(lambda e: e.matmul(p_bc[:, :], s_ones65[:, :],
                                                        s_rden[:, :],
                                                        start=True, stop=True,
                                                        skip_group_check=True),
                                     ch="t", key=("bc", h, qg))
            prog.body("tensor", p2_tensor_att)

            def p2_scalar_att(E, h=h, seq=seq):
                pairs = [(seq[k], seq[k + 1]) for k in range(0, len(seq), 2)]
                for p in range(len(pairs)):
                    E.wait("t", ("st", h, p))
                    if p >= 2:
                        E.wait("v", ("pt", h, p - 2))
                    E.op(lambda e, p=p: e.activation(s_pex[p % 2][:, :],
                                                     p_pair[p % 2][:, :], AF.Exp),
                         ch="s", key=("exp", h, p))
                    for half in range(2):
                        qg, kt = pairs[p][half]
                        kts = kt_list(qg)
                        if kt == kts[-1]:
                            E.wait("t", ("av", h, qg, kt))
                            if (h, qg) != (0, 0):
                                E.wait("v", ("onorm",) + _prev_qg(h, qg))
                            E.op(lambda e: e.copy(s_oun[:, :], p_av[:, :]),
                                 ch="s", key=("oun", h, qg))
            prog.body("scalar", p2_scalar_att)

            def p2_vector_att(E, h=h, seq=seq):
                pairs = [(seq[k], seq[k + 1]) for k in range(0, len(seq), 2)]
                for p in range(len(pairs)):
                    E.wait("s", ("exp", h, p))
                    if p >= 2:
                        E.wait("t", ("av", h) + pairs[p - 2][1])
                    for half in range(2):
                        qg, kt = pairs[p][half]
                        mid = _mask_id(kt, qg, 0 if h < 8 else 1)
                        E.op(lambda e, p=p, half=half, mid=mid: e.tensor_mul(
                            s_pt[p % 2][:, half * 512:(half + 1) * 512],
                            s_pex[p % 2][:, half * 512:(half + 1) * 512],
                            s_masks[:, mid * 512:(mid + 1) * 512]),
                            ch=("v" if half == 1 else None),
                            key=(("pt", h, p) if half == 1 else None))
                    for half in range(2):
                        qg, kt = pairs[p][half]
                        kts = kt_list(qg)
                        if kt == kts[-1]:
                            E.wait("s", ("oun", h, qg))
                            E.op(lambda e: e.reciprocal(s_rden[:, :], s_oun[64:65, :]),
                                 ch="v", key=("rden", h, qg))
                            E.wait("t", ("bc", h, qg))
                            E.op(lambda e, h=h, qg=qg: e.tensor_mul(
                                o_all[h // 2][64 * (h % 2):64 * (h % 2) + 64,
                                              qg * 512:(qg + 1) * 512],
                                s_oun[0:64, :], p_bc[0:64, :]),
                                ch="v", key=("onorm", h, qg))
            prog.body("vector", p2_vector_att)

        def p3_sync(E):
            for i in range(NT):
                j = i % 2
                if i >= 2:
                    E.wait("v", ("add", i - 2))
                elif i == 0:
                    E.wait("s", ("xn", NT - 1))
                E.op(lambda e, i=i, j=j: e.dma_start(
                    out=s_x[j][:, :], in_=x[i * 128:(i + 1) * 128, :]),
                    ch="x3", key=("x3", i), k=16)
        prog.body("sync", p3_sync)

        def p3_tensor(E):
            E.wait("v", ("onorm", NH - 1, 7))
            E.wait("s", ("oun", NH - 1, 7))
            for i in range(NT):
                if i >= 1:
                    E.wait("v", ("add", i - 1))
                for half in range(2):
                    for c in range(8):
                        last = (half == 1 and c == 7)
                        E.op(lambda e, i=i, half=half, c=c: e.matmul(
                            p_proj[half],
                            o_all[c][:, i * 128:(i + 1) * 128],
                            s_wo[:, c * D + half * 512: c * D + (half + 1) * 512],
                            start=(c == 0), stop=(c == 7), skip_group_check=True),
                            ch=("t" if last else None),
                            key=(("proj", i) if last else None))
        prog.body("tensor", p3_tensor)

        def p3_vector(E):
            for i in range(NT):
                j = i % 2
                E.wait("t", ("proj", i))
                E.wait("x3", ("x3", i))
                if i >= 2:
                    E.wait("st", ("y", i - 2))
                E.op(lambda e, j=j: e.tensor_add(s_out[j][:, 0:512], s_x[j][:, 0:512],
                                                 p_proj[0]), ch="v", key=("adda", i))
                E.op(lambda e, j=j: e.tensor_add(s_out[j][:, 512:1024], s_x[j][:, 512:1024],
                                                 p_proj[1]), ch="v", key=("add", i))
        prog.body("vector", p3_vector)

        def p3_gpsimd(E):
            for i in range(NT):
                j = i % 2
                E.wait("v", ("add", i))
                E.op(lambda e, i=i, j=j: e.dma_start(
                    out=y[i * 128:(i + 1) * 128, :], in_=s_out[j][:, :]),
                    ch="st", key=("y", i), k=16)
            E.wait("st", ("y", NT - 1))
        prog.body("gpsimd", p3_gpsimd)



def _last_qgkt(kt_list):
    kts = kt_list(7)
    return 7, kts[-1]


def _prev_proj(h, tg, proj):
    order = ["K", "Q", "V"]
    idx = order.index(proj)
    if idx > 0:
        return (h, tg, order[idx - 1])
    if tg > 0:
        return (h, tg - 1, "V")
    if h > 0:
        return (h - 1, 7, "V")
    return None


def _prev_qg(h, qg):
    if qg > 0:
        return (h, qg - 1)
    return (h - 1, 7)




def prep_l0_ffn(ins):
    wup = ins["L0_ffn_w_up"]
    wdn = ins["L0_ffn_w_down"]
    wrt = ins["L0_ffn_w_router"]
    ns = ins["L0_ffn_norm_scale"]
    wupT = np.ascontiguousarray(wup.transpose(0, 2, 1)).astype(bf16)
    wdnT = np.ascontiguousarray(wdn.transpose(0, 2, 1)).astype(bf16)
    wrtC = np.ascontiguousarray(wrt.T.reshape(8, 128, 8).transpose(1, 0, 2)
                                .reshape(128, 64)).astype(bf16)
    nsC = np.ascontiguousarray(ns.reshape(8, 128).T).astype(np.float32)
    return {
        "wup": wupT, "wdn": wdnT, "wrt": wrtC, "nsc": nsC,
        "ident": np.eye(128, dtype=np.float32).astype(bf16),
        "epsv": np.full((128, 1), EPS_LN, np.float32),
        "epsr": np.full((128, 1), EPS_RMS, np.float32),
    }


def emit_l0_ffn(nc, prog, ar, aps, pool):
    HHC = 11
    NE = 8
    NTG = 8
    with_phase2 = True
    dump_xn = False
    x = aps["interA"]
    wup = aps["B_wup"]
    wdn = aps["B_wdn"]
    wrt = aps["B_wrt"]
    nsc = aps["B_nsc"]
    ident = aps["B_ident"]
    epsv = aps["B_epsv"]
    epsr = aps["B_epsr"]
    y = aps["interB"]
    dxn = None
    xnT2 = aps["xnT2"]

    if True:
        sb = ar.sb

        s_id = sb("s_id", [128, 128], BF16)
        s_wrt = sb("s_wrt", [128, 64], BF16)
        s_nsc = sb("s_nsc", [128, 8])
        s_eps = sb("s_eps", [128, 1])
        s_epsr = sb("s_epsr", [128, 1])
        s_x = [sb(f"s_x{j}", [128, D]) for j in range(2)]
        s_ln = [sb(f"s_ln{j}", [128, D]) for j in range(2)]
        s_sq = sb("s_sq", [128, D])
        s_mean = [sb(f"s_mean{j}", [128, 1]) for j in range(2)]
        s_ss = sb("s_ss", [128, 1])
        s_nvar = sb("s_nvar", [128, 1])
        s_std = sb("s_std", [128, 1])
        s_rstd = [sb(f"s_rstd{j}", [128, 1]) for j in range(2)]
        s_negmb = [sb(f"s_negmb{j}", [128, 1]) for j in range(2)]
        s_ms = sb("s_ms", [128, 1])
        s_rstd2 = [sb(f"s_rstd2{j}", [128, 1]) for j in range(2)]
        s_xn = [sb(f"s_xn{j}", [128, D], BF16) for j in range(2)]
        s_xnT = [sb(f"s_xnT{j}", [128, D], BF16) for j in range(2)]
        s_tp8 = sb("s_tp8", [128, 8])
        s_negmax = sb("s_negmax", [128, 1])
        s_pe = sb("s_pe", [128, 8])
        s_pd = sb("s_pd", [128, 1])
        s_pr = sb("s_pr", [128, 1])
        s_msk = sb("s_msk", [128, 8])
        s_gun = sb("s_gun", [128, 8])
        s_g = sb("s_g", [128, NT * 8])
        s_acc = sb("s_acc", [128, 16 * D])
        s_wup = sb("s_wup", [128, 8 * 2816], BF16)
        s_wdn = sb("s_wdn", [128, HHC * D], BF16)
        s_xs = [sb(f"s_xs{j}", [128, 8 * 256], BF16) for j in range(2)]
        s_sil = [sb(f"s_sil{j}", [128, 512], BF16) for j in range(2)]
        s_act = [sb(f"s_act{j}", [128, 512], BF16) for j in range(2)]

        p_tr = pool["tr"]
        pv = [pool["f0"][:, :], pool["b6"][:, :]]
        pg = [pool["f1"][:, :], pool["b7"][:, :]]
        ps_sc = pool["f0"][:, 0:8]
        p_yv = [[pool["w"][:, 0:512], pool["w"][:, 512:1024]],
                [pool["f2"][:, :], pool["f3"][:, :]]]

        def body_const(E):
            E.op(lambda e: e.dma_start(out=s_id[:, :], in_=ident), ch="const", key="c0", k=16)
            E.op(lambda e: e.dma_start(out=s_wrt[:, :], in_=wrt), ch="const", key="c1", k=16)
            E.op(lambda e: e.dma_start(out=s_nsc[:, :], in_=nsc), ch="const", key="c2", k=16)
            E.op(lambda e: e.dma_start(out=s_eps[:, :], in_=epsv), ch="const", key="c3", k=16)
            E.op(lambda e: e.dma_start(out=s_epsr[:, :], in_=epsr), ch="const", key="loads", k=16)
        prog.body("sync", body_const)

        for th in range(2):
            TOFF = th * 2048

            def p1_sync(E, th=th):
                for i in range(16):
                    j = i % 2
                    gi = th * 16 + i
                    if th == 0 and i == 0:
                        E.wait("const", "loads")
                    if i >= 2:
                        E.wait("s", ("xn", th, i - 2))
                    elif th == 1:
                        E.wait("s", ("xn", 0, 14 + i))
                    if th == 1 and i == 0:
                        E.wait("yst", ("y", 0, 15))
                    if th == 1 and i < 2 and not with_phase2:
                        pass
                    E.op(lambda e, i=i, j=j, th=th: e.dma_start(
                        out=s_x[j][:, :],
                        in_=x[(th * 16 + i) * 128:(th * 16 + i + 1) * 128, :]),
                        ch="x", key=("x", th, i), k=16)
            prog.body("sync", p1_sync)

            def p1_scalar(E, th=th):
                for i in range(16):
                    j = i % 2
                    E.wait("x", ("x", th, i))
                    if (th, i) != (0, 0):
                        E.wait("v", ("ssm", *_previ(th, i)))
                    E.op(lambda e, j=j: e.activation(s_sq[:, :], s_x[j][:, :], AF.Square),
                         ch="s", key=("sq", th, i))
                    E.wait("v", ("nvar", th, i))
                    if (th, i) != (0, 0):
                        E.wait("v", ("rstd2", *_previ(th, i)))
                    E.op(lambda e: e.activation(s_std[:, :], s_nvar[:, :], AF.Sqrt,
                                                bias=s_eps[:, 0:1], scale=-1.0),
                         ch="s", key=("std", th, i))
                    E.wait("v", ("negmb", th, i))
                    if (th, i) not in ((0, 0), (0, 1)):
                        E.wait("v", ("acc0", *_previ(th, i, 2)))
                        E.wait("s", ("xn", *_previ(th, i, 2)))
                    E.op(lambda e, j=j: e.activation(s_ln[j][:, :], s_x[j][:, :], AF.Identity,
                                                     bias=s_negmb[j][:, 0:1],
                                                     scale=s_rstd[j][:, 0:1]),
                         ch="s", key=("ln", th, i))
                    E.wait("s", ("ln", th, i))
                    E.op(lambda e, j=j: e.activation(s_sq[:, :], s_ln[j][:, :], AF.Square),
                         ch="s", key=("sq2", th, i))
                    E.wait("v", ("msm", th, i))
                    E.wait("v", ("rstd", th, i))
                    E.op(lambda e: e.activation(s_std[:, :], s_ms[:, :], AF.Sqrt,
                                                bias=s_epsr[:, 0:1]),
                         ch="s", key=("std2", th, i))
                    E.wait("v", ("rstd2", th, i))
                    if (th, i) not in ((0, 0), (0, 1)):
                        E.wait("t", ("tr", *_previ(th, i, 2), 7))
                    E.op(lambda e, j=j: e.activation(s_xn[j][:, :], s_ln[j][:, :], AF.Identity,
                                                     scale=s_rstd2[j][:, 0:1]),
                         ch="s", key=("xn", th, i))
                    E.wait("v", ("negmax", th, i))
                    E.op(lambda e: e.activation(s_pe[:, :], ps_sc, AF.Exp,
                                                bias=s_negmax[:, 0:1]),
                         ch="s", key=("pex", th, i))
            prog.body("scalar", p1_scalar)

            def p1_vector(E, th=th):
                for i in range(16):
                    j = i % 2
                    gi = th * 16 + i
                    E.wait("x", ("x", th, i))
                    E.op(lambda e, j=j: e.reduce_sum(s_mean[j][:, :], s_x[j][:, :], axis=AX.X),
                         ch="v", key=("meansum", th, i))
                    E.wait("v", ("meansum", th, i))
                    E.op(lambda e, j=j: e.tensor_scalar_mul(s_mean[j][:, :], s_mean[j][:, :],
                                                            1.0 / D), ch="v", key=("mean", th, i))
                    E.wait("s", ("sq", th, i))
                    E.op(lambda e: e.reduce_sum(s_ss[:, :], s_sq[:, :], axis=AX.X),
                         ch="v", key=("ss", th, i))
                    E.wait("v", ("ss", th, i))
                    E.op(lambda e: e.tensor_scalar_mul(s_ss[:, :], s_ss[:, :], 1.0 / D),
                         ch="v", key=("ssm", th, i))
                    E.wait("v", ("ssm", th, i))
                    E.op(lambda e, j=j: e.scalar_tensor_tensor(
                        s_nvar[:, :], s_mean[j][:, :], s_mean[j][:, 0:1], s_ss[:, :],
                        ALU.mult, ALU.subtract), ch="v", key=("nvar", th, i))
                    E.wait("s", ("std", th, i))
                    E.op(lambda e, j=j: e.reciprocal(s_rstd[j][:, :], s_std[:, :]),
                         ch="v", key=("rstd", th, i))
                    E.wait("v", ("rstd", th, i))
                    E.op(lambda e, j=j: e.tensor_scalar(
                        s_negmb[j][:, :], s_mean[j][:, :], s_rstd[j][:, 0:1], -1.0,
                        ALU.mult, ALU.mult), ch="v", key=("negmb", th, i))
                    E.wait("s", ("ln", th, i))
                    E.op(lambda e, j=j, i=i: e.tensor_add(
                        s_acc[:, i * D:(i + 1) * D], s_x[j][:, :], s_ln[j][:, :]),
                        ch="v", key=("acc0", th, i))
                    E.wait("s", ("sq2", th, i))
                    E.op(lambda e: e.reduce_sum(s_ms[:, :], s_sq[:, :], axis=AX.X),
                         ch="v", key=("msr", th, i))
                    E.wait("v", ("msr", th, i))
                    E.op(lambda e: e.tensor_scalar_mul(s_ms[:, :], s_ms[:, :], 1.0 / D),
                         ch="v", key=("msm", th, i))
                    E.wait("s", ("std2", th, i))
                    E.op(lambda e, j=j: e.reciprocal(s_rstd2[j][:, :], s_std[:, :]),
                         ch="v", key=("rstd2", th, i))
                    for c in range(8):
                        E.wait("t", ("tr", th, i, c))
                        if (c <= 1) and (th, i) not in ((0, 0), (0, 1)):
                            E.wait("scr", ("w", *_previ(th, i, 2)))
                        E.op(lambda e, j=j, c=c: e.tensor_scalar(
                            s_xnT[j][:, c * 128:(c + 1) * 128], p_tr[c % 2][:, :],
                            s_nsc[:, c:c + 1], None, ALU.mult),
                            ch="v", key=("ev", th, i, c))
                    E.wait("t", ("scm", th, i))
                    E.op(lambda e: e.max(out=s_tp8[:, :], in_=ps_sc),
                         ch="v", key=("tp8", th, i))
                    E.wait("v", ("tp8", th, i))
                    E.op(lambda e: e.tensor_scalar_mul(s_negmax[:, :], s_tp8[:, 0:1], -1.0),
                         ch="v", key=("negmax", th, i))
                    E.wait("s", ("pex", th, i))
                    E.op(lambda e: e.reduce_sum(s_pd[:, :], s_pe[:, :], axis=AX.X),
                         ch="v", key=("pd", th, i))
                    E.wait("v", ("pd", th, i))
                    E.op(lambda e: e.reciprocal(s_pr[:, :], s_pd[:, :]),
                         ch="v", key=("pr", th, i))
                    E.op(lambda e: e.tensor_scalar(s_msk[:, :], ps_sc, s_tp8[:, 1:2],
                                                   None, ALU.is_ge),
                         ch="v", key=("msk", th, i))
                    E.wait("v", ("msk", th, i))
                    E.op(lambda e: e.tensor_mul(s_gun[:, :], s_pe[:, :], s_msk[:, :]),
                         ch="v", key=("gun", th, i))
                    E.wait("v", ("gun", th, i))
                    E.wait("v", ("pr", th, i))
                    E.op(lambda e, gi=gi: e.tensor_scalar(
                        s_g[:, gi * 8:(gi + 1) * 8], s_gun[:, :], s_pr[:, 0:1], None,
                        ALU.mult), ch="v", key=("g", th, i))
            prog.body("vector", p1_vector)

            def p1_tensor(E, th=th):
                for i in range(16):
                    j = i % 2
                    gi = th * 16 + i
                    E.wait("s", ("xn", th, i))
                    for c in range(8):
                        k = 16 * th * 8 + 8 * i + c
                        if (th, i, c) > (0, 0, 1):
                            pc = _prevc(th, i, c, 2)
                            E.wait("v", ("ev",) + pc)
                        E.op(lambda e, j=j, c=c: e.transpose(
                            p_tr[c % 2][:, :], s_xn[j][:, c * 128:(c + 1) * 128], s_id[:, :]),
                            ch="t", key=("tr", th, i, c))
                    if th == 0 and i == 0:
                        E.wait("const", "loads")
                    else:
                        E.wait("v", ("msk", *_previ(th, i)))
                        E.wait("s", ("pex", *_previ(th, i)))
                    if (th, i) == (1, 0) and with_phase2:
                        E.wait("v", ("accend", 0))
                    for c in range(8):
                        E.wait("v", ("ev", th, i, c))
                        E.op(lambda e, j=j, c=c: e.matmul(
                            ps_sc, s_xnT[j][:, c * 128:(c + 1) * 128],
                            s_wrt[:, c * 8:(c + 1) * 8],
                            start=(c == 0), stop=(c == 7), skip_group_check=True),
                            ch=("t" if c == 7 else None),
                            key=(("scm", th, i) if c == 7 else None))
            prog.body("tensor", p1_tensor)

            def p1_gpsimd(E, th=th):
                for i in range(16):
                    j = i % 2
                    gi = th * 16 + i
                    E.wait("v", ("ev", th, i, 7))
                    E.op(lambda e, gi=gi, j=j: e.dma_start(
                        out=AP(xnT2.tensor, gi * 128, [[T, 128], [128 * T, 8], [1, 128]]),
                        in_=s_xnT[j][:, :]), ch="scr", key=("w", th, i), k=16)
            prog.body("gpsimd", p1_gpsimd)

            def hcg_of(ew, tg, hc):
                return (ew * NTG + tg) * HHC + hc

            SP = ((0, 1), (2, 3), (4, 5), (6, 7), (8, 9), (10,))
            NSP = 6

            def gs_of(ew, tg, si):
                return (ew * NTG + tg) * NSP + si

            NEW = NE * 2

            QS = ((0, 2), (3, 5), (6, 8), (9, 10))

            def p2_sync(E, th=th):
                for ew in range(NEW):
                    e_, wh = divmod(ew, 2)
                    ub = e_ * D * 5632 + wh * 1408
                    db = e_ * 2816 * D + wh * 1408 * D
                    for q, (hs, he) in enumerate(QS):
                        he_s = {2: 1, 5: 2, 8: 4, 10: 5}[he]
                        if ew == 0:
                            if q == 0:
                                E.wait("v", ("g", th, 15))
                            if th == 1:
                                E.wait("t", ("gmm", 0, gs_of(NEW - 1, NTG - 1, he_s)))
                                E.wait("t", ("dmm", 0, gs_of(NEW - 1, NTG - 1, he_s)))
                        else:
                            E.wait("t", ("gmm", th, gs_of(ew - 1, NTG - 1, he_s)))
                            E.wait("t", ("dmm", th, gs_of(ew - 1, NTG - 1, he_s)))
                        nc_ = (he + 1 - hs) * 128
                        E.op(lambda e, ub=ub, hs=hs, nc_=nc_: e.dma_start(
                            out=AP(s_wup, hs * 128, [[8 * 2816, 128], [2816, 8], [1, nc_]]),
                            in_=AP(wup.tensor, ub + hs * 128,
                                   [[5632, 128], [128 * 5632, 8], [1, nc_]])),
                            ch="wt", key=("wv", th, ew, q), k=16)
                        E.op(lambda e, ub=ub, hs=hs, nc_=nc_: e.dma_start(
                            out=AP(s_wup, 1408 + hs * 128,
                                   [[8 * 2816, 128], [2816, 8], [1, nc_]]),
                            in_=AP(wup.tensor, ub + 2816 + hs * 128,
                                   [[5632, 128], [128 * 5632, 8], [1, nc_]])),
                            ch="wt", key=("wg", th, ew, q), k=16)
                        nhc = he + 1 - hs
                        E.op(lambda e, db=db, hs=hs, nhc=nhc: e.dma_start(
                            out=AP(s_wdn, hs * D, [[HHC * D, 128], [D, nhc], [1, D]]),
                            in_=AP(wdn.tensor, db + hs * 128 * D,
                                   [[D, 128], [128 * D, nhc], [1, D]])),
                            ch="wt", key=("wd", th, ew, q), k=16)
            if with_phase2:
                prog.body("sync", p2_sync)

            def p2_gpsimd(E, th=th):
                for ew in range(NEW):
                    for tg in range(NTG):
                        jb = tg % 2
                        if ew == 0 and tg == 0:
                            E.wait("scr", ("w", th, 15))
                        if tg >= 2:
                            E.wait("t", ("gmm", th, gs_of(ew, tg - 2, NSP - 1)))
                        elif ew >= 1:
                            E.wait("t", ("gmm", th, gs_of(ew - 1, NTG - 2 + tg, NSP - 1)))
                        E.op(lambda e, tg=tg, jb=jb, th=th: e.dma_start(
                            out=s_xs[jb][:, :],
                            in_=AP(xnT2.tensor, th * 2048 + tg * 256,
                                   [[T, 128], [128 * T, 8], [1, 256]])),
                            ch="xs", key=("xs", th, ew, tg), k=16)
            if with_phase2:
                prog.body("gpsimd", p2_gpsimd)

            def p2_tensor(E, th=th):
                for ew in range(NEW):
                    e_, wh = divmod(ew, 2)
                    for tg in range(NTG):
                        jb = tg % 2
                        E.wait("xs", ("xs", th, ew, tg))
                        def emit_down(E, si, hcs, tg=tg, ew=ew):
                            gs = gs_of(ew, tg, si)
                            if tg == 0:
                                qd = {0: 0, 1: 1, 2: 1, 3: 2, 4: 3, 5: 3}[si]
                                E.wait("wt", ("wd", th, ew, qd))
                            E.wait("v", ("act", th, gs))
                            if si == 0:
                                if tg >= 1:
                                    E.wait("v", ("acc", th, ew, tg - 1))
                                elif ew >= 1:
                                    E.wait("v", ("acc", th, ew - 1, NTG - 1))
                                elif th == 1:
                                    E.wait("v", ("acc", 0, NEW - 1, NTG - 1))
                            for k_, hc in enumerate(hcs):
                                for sub in range(2):
                                    for nh in range(2):
                                        last = (k_ == len(hcs) - 1 and sub == 1 and nh == 1)
                                        E.op(lambda e, sub=sub, nh=nh, hc=hc, si=si,
                                             k_=k_: e.matmul(
                                            p_yv[sub][nh],
                                            s_act[si % 2][:, k_ * 256 + sub * 128:
                                                          k_ * 256 + (sub + 1) * 128],
                                            s_wdn[:, hc * D + nh * 512: hc * D + (nh + 1) * 512],
                                            start=(hc == 0), stop=(hc == HHC - 1),
                                            skip_group_check=True),
                                            ch=("t" if last else None),
                                            key=(("dmm", th, gs) if last else None))

                        pend = None
                        for si, hcs in enumerate(SP):
                            gs = gs_of(ew, tg, si)
                            LASTGS = gs_of(NEW - 1, NTG - 1, NSP - 1)
                            if tg == 0 and si in (0, 1, 3, 4):
                                q = {0: 0, 1: 1, 3: 2, 4: 3}[si]
                                E.wait("wt", ("wv", th, ew, q))
                                E.wait("wt", ("wg", th, ew, q))
                            if gs >= 2:
                                E.wait("v", ("act", th, gs - 2))
                            elif th == 1:
                                E.wait("v", ("act", 0, LASTGS - 1 + gs))
                            if gs == 0:
                                E.wait("v", ("msk", th, 15))
                                E.wait("s", ("pex", th, 15))
                            for k_, hc in enumerate(hcs):
                                for c in range(8):
                                    last = (k_ == len(hcs) - 1 and c == 7)
                                    E.op(lambda e, c=c, hc=hc, jb=jb, si=si, k_=k_: e.matmul(
                                        pv[si % 2][:, k_ * 256:(k_ + 1) * 256],
                                        s_wup[:, c * 2816 + hc * 128: c * 2816 + (hc + 1) * 128],
                                        s_xs[jb][:, c * 256:(c + 1) * 256],
                                        start=(c == 0), stop=(c == 7), skip_group_check=True),
                                        ch=("t" if last else None),
                                        key=(("vmm", th, gs) if last else None))
                            for k_, hc in enumerate(hcs):
                                for c in range(8):
                                    last = (k_ == len(hcs) - 1 and c == 7)
                                    E.op(lambda e, c=c, hc=hc, jb=jb, si=si, k_=k_: e.matmul(
                                        pg[si % 2][:, k_ * 256:(k_ + 1) * 256],
                                        s_wup[:, c * 2816 + 1408 + hc * 128:
                                               c * 2816 + 1408 + (hc + 1) * 128],
                                        s_xs[jb][:, c * 256:(c + 1) * 256],
                                        start=(c == 0), stop=(c == 7), skip_group_check=True),
                                        ch=("t" if last else None),
                                        key=(("gmm", th, gs) if last else None))
                            if pend is not None:
                                emit_down(E, pend, SP[pend])
                            pend = si
                        emit_down(E, pend, SP[pend])
            if with_phase2:
                prog.body("tensor", p2_tensor)

            def p2_scalar(E, th=th):
                for ew in range(NEW):
                    for tg in range(NTG):
                        for si, hcs in enumerate(SP):
                            gs = gs_of(ew, tg, si)
                            LASTGS = gs_of(NEW - 1, NTG - 1, NSP - 1)
                            nc_ = len(hcs) * 256
                            E.wait("t", ("gmm", th, gs))
                            if gs >= 2:
                                E.wait("v", ("act", th, gs - 2))
                            elif th == 1:
                                E.wait("v", ("act", 0, LASTGS - 1 + gs))
                            E.op(lambda e, gs=gs, si=si, nc_=nc_: e.activation(
                                s_sil[si % 2][:, 0:nc_], pg[si % 2][:, 0:nc_], AF.Silu),
                                 ch="s", key=("sil", th, gs))
            if with_phase2:
                prog.body("scalar", p2_scalar)

            def p2_vector(E, th=th):
                for ew in range(NEW):
                    e_, wh = divmod(ew, 2)
                    for tg in range(NTG):
                        for si, hcs in enumerate(SP):
                            gs = gs_of(ew, tg, si)
                            LASTGS = gs_of(NEW - 1, NTG - 1, NSP - 1)
                            nc_ = len(hcs) * 256
                            E.wait("t", ("vmm", th, gs))
                            E.wait("s", ("sil", th, gs))
                            if gs >= 2:
                                E.wait("t", ("dmm", th, gs - 2))
                            elif th == 1:
                                E.wait("t", ("dmm", 0, LASTGS - 1 + gs))
                            E.op(lambda e, gs=gs, si=si, nc_=nc_: e.tensor_mul(
                                s_act[si % 2][:, 0:nc_], pv[si % 2][:, 0:nc_],
                                s_sil[si % 2][:, 0:nc_]),
                                 ch="v", key=("act", th, gs))
                        E.wait("t", ("dmm", th, gs_of(ew, tg, NSP - 1)))
                        for sub in range(2):
                            gtile = th * 16 + tg * 2 + sub
                            ai = (tg * 2 + sub) * D
                            for nh in range(2):
                                E.op(lambda e, sub=sub, ai=ai, gtile=gtile, e_=e_,
                                     nh=nh: e.scalar_tensor_tensor(
                                    s_acc[:, ai + nh * 512:ai + (nh + 1) * 512],
                                    p_yv[sub][nh],
                                    s_g[:, gtile * 8 + e_: gtile * 8 + e_ + 1],
                                    s_acc[:, ai + nh * 512:ai + (nh + 1) * 512],
                                    ALU.mult, ALU.add),
                                    ch="v", key=(("acc", th, ew, tg)
                                                 if (sub == 1 and nh == 1)
                                                 else ("acca", th, ew, tg, sub, nh)))
                E.op(lambda e: e.engine_nop(), ch="v", key=("accend", th))
            if with_phase2:
                prog.body("vector", p2_vector)

            def p_store(E, th=th):
                for i in range(16):
                    tg, sub = divmod(i, 2)
                    if with_phase2:
                        E.wait("v", ("acc", th, NEW - 1, tg))
                    else:
                        E.wait("v", ("acc0", th, i))
                    E.op(lambda e, i=i, th=th: e.dma_start(
                        out=y[(th * 16 + i) * 128:(th * 16 + i + 1) * 128, :],
                        in_=s_acc[:, i * D:(i + 1) * D]),
                        ch="yst", key=("y", th, i), k=16)
                if th == 1:
                    if dump_xn:
                        E.op(lambda e: e.dma_start(out=dxn, in_=xnT2),
                             ch="yst", key="dxn", k=16)
                        E.wait("yst", "dxn")
                    E.wait("yst", ("y", 1, 15))
            prog.body("gpsimd", p_store)



def _previ(th, i, k=1):
    gi = th * 16 + i - k
    return (gi // 16, gi % 16)


def _prevc(th, i, c, k=1):
    g = (th * 16 + i) * 8 + c - k
    return (g // 128, (g % 128) // 8, g % 8)




def prep_l1_ffn(ins):
    wg1 = ins["L1_ffn_wg1"]
    bg1 = ins["L1_ffn_bg1"]
    wg2 = ins["L1_ffn_wg2"]
    bg2 = ins["L1_ffn_bg2"]
    We = ins["L1_ffn_We"]
    be = ins["L1_ffn_be"]
    wg1C = np.ascontiguousarray(
        wg1.reshape(8, 128, 256).transpose(1, 0, 2).reshape(128, 8 * 256)).astype(bf16)
    bg1C = np.ascontiguousarray(bg1.reshape(2, 128).T).astype(np.float32)
    wg2C = np.ascontiguousarray(
        wg2[:, :20].reshape(2, 128, 20).transpose(1, 0, 2).reshape(128, 40)).astype(bf16)
    bg2C = np.ascontiguousarray(bg2[:20].reshape(1, 20)).astype(np.float32)
    WeC = np.ascontiguousarray(
        We.reshape(20, 8, 128, 1024).transpose(0, 2, 1, 3).reshape(20, 128, 8 * 1024)
    ).astype(bf16)
    return {
        "wg1": wg1C, "bg1": bg1C, "wg2": wg2C, "bg2": bg2C,
        "we": WeC, "be": np.ascontiguousarray(be).astype(np.float32),
        "ident": np.eye(128, dtype=np.float32).astype(bf16),
        "identf": np.eye(128, dtype=np.float32),
        "epsv": np.full((128, 1), EPS_LN, np.float32),
        "ones1": np.ones((1, 128), np.float32),
    }


def emit_l1_ffn(nc, prog, ar, aps, pool):
    NE = 20
    x = aps["interC"]
    wg1 = aps["D_wg1"]
    bg1 = aps["D_bg1"]
    wg2 = aps["D_wg2"]
    bg2 = aps["D_bg2"]
    we = aps["D_we"]
    be = aps["D_be"]
    ident = aps["D_ident"]
    identf = aps["D_identf"]
    epsv = aps["D_epsv"]
    ones1 = aps["D_ones1"]
    y = aps["y"]

    if True:
        sb = ar.sb
        s_id = sb("s_id", [128, 128], BF16)
        s_wg1 = sb("s_wg1", [128, 8 * 256], BF16)
        s_bg1 = sb("s_bg1", [128, 2])
        s_wg2 = sb("s_wg2", [128, 40], BF16)
        s_bg2 = sb("s_bg2", [1, 20])
        s_ones1 = sb("s_ones1", [1, 128])
        s_eps = sb("s_eps", [128, 1])
        s_x = [sb(f"s_x{j}", [128, D]) for j in range(2)]
        s_sq = sb("s_sq", [128, D])
        s_mean = [sb(f"s_mean{j}", [128, 1]) for j in range(2)]
        s_ss = sb("s_ss", [128, 1])
        s_nvar = sb("s_nvar", [128, 1])
        s_std = sb("s_std", [128, 1])
        s_rstd = [sb(f"s_rstd{j}", [128, 1]) for j in range(2)]
        s_negmb = [sb(f"s_negmb{j}", [128, 1]) for j in range(2)]
        s_xn = [sb(f"s_xn{j}", [128, D], BF16) for j in range(2)]
        s_xnT = [sb(f"s_xnT{c}", [128, 2048], BF16) for c in range(8)]
        s_h1 = [sb(f"s_h1{c}", [128, 2048], BF16) for c in range(2)]
        s_tp8 = sb("s_tp8", [128, 8])
        s_negmax = sb("s_negmax", [128, 1])
        s_pe = sb("s_pe", [128, 20])
        s_pd = sb("s_pd", [128, 1])
        s_pr = sb("s_pr", [128, 1])
        s_msk = sb("s_msk", [128, 20])
        s_gun = sb("s_gun", [128, 20])
        s_g = sb("s_g", [128, NT * 20])
        s_acc = sb("s_acc", [128, 16 * D])
        s_we = [sb(f"s_we{j}", [128, 8 * D], BF16) for j in range(2)]
        s_identf = sb("s_identf", [128, 128])
        s_beall = sb("s_beall", [20, D])
        s_gT = sb("s_gT", [20, NT * 128])

        p_tr = pool["tr"]
        p_h = pool["f0"][:, :]
        p_lg = pool["f1"][:, 0:20]
        p_yav = [[pool["w"][:, 0:512], pool["w"][:, 512:1024]],
                 [pool["f2"][:, :], pool["f3"][:, :]]]

        def body_const(E):
            for i, (src_ap, dst) in enumerate((
                    (ident, s_id), (wg1, s_wg1), (bg1, s_bg1), (wg2, s_wg2),
                    (bg2, s_bg2), (ones1, s_ones1), (epsv, s_eps),
                    (identf, s_identf), (be, s_beall))):
                E.op(lambda e, src_ap=src_ap, dst=dst: e.dma_start(out=dst[:, :], in_=src_ap),
                     ch="const", key=("c", i), k=16)
        prog.body("sync", body_const)
        NCONST = 9

        for th in range(2):
            def p1_sync(E, th=th):
                for i in range(16):
                    j = i % 2
                    gi = th * 16 + i
                    if th == 0 and i == 0:
                        E.wait("const", ("c", NCONST - 1))
                    if i >= 2:
                        E.wait("v", ("acc0", th, i - 2))
                    elif th == 1:
                        E.wait("v", ("acc0", 0, 14 + i))
                    E.op(lambda e, gi=gi, j=j: e.dma_start(
                        out=s_x[j][:, :], in_=x[gi * 128:(gi + 1) * 128, :]),
                        ch="x", key=("x", th, i), k=16)
            prog.body("sync", p1_sync)

            def p1_scalar(E, th=th):
                for i in range(16):
                    j = i % 2
                    E.wait("x", ("x", th, i))
                    if (th, i) != (0, 0):
                        E.wait("v", ("ssm", *_previ(th, i)))
                    E.op(lambda e, j=j: e.activation(s_sq[:, :], s_x[j][:, :], AF.Square),
                         ch="s", key=("sq", th, i))
                    E.wait("v", ("nvar", th, i))
                    if (th, i) != (0, 0):
                        E.wait("v", ("rstd", *_previ(th, i)))
                    E.op(lambda e: e.activation(s_std[:, :], s_nvar[:, :], AF.Sqrt,
                                                bias=s_eps[:, 0:1], scale=-1.0),
                         ch="s", key=("std", th, i))
                    E.wait("v", ("negmb", th, i))
                    if (th, i) not in ((0, 0), (0, 1)):
                        E.wait("t", ("tr", *_previ(th, i, 2), 7))
                    E.op(lambda e, j=j: e.activation(s_xn[j][:, :], s_x[j][:, :], AF.Identity,
                                                     bias=s_negmb[j][:, 0:1],
                                                     scale=s_rstd[j][:, 0:1]),
                         ch="s", key=("xn", th, i))
                    for hcb in range(2):
                        E.wait("t", ("h1m", th, i, hcb))
                        E.op(lambda e, hcb=hcb, i=i: e.activation(
                            s_h1[hcb][:, i * 128:(i + 1) * 128], p_h[:, 0:128], AF.Gelu,
                            bias=s_bg1[:, hcb:hcb + 1]),
                            ch="s", key=("h1e", th, i, hcb))
                    E.wait("v", ("negmax", th, i))
                    E.op(lambda e: e.activation(s_pe[:, :], p_lg[:, :], AF.Exp,
                                                bias=s_negmax[:, 0:1]),
                         ch="s", key=("pex", th, i))
            prog.body("scalar", p1_scalar)

            def p1_vector(E, th=th):
                for i in range(16):
                    j = i % 2
                    gi = th * 16 + i
                    E.wait("x", ("x", th, i))
                    E.op(lambda e, j=j: e.reduce_sum(s_mean[j][:, :], s_x[j][:, :], axis=AX.X),
                         ch="v", key=("meansum", th, i))
                    E.wait("v", ("meansum", th, i))
                    E.op(lambda e, j=j: e.tensor_scalar_mul(s_mean[j][:, :], s_mean[j][:, :],
                                                            1.0 / D), ch="v", key=("mean", th, i))
                    E.wait("s", ("sq", th, i))
                    E.op(lambda e: e.reduce_sum(s_ss[:, :], s_sq[:, :], axis=AX.X),
                         ch="v", key=("ss", th, i))
                    E.wait("v", ("ss", th, i))
                    E.op(lambda e: e.tensor_scalar_mul(s_ss[:, :], s_ss[:, :], 1.0 / D),
                         ch="v", key=("ssm", th, i))
                    E.wait("v", ("ssm", th, i))
                    E.op(lambda e, j=j: e.scalar_tensor_tensor(
                        s_nvar[:, :], s_mean[j][:, :], s_mean[j][:, 0:1], s_ss[:, :],
                        ALU.mult, ALU.subtract), ch="v", key=("nvar", th, i))
                    E.wait("s", ("std", th, i))
                    E.op(lambda e, j=j: e.reciprocal(s_rstd[j][:, :], s_std[:, :]),
                         ch="v", key=("rstd", th, i))
                    E.wait("v", ("rstd", th, i))
                    E.op(lambda e, j=j: e.tensor_scalar(
                        s_negmb[j][:, :], s_mean[j][:, :], s_rstd[j][:, 0:1], -1.0,
                        ALU.mult, ALU.mult), ch="v", key=("negmb", th, i))
                    for c in range(8):
                        E.wait("t", ("tr", th, i, c))
                        E.op(lambda e, c=c, i=i: e.tensor_copy(
                            s_xnT[c][:, i * 128:(i + 1) * 128], p_tr[c % 2][:, :]),
                            ch="v", key=("ev", th, i, c))
                    E.wait("t", ("lgm", th, i))
                    E.op(lambda e: e.max(out=s_tp8[:, :], in_=p_lg[:, :]),
                         ch="v", key=("tp8", th, i))
                    E.wait("v", ("tp8", th, i))
                    E.op(lambda e: e.tensor_scalar_mul(s_negmax[:, :], s_tp8[:, 0:1], -1.0),
                         ch="v", key=("negmax", th, i))
                    E.wait("s", ("pex", th, i))
                    E.op(lambda e: e.reduce_sum(s_pd[:, :], s_pe[:, :], axis=AX.X),
                         ch="v", key=("pd", th, i))
                    E.wait("v", ("pd", th, i))
                    E.op(lambda e: e.reciprocal(s_pr[:, :], s_pd[:, :]),
                         ch="v", key=("pr", th, i))
                    E.op(lambda e: e.tensor_scalar(s_msk[:, :], p_lg[:, :], s_tp8[:, 1:2],
                                                   None, ALU.is_ge),
                         ch="v", key=("msk", th, i))
                    E.wait("v", ("msk", th, i))
                    E.op(lambda e: e.tensor_mul(s_gun[:, :], s_pe[:, :], s_msk[:, :]),
                         ch="v", key=("gun", th, i))
                    E.wait("v", ("gun", th, i))
                    E.wait("v", ("pr", th, i))
                    E.op(lambda e, gi=gi: e.tensor_scalar(
                        s_g[:, gi * 20:(gi + 1) * 20], s_gun[:, :], s_pr[:, 0:1], None,
                        ALU.mult), ch="v", key=("g", th, i))
                    E.wait("t", ("gtr", th, i))
                    E.op(lambda e, gi=gi: e.tensor_copy(
                        s_gT[:, gi * 128:(gi + 1) * 128], pool["f1"][0:20, 0:128]),
                        ch="v", key=("gTe", th, i))
                    E.wait("t", ("gbm", th, i))
                    if th == 1 and i == 0:
                        E.wait("yst", ("y", 0, 15))
                    E.op(lambda e, i=i, j=j: e.tensor_add(
                        s_acc[:, i * D:i * D + 512], s_x[j][:, 0:512],
                        pool["w"][:, 0:512]), ch="v", key=("acc0a", th, i))
                    E.op(lambda e, i=i, j=j: e.tensor_add(
                        s_acc[:, i * D + 512:(i + 1) * D], s_x[j][:, 512:1024],
                        pool["w"][:, 512:1024]), ch="v", key=("acc0", th, i))
            prog.body("vector", p1_vector)

            def p1_tensor(E, th=th):
                for i in range(16):
                    j = i % 2
                    gi = th * 16 + i
                    E.wait("s", ("xn", th, i))
                    for c in range(8):
                        if (th, i, c) > (0, 0, 1):
                            E.wait("v", ("ev",) + _prevc(th, i, c, 2))
                        E.op(lambda e, j=j, c=c: e.transpose(
                            p_tr[c % 2][:, :], s_xn[j][:, c * 128:(c + 1) * 128], s_id[:, :]),
                            ch="t", key=("tr", th, i, c))
                    for hcb in range(2):
                        if hcb == 0 and (th, i) != (0, 0):
                            E.wait("s", ("h1e", *_previ(th, i), 1))
                        elif hcb == 1:
                            E.wait("s", ("h1e", th, i, 0))
                        for c in range(8):
                            E.wait("v", ("ev", th, i, c))
                            E.op(lambda e, c=c, i=i, hcb=hcb: e.matmul(
                                p_h[:, 0:128],
                                s_wg1[:, c * 256 + hcb * 128: c * 256 + (hcb + 1) * 128],
                                s_xnT[c][:, i * 128:(i + 1) * 128],
                                start=(c == 0), stop=(c == 7), skip_group_check=True),
                                ch=("t" if c == 7 else None),
                                key=(("h1m", th, i, hcb) if c == 7 else None))
                    E.wait("s", ("h1e", th, i, 1))
                    if (th, i) != (0, 0):
                        E.wait("s", ("pex", *_previ(th, i)))
                        E.wait("v", ("msk", *_previ(th, i)))
                        E.wait("v", ("gTe", *_previ(th, i)))
                    E.op(lambda e: e.matmul(p_lg[:, :], s_ones1[:, :], s_bg2[:, :],
                                            start=True, stop=False, skip_group_check=True))
                    for hc in range(2):
                        E.op(lambda e, hc=hc, i=i: e.matmul(
                            p_lg[:, :], s_h1[hc][:, i * 128:(i + 1) * 128],
                            s_wg2[:, hc * 20:(hc + 1) * 20],
                            start=False, stop=(hc == 1), skip_group_check=True),
                            ch=("t" if hc == 1 else None),
                            key=(("lgm", th, i) if hc == 1 else None))
                    E.wait("v", ("g", th, i))
                    E.op(lambda e, gi=gi: e.transpose(
                        pool["f1"][0:20, 0:128], s_g[:, gi * 20:(gi + 1) * 20],
                        s_identf[:, :]), ch="t", key=("gtr", th, i))
                    E.wait("v", ("gTe", th, i))
                    if (th, i) != (0, 0):
                        E.wait("v", ("acc0", *_previ(th, i)))
                    for half in range(2):
                        E.op(lambda e, gi=gi, half=half: e.matmul(
                            pool["w"][:, half * 512:(half + 1) * 512],
                            s_gT[:, gi * 128:(gi + 1) * 128],
                            s_beall[:, half * 512:(half + 1) * 512],
                            start=True, stop=True, skip_group_check=True),
                            ch=("t" if half == 1 else None),
                            key=(("gbm", th, i) if half == 1 else None))
            prog.body("tensor", p1_tensor)

            def p2_sync(E, th=th):
                for e_ in range(NE):
                    jw = e_ % 2
                    if th == 0 and e_ < 2:
                        E.wait("const", ("c", NCONST - 1))
                    else:
                        pe_ = (th, e_ - 2) if e_ >= 2 else (0, NE - 2 + e_)
                        E.wait("t", ("em", pe_[0], pe_[1], 15))
                    E.op(lambda e, e_=e_, jw=jw: e.dma_start(
                        out=s_we[jw][:, :], in_=we[e_, :, :]),
                        ch="wt", key=("we", th, e_), k=16)
            prog.body("sync", p2_sync)

            def p2_tensor(E, th=th):
                for e_ in range(NE):
                    jw = e_ % 2
                    E.wait("wt", ("we", th, e_))
                    if e_ == 0:
                        E.wait("s", ("h1e", th, 15, 1))
                        E.wait("v", ("g", th, 15))
                    for i in range(16):
                        jp = i % 2
                        if e_ > 0 or i >= 2:
                            pk = (th, e_, i - 2) if i >= 2 else (th, e_ - 1, 14 + i)
                            E.wait("v", ("ya",) + pk)
                        elif th == 1:
                            E.wait("v", ("ya", 0, NE - 1, 14 + i))
                        if e_ == 0 and i == 0:
                            E.wait("v", ("acc0", th, 15))
                        for half in range(2):
                            for c in range(8):
                                E.op(lambda e, c=c, i=i, jw=jw, jp=jp, half=half: e.matmul(
                                    p_yav[jp][half],
                                    s_xnT[c][:, i * 128:(i + 1) * 128],
                                    s_we[jw][:, c * D + half * 512: c * D + (half + 1) * 512],
                                    start=(c == 0), stop=(c == 7), skip_group_check=True),
                                    ch=("t" if (half == 1 and c == 7) else None),
                                    key=(("em", th, e_, i) if (half == 1 and c == 7) else None))
            prog.body("tensor", p2_tensor)

            def p2_vector(E, th=th):
                for e_ in range(NE):
                    for i in range(16):
                        jp = i % 2
                        gi = th * 16 + i
                        E.wait("t", ("em", th, e_, i))
                        for nh in range(2):
                            E.op(lambda e, i=i, jp=jp, gi=gi, e_=e_, nh=nh: e.scalar_tensor_tensor(
                                s_acc[:, i * D + nh * 512:i * D + (nh + 1) * 512],
                                p_yav[jp][nh],
                                s_g[:, gi * 20 + e_: gi * 20 + e_ + 1],
                                s_acc[:, i * D + nh * 512:i * D + (nh + 1) * 512],
                                ALU.mult, ALU.add),
                                ch="v", key=(("ya", th, e_, i) if nh == 1
                                             else ("yaa", th, e_, i)))
            prog.body("vector", p2_vector)

            def p_store(E, th=th):
                for i in range(16):
                    E.wait("v", ("ya", th, NE - 1, i))
                    E.op(lambda e, i=i, th=th: e.dma_start(
                        out=y[(th * 16 + i) * 128:(th * 16 + i + 1) * 128, :],
                        in_=s_acc[:, i * D:(i + 1) * D]),
                        ch="yst", key=("y", th, i), k=16)
                if th == 1:
                    E.wait("yst", ("y", 1, 15))
            prog.body("gpsimd", p_store)



C_NAMES = (
    ("wdisp", [128, 8 * 256], BF16), ("bdisp", [1, 256], F32),
    ("wval", [128, 8 * 1024], BF16), ("bval", [1, 1024], F32),
    ("wcproj", [128, 8 * 1024], BF16), ("bcproj", [1, 1024], F32),
    ("bd_state", [128, 128], BF16), ("bd_strain", [128, 128], BF16),
    ("bd_statei", [128, 128], BF16), ("bd_statej", [128, 128], BF16),
    ("bd_damage", [128, 128], BF16), ("sumbd", [128, 16], BF16),
    ("bolhs", [128, 16 * 128], BF16), ("dolhs", [128, 16 * 128], BF16),
    ("bstrainR", [128, 1], F32), ("bdamageR", [128, 1], F32),
    ("bstateR", [128, 1], F32), ("bdoutR", [128, 1], F32),
    ("bboR", [1, 512], F32), ("relembT", [16, 32], F32),
    ("wpos", [16, 16], F32), ("cmstrain", [128, 512], BF16),
    ("clmask", [128, 512], F32), ("invcnt", [16, 128], F32),
    ("ident", [128, 128], BF16), ("identf", [128, 128], F32),
    ("epsv", [128, 1], F32),
    ("ones1", [1, 128], F32))


def _bd8(w):
    out = np.zeros((128, 128), np.float32)
    for ds in range(8):
        out[ds * 16:(ds + 1) * 16, ds * 16:(ds + 1) * 16] = w
    return out.astype(bf16)


def prep_l1_attn(ins):
    g = lambda n: ins["L1_attn_" + n]
    d = {}
    d["wdisp"] = np.ascontiguousarray(
        g("w_disp").reshape(8, 128, 256).transpose(1, 0, 2).reshape(128, 8 * 256)).astype(bf16)
    d["bdisp"] = np.ascontiguousarray(g("b_disp").reshape(1, 256)).astype(np.float32)
    d["wval"] = np.ascontiguousarray(
        g("w_val").reshape(8, 128, 1024).transpose(1, 0, 2).reshape(128, 8 * 1024)).astype(bf16)
    d["bval"] = np.ascontiguousarray(g("b_val").reshape(1, 1024)).astype(np.float32)
    d["wcproj"] = np.ascontiguousarray(
        g("w_cproj").reshape(8, 128, 1024).transpose(1, 0, 2).reshape(128, 8 * 1024)).astype(bf16)
    d["bcproj"] = np.ascontiguousarray(g("b_cproj").reshape(1, 1024)).astype(np.float32)
    d["bd_state"] = _bd8(g("w_state"))
    d["bd_strain"] = _bd8(g("w_strain"))
    d["bd_statei"] = _bd8(g("w_state_i"))
    d["bd_statej"] = _bd8(g("w_state_j"))
    d["bd_damage"] = _bd8(g("w_damage"))
    sumbd = np.zeros((128, 16), np.float32)
    for ds in range(8):
        sumbd[ds * 16:(ds + 1) * 16, :] = g("w_state")
    d["sumbd"] = sumbd.astype(bf16)
    bo = np.zeros((16, 128, 128), np.float32)
    do_ = np.zeros((16, 128, 128), np.float32)
    for h in range(16):
        for ds in range(8):
            for bd in range(16):
                bo[h, ds * 16 + bd, 8 * h + ds] = g("w_bond_out")[bd, 0]
                do_[h, ds * 16 + bd, 8 * h + ds] = g("w_damage_out")[bd, 0]
    d["bolhs"] = np.ascontiguousarray(bo.transpose(1, 0, 2).reshape(128, 16 * 128)).astype(bf16)
    d["dolhs"] = np.ascontiguousarray(do_.transpose(1, 0, 2).reshape(128, 16 * 128)).astype(bf16)
    bstr = np.zeros((128, 1), np.float32)
    bdmg = np.zeros((128, 1), np.float32)
    for ds in range(8):
        bstr[ds * 16:(ds + 1) * 16, 0] = g("b_strain")
        bdmg[ds * 16:(ds + 1) * 16, 0] = g("b_damage")
    d["bstrainR"] = bstr
    d["bdamageR"] = bdmg
    bst = np.zeros((128, 1), np.float32)
    bst[:16, 0] = g("b_state")
    d["bstateR"] = bst
    d["bdoutR"] = np.full((128, 1), g("b_damage_out")[0], np.float32)
    d["bboR"] = np.full((1, 512), g("b_bond_out")[0], np.float32)
    d["relembT"] = np.ascontiguousarray(g("rel_pos_emb").T).astype(np.float32)
    d["wpos"] = np.ascontiguousarray(g("w_pos")).astype(np.float32)
    dd = (np.arange(8)[:, None, None] + 8 * np.arange(4)[None, :, None])
    tt = np.arange(128)[None, None, :]
    valid = (tt - 31 + dd) >= 0
    cmask_strain = np.repeat(valid[:, None, :, :], 16, axis=1).reshape(128, 512)
    d["cmstrain"] = cmask_strain.astype(np.float32).astype(bf16)
    clm = np.where(valid, 0.0, -1e9).astype(np.float32)
    clm_full = np.zeros((128, 512), np.float32)
    for h in range(16):
        for ds in range(8):
            clm_full[8 * h + ds] = clm[ds].reshape(512)
    d["clmask"] = clm_full
    inv = np.minimum(np.arange(128) + 1, 32).astype(np.float32)
    d["invcnt"] = np.broadcast_to(1.0 / inv, (16, 128)).copy()
    d["ident"] = np.eye(128, dtype=np.float32).astype(bf16)
    d["identf"] = np.eye(128, dtype=np.float32)
    d["epsv"] = np.full((128, 1), EPS_LN, np.float32)
    d["ones1"] = np.ones((1, 128), np.float32)
    return d


def emit_l1_attn(nc, prog, ar, aps, pool):
    TP = 4128
    REG = 32768
    NAMES = C_NAMES
    x = aps["interB"]
    ap_in = {n: aps["C_" + n] for n, _, _ in NAMES}
    y = aps["interC"]
    dispT = aps["dispT"]
    stateT = aps["stateT"]
    valS = aps["valS"]
    bexp = aps["bexp"]
    posscr = aps["posscr"]

    if True:
        sb = ar.sb
        s_in = {n: sb("s_" + n, s, dt) for n, s, dt in NAMES}

        s_x = [sb(f"s_x{j}", [128, D]) for j in range(2)]
        s_sq = sb("s_sq", [128, D])
        s_mean = [sb(f"s_mean{j}", [128, 1]) for j in range(2)]
        s_ss = sb("s_ss", [128, 1])
        s_nvar = sb("s_nvar", [128, 1])
        s_std = sb("s_std", [128, 1])
        s_rstd = [sb(f"s_rstd{j}", [128, 1]) for j in range(2)]
        s_negmb = [sb(f"s_negmb{j}", [128, 1]) for j in range(2)]
        s_xn = [sb(f"s_xn{j}", [128, D], BF16) for j in range(2)]
        s_xnT = [sb(f"s_xnT{j}", [128, D], BF16) for j in range(2)]
        s_dv = [sb(f"s_dv{j}", [128, 1024 + 256], BF16) for j in range(2)]
        s_dtr = [sb(f"s_dtr{j}", [128, 256], BF16) for j in range(2)]
        s_zero = sb("s_zero", [128, 1024], BF16)
        s_posj = sb("s_posj", [32, 16])
        s_posb4 = sb("s_posb4", [128, 4])
        s_posb = sb("s_posb", [128, 512])
        s_e1 = [sb(f"s_e1{j}", [128, 512]) for j in range(2)]
        s_dwin = sb("s_dwin", [128, 16 * 160], BF16)
        s_dcen = sb("s_dcen", [128, 16 * 128], BF16)
        s_swin = sb("s_swin", [128, 16 * 160], BF16)
        s_scen = sb("s_scen", [128, 16 * 128], BF16)
        s_strain = sb("s_strain", [128, 16 * 512], BF16)
        s_strainM = sb("s_strainM", [128, 512], BF16)
        s_state = [sb(f"s_state{j}", [16, 128], BF16) for j in range(2)]
        s_ba = sb("s_ba", [128, 512], BF16)
        s_da = sb("s_da", [128, 512], BF16)
        s_sg = sb("s_sg", [128, 512])
        s_lg = sb("s_lg", [128, 512])
        s_wall = sb("s_wall", [128, 512])
        s_wexp = sb("s_wexp", [128, 512], BF16)
        s_den = sb("s_den", [128, 16])
        s_rden = sb("s_rden", [128, 16])
        s_bt = [sb(f"s_bt{j}", [128, 256], BF16) for j in range(2)]
        s_bA = sb("s_bA", [128, 128], BF16)
        s_bB = sb("s_bB", [128, 128], BF16)
        s_vA = [sb(f"s_vA{j}", [128, 64], BF16) for j in range(2)]
        s_vB = [sb(f"s_vB{j}", [128, 64], BF16) for j in range(2)]
        s_attno = sb("s_attno", [128, 1024], BF16)
        s_attnoT = sb("s_attnoT", [128, 1024], BF16)
        s_out = [sb(f"s_out{j}", [128, D]) for j in range(2)]

        p_tr = pool["tr"]
        p_s2 = pool["f0"][:, :]
        p_s3 = pool["f1"][:, :]
        p_bl = pool["f2"][:, :]
        p_dl = pool["f3"][:, :]
        p_cp = pool["w"][:, :]

        def body_const(E):
            for name in s_in:
                E.op(lambda e, name=name: e.dma_start(out=s_in[name][:, :], in_=ap_in[name]),
                     ch="const", key=("c", name), k=16)
        prog.body("sync", body_const)
        LASTC = ("c", NAMES[-1][0])

        def body_zero(E):
            E.op(lambda e: e.memset(s_zero[:, :], 0.0), ch="d0", key="z0")
            E.wait("d0", "z0")
            for si, scr in enumerate((dispT, stateT)):
                for half in range(2):
                    E.op(lambda e, scr=scr, half=half: e.dma_start(
                        out=AP(scr.tensor, half * 128 * TP, [[TP, 128], [1, 31]]),
                        in_=s_zero[:, 0:31]),
                        ch="d0", key=("zg", si, half), k=16)
            E.op(lambda e: e.dma_start(
                out=AP(valS.tensor, 0, [[1024, 31], [1, 1024]]),
                in_=s_zero[0:31, :]), ch="d0", key="zv", k=16)
            for h in range(16):
                E.op(lambda e, h=h: e.dma_start(
                    out=AP(bexp.tensor, h * REG, [[512, 64], [1, 512]]),
                    in_=s_zero[0:64, 0:512]),
                    ch="d0", key=("zb", h), k=16)
        prog.body("gpsimd", body_zero)

        def pos_tensor(E):
            E.wait("const", LASTC)
            E.op(lambda e: e.matmul(p_s2[0:32, 0:16], s_in["relembT"][:, :],
                                    s_in["wpos"][:, :], start=True, stop=True,
                                    skip_group_check=True),
                 ch="t", key="posmm")
        prog.body("tensor", pos_tensor)

        def pos_vector(E):
            E.wait("t", "posmm")
            E.op(lambda e: e.tensor_copy(s_posj[:, :], p_s2[0:32, 0:16]),
                 ch="v", key="posev")
        prog.body("vector", pos_vector)

        def pos_gpsimd(E):
            E.wait("v", "posev")
            E.op(lambda e: e.dma_start(
                out=AP(posscr.tensor, 0, [[16, 32], [1, 16]]), in_=s_posj[:, :]),
                ch="d0", key="posw", k=16)
            E.wait("d0", "posw")
            E.op(lambda e: e.dma_start(
                out=s_posb4[:, :],
                in_=AP(posscr.tensor, 0, [[16, 8], [1, 16], [128, 4]]),
                allow_slow_non_contiguous=True),
                ch="d0", key="posr", k=16)
        prog.body("gpsimd", pos_gpsimd)

        def pos_vector2(E):
            E.wait("d0", "posr")
            for do in range(4):
                E.op(lambda e, do=do: e.tensor_copy(
                    s_posb[:, do * 128:(do + 1) * 128],
                    s_posb4[:, do:do + 1].to_broadcast([128, 128])),
                    ch="v", key=("posbc", do))
            E.wait("v", ("posbc", 3))
            E.op(lambda e: e.tensor_scalar(
                s_posb[:, :], s_posb[:, :], s_in["bstrainR"][:, 0:1], None, ALU.add),
                ch="v", key="posb")
        prog.body("vector", pos_vector2)

        def p0_sync(E):
            for i in range(NT):
                j = i % 2
                if i >= 2:
                    E.wait("s", ("xn", i - 2))
                E.op(lambda e, i=i, j=j: e.dma_start(
                    out=s_x[j][:, :], in_=x[i * 128:(i + 1) * 128, :]),
                    ch="x", key=("x", i), k=16)
        prog.body("sync", p0_sync)

        def p0_scalar(E):
            E.wait("const", LASTC)
            for i in range(NT):
                j = i % 2
                E.wait("x", ("x", i))
                if i >= 1:
                    E.wait("v", ("ss", i - 1))
                E.op(lambda e, j=j: e.activation(s_sq[:, :], s_x[j][:, :], AF.Square),
                     ch="s", key=("sq", i))
                E.wait("v", ("nvar", i))
                if i >= 1:
                    E.wait("v", ("rstd", i - 1))
                E.op(lambda e: e.activation(s_std[:, :], s_nvar[:, :], AF.Sqrt,
                                            bias=s_in["epsv"][:, 0:1], scale=-1.0),
                     ch="s", key=("std", i))
                E.wait("v", ("negmb", i))
                if i >= 2:
                    E.wait("t", ("tr", i - 2, 7))
                E.op(lambda e, j=j: e.activation(s_xn[j][:, :], s_x[j][:, :], AF.Identity,
                                                 bias=s_negmb[j][:, 0:1],
                                                 scale=s_rstd[j][:, 0:1]),
                     ch="s", key=("xn", i))
                E.wait("t", ("vm", i, 0))
                if i >= 2:
                    E.wait("d0", ("vst", i - 2))
                E.op(lambda e, j=j: e.activation(s_dv[j][:, 0:512], p_s2[:, :], AF.Identity),
                     ch="s", key=("ve", i, 0))
                E.wait("t", ("vm", i, 1))
                E.op(lambda e, j=j: e.activation(s_dv[j][:, 512:1024], p_s2[:, :], AF.Identity),
                     ch="s", key=("ve", i, 1))
                E.wait("t", ("dm", i))
                E.op(lambda e, j=j: e.activation(s_dv[j][:, 1024:1280], p_s2[:, 0:256],
                                                 AF.Identity), ch="s", key=("de", i))
        prog.body("scalar", p0_scalar)

        def p0_vector(E):
            for i in range(NT):
                j = i % 2
                E.wait("x", ("x", i))
                E.op(lambda e, j=j: e.reduce_sum(s_mean[j][:, :], s_x[j][:, :], axis=AX.X),
                     ch="v", key=("meansum", i))
                E.wait("v", ("meansum", i))
                E.op(lambda e, j=j: e.tensor_scalar_mul(s_mean[j][:, :], s_mean[j][:, :],
                                                        1.0 / D), ch="v", key=("mean", i))
                E.wait("s", ("sq", i))
                E.op(lambda e: e.reduce_sum(s_ss[:, :], s_sq[:, :], axis=AX.X),
                     ch="v", key=("ssr", i))
                E.wait("v", ("ssr", i))
                E.op(lambda e: e.tensor_scalar_mul(s_ss[:, :], s_ss[:, :], 1.0 / D),
                     ch="v", key=("ss", i))
                E.wait("v", ("ss", i))
                E.op(lambda e, j=j: e.scalar_tensor_tensor(
                    s_nvar[:, :], s_mean[j][:, :], s_mean[j][:, 0:1], s_ss[:, :],
                    ALU.mult, ALU.subtract), ch="v", key=("nvar", i))
                E.wait("s", ("std", i))
                E.op(lambda e, j=j: e.reciprocal(s_rstd[j][:, :], s_std[:, :]),
                     ch="v", key=("rstd", i))
                E.wait("v", ("rstd", i))
                E.op(lambda e, j=j: e.tensor_scalar(
                    s_negmb[j][:, :], s_mean[j][:, :], s_rstd[j][:, 0:1], -1.0,
                    ALU.mult, ALU.mult), ch="v", key=("negmb", i))
                for c in range(8):
                    E.wait("t", ("tr", i, c))
                    E.op(lambda e, j=j, c=c: e.tensor_copy(
                        s_xnT[j][:, c * 128:(c + 1) * 128], p_tr[c % 2][:, :]),
                        ch="v", key=("ev", i, c))
                for c in range(2):
                    E.wait("t", ("dtr", i, c))
                    if c == 0 and i >= 2:
                        E.wait("d0", ("dst", i - 2, 1))
                    E.op(lambda e, j=j, c=c: e.tensor_copy(
                        s_dtr[j][:, c * 128:(c + 1) * 128], p_tr[c % 2][:, :]),
                        ch="v", key=("dev", i, c))
        prog.body("vector", p0_vector)

        def p0_tensor(E):
            for i in range(NT):
                j = i % 2
                E.wait("s", ("xn", i))
                for c in range(8):
                    k = 8 * i + c
                    if k >= 2:
                        kk = k - 2
                        E.wait("v", ("ev", kk // 8, kk % 8))
                    E.op(lambda e, j=j, c=c: e.transpose(
                        p_tr[c % 2][:, :], s_xn[j][:, c * 128:(c + 1) * 128],
                        s_in["ident"][:, :]), ch="t", key=("tr", i, c))
                for half in range(2):
                    if i == 0 and half == 0:
                        E.wait("v", "posev")
                    else:
                        E.wait("s", ("ve", i, half - 1) if half == 1 else ("de", i - 1))
                    E.op(lambda e, half=half: e.matmul(
                        p_s2[:, :], s_in["ones1"][:, :],
                        s_in["bval"][0:1, half * 512:(half + 1) * 512],
                        start=True, stop=False, skip_group_check=True))
                    for c in range(8):
                        E.wait("v", ("ev", i, c))
                        E.op(lambda e, j=j, c=c, half=half: e.matmul(
                            p_s2[:, :], s_xnT[j][:, c * 128:(c + 1) * 128],
                            s_in["wval"][:, c * 1024 + half * 512: c * 1024 + (half + 1) * 512],
                            start=False, stop=(c == 7), skip_group_check=True),
                            ch=("t" if c == 7 else None),
                            key=(("vm", i, half) if c == 7 else None))
                E.wait("s", ("ve", i, 1))
                E.op(lambda e: e.matmul(
                    p_s2[:, 0:256], s_in["ones1"][:, :], s_in["bdisp"][0:1, :],
                    start=True, stop=False, skip_group_check=True))
                for c in range(8):
                    E.op(lambda e, j=j, c=c: e.matmul(
                        p_s2[:, 0:256], s_xnT[j][:, c * 128:(c + 1) * 128],
                        s_in["wdisp"][:, c * 256:(c + 1) * 256],
                        start=False, stop=(c == 7), skip_group_check=True),
                        ch=("t" if c == 7 else None), key=(("dm", i) if c == 7 else None))
                E.wait("s", ("de", i))
                for c in range(2):
                    E.wait("v", ("dev", i, c - 2) if c >= 2 else
                           (("dev", i - 1, c) if i >= 1 else ("ev", i, 7)))
                    E.op(lambda e, j=j, c=c: e.transpose(
                        p_tr[c % 2][:, :], s_dv[j][:, 1024 + c * 128: 1024 + (c + 1) * 128],
                        s_in["ident"][:, :]), ch="t", key=("dtr", i, c))
        prog.body("tensor", p0_tensor)

        def p0_gpsimd(E):
            E.wait("d0", ("zb", 15))
            E.wait("d0", "posr")
            for i in range(NT):
                j = i % 2
                E.wait("s", ("ve", i, 1))
                E.op(lambda e, i=i, j=j: e.dma_start(
                    out=AP(valS.tensor, (31 + i * 128) * 1024, [[1024, 128], [1, 1024]]),
                    in_=s_dv[j][:, 0:1024]), ch="d0", key=("vst", i), k=16)
                for c in range(2):
                    E.wait("v", ("dev", i, c))
                    E.op(lambda e, i=i, j=j, c=c: e.dma_start(
                        out=AP(dispT.tensor, c * 128 * TP + 31 + i * 128,
                               [[TP, 128], [1, 128]]),
                        in_=s_dtr[j][:, c * 128:(c + 1) * 128]),
                        ch="d0", key=("dst", i, c), k=16)
        prog.body("gpsimd", p0_gpsimd)

        for i in range(NT):
            t0 = i * 128

            def p1_sl(E, i=i, t0=t0):
                if i == 0:
                    E.wait("d0", ("dst", NT - 1, 1))
                    E.wait("d0", ("vst", NT - 1))
                for ds in range(8):
                    if i >= 1 and ds == 0:
                        E.wait("v", ("strain", i - 1, 15))
                    E.op(lambda e, ds=ds, t0=t0: e.dma_start(
                        out=AP(s_dwin, ds * 16 * (16 * 160), [[16 * 160, 16], [160, 16],
                                                              [1, 152]]),
                        in_=AP(dispT.tensor, t0 + ds, [[TP, 16], [16 * TP, 16], [1, 152]])),
                        ch="sl", key=("dw", i, ds), k=16)
                    E.op(lambda e, ds=ds, t0=t0: e.dma_start(
                        out=AP(s_dcen, ds * 16 * (16 * 128), [[16 * 128, 16], [128, 16],
                                                              [1, 128]]),
                        in_=AP(dispT.tensor, 31 + t0, [[TP, 16], [16 * TP, 16], [1, 128]])),
                        ch="sl", key=("dc", i, ds), k=16)
                for ds in range(8):
                    if ds == 0:
                        E.wait("d1", ("sst", i, 15))
                    if i >= 1 and ds == 0:
                        E.wait("t", ("bd", i - 1, 15))
                    E.op(lambda e, ds=ds, t0=t0: e.dma_start(
                        out=AP(s_swin, ds * 16 * (16 * 160), [[16 * 160, 16], [160, 16],
                                                              [1, 152]]),
                        in_=AP(stateT.tensor, t0 + ds, [[TP, 16], [16 * TP, 16], [1, 152]])),
                        ch="sl", key=("sw", i, ds), k=16)
                    E.op(lambda e, ds=ds, t0=t0: e.dma_start(
                        out=AP(s_scen, ds * 16 * (16 * 128), [[16 * 128, 16], [128, 16],
                                                              [1, 128]]),
                        in_=AP(stateT.tensor, 31 + t0, [[TP, 16], [16 * TP, 16], [1, 128]])),
                        ch="sl", key=("sc", i, ds), k=16)
                for h in range(16):
                    if h >= 2:
                        E.wait("t", ("av", i, h - 2))
                    elif i >= 1:
                        E.wait("t", ("av", i - 1, 14 + h))
                    if i >= 1:
                        E.op(lambda e, h=h, t0=t0: e.dma_start(
                            out=s_vA[h % 2][:, :],
                            in_=AP(valS.tensor, (31 + t0 - 128) * 1024 + h * 64,
                                   [[1024, 128], [1, 64]])),
                            ch="sl", key=("vA", i, h), k=16)
                    E.op(lambda e, h=h, t0=t0: e.dma_start(
                        out=s_vB[h % 2][:, :],
                        in_=AP(valS.tensor, (31 + t0) * 1024 + h * 64,
                               [[1024, 128], [1, 64]])),
                        ch="sl", key=("vB", i, h), k=16)
                if i == 0:
                    E.wait("s", ("xn", NT - 1))
                elif i >= 2:
                    E.wait("v", ("add", i - 2))
                E.op(lambda e, i=i: e.dma_start(
                    out=s_x[i % 2][:, :], in_=x[i * 128:(i + 1) * 128, :]),
                    ch="x", key=("x3", i), k=16)
            prog.body("sync", p1_sl)

            def p1_vec_strain(E, i=i):
                for h in range(16):
                    if h == 0:
                        E.wait("sl", ("dw", i, 7))
                        E.wait("sl", ("dc", i, 7))
                        if i >= 1:
                            E.wait("t", ("dmg", i - 1, 15))
                    E.op(lambda e, h=h: e.tensor_tensor(
                        s_strain[:, h * 512:(h + 1) * 512],
                        AP(s_dwin, h * 160, [[16 * 160, 128], [8, 4], [1, 128]]),
                        AP(s_dcen, h * 128, [[16 * 128, 128], [0, 4], [1, 128]]),
                        ALU.subtract),
                        ch="v", key=("strain", i, h))
                    if i == 0:
                        E.wait("v", ("strain", i, h))
                        E.op(lambda e, h=h: e.tensor_mul(
                            s_strainM[:, :], s_strain[:, h * 512:(h + 1) * 512],
                            s_in["cmstrain"][:, :]), ch="v", key=("strainM", i, h))
            prog.body("vector", p1_vec_strain)

            def p1_ten_state(E, i=i):
                for h in range(16):
                    E.wait("v", ("strainM", i, h) if i == 0 else ("strain", i, h))
                    if h <= 1 and i >= 1:
                        E.wait("v", ("attno", i - 1, 15))
                    if h >= 2:
                        E.wait("v", ("stfin", i, h - 2))
                    for do in range(4):
                        E.op(lambda e, h=h, do=do, i=i: e.matmul(
                            (p_s2 if h % 2 == 0 else p_s3)[0:16, 0:128],
                            s_in["sumbd"][:, :],
                            (s_strainM[:, do * 128:(do + 1) * 128] if i == 0 else
                             s_strain[:, h * 512 + do * 128: h * 512 + (do + 1) * 128]),
                            start=(do == 0), stop=(do == 3), skip_group_check=True),
                            ch=("t" if do == 3 else None),
                            key=(("stmm", i, h) if do == 3 else None))
            prog.body("tensor", p1_ten_state)

            def p1_vec_state(E, i=i):
                for h in range(16):
                    E.wait("t", ("stmm", i, h))
                    if h >= 2:
                        E.wait("d1", ("sst", i, h - 2))
                    psrc = (p_s2 if h % 2 == 0 else p_s3)[0:16, 0:128]
                    if i == 0:
                        E.op(lambda e, h=h, psrc=psrc: e.tensor_mul(
                            s_state[h % 2][:, :], psrc, s_in["invcnt"][:, :]),
                            ch="v", key=("stf0", i, h))
                        E.wait("v", ("stf0", i, h))
                        E.op(lambda e, h=h: e.tensor_scalar(
                            s_state[h % 2][:, :], s_state[h % 2][:, :],
                            s_in["bstateR"][0:16, 0:1], None, ALU.add),
                            ch="v", key=("stfin", i, h))
                    else:
                        E.op(lambda e, h=h, psrc=psrc: e.tensor_scalar(
                            s_state[h % 2][:, :], psrc, 1.0 / 32,
                            s_in["bstateR"][0:16, 0:1], ALU.mult, ALU.add),
                            ch="v", key=("stfin", i, h))
            prog.body("vector", p1_vec_state)

            def p1_gp_state(E, i=i, t0=t0):
                for h in range(16):
                    E.wait("v", ("stfin", i, h))
                    E.op(lambda e, h=h, t0=t0: e.dma_start(
                        out=AP(stateT.tensor, h * 16 * TP + 31 + t0, [[TP, 16], [1, 128]]),
                        in_=s_state[h % 2][:, :]), ch="d1", key=("sst", i, h), k=16)
            prog.body("gpsimd", p1_gp_state)

            def p1_ten_bond(E, i=i):
                for h in range(16):
                    if h == 0:
                        E.wait("sl", ("sw", i, 7))
                        E.wait("sl", ("sc", i, 7))
                        E.wait("v", ("stfin", i, 15))
                        E.wait("v", "posb")
                        if i >= 1:
                            E.wait("s", ("sgm", i - 1))
                            E.wait("v", ("lgs", i - 1))
                    else:
                        E.wait("v", ("e1", i, h - 1))
                    def _rhs_bond(h, kind):
                        if kind == 0:
                            return s_strain[:, h * 512:(h + 1) * 512]
                        if kind == 1:
                            return AP(s_swin, h * 160, [[16 * 160, 128], [8, 4], [1, 128]])
                        return AP(s_scen, h * 128, [[16 * 128, 128], [0, 4], [1, 128]])
                    for mi, lhs in enumerate(("bd_strain", "bd_statej", "bd_statei")):
                        E.op(lambda e, h=h, lhs=lhs, mi=mi: e.matmul(
                            p_s2[:, :], s_in[lhs][:, :], _rhs_bond(h, mi),
                            start=(mi == 0), stop=(mi == 2), skip_group_check=True),
                            ch=("t" if mi == 2 else None),
                            key=(("bd", i, h) if mi == 2 else None))
                    if h >= 1:
                        E.wait("s", ("dae", i, h - 1))
                    E.op(lambda e, h=h: e.matmul(
                        p_s3[:, :], s_in["bd_damage"][:, :],
                        s_strain[:, h * 512:(h + 1) * 512],
                        start=True, stop=True, skip_group_check=True),
                        ch="t", key=("dmg", i, h))
                    if h == 0:
                        E.op(lambda e: e.matmul(
                            p_bl[:, :], s_in["ones1"][:, :], s_in["bboR"][:, :],
                            start=True, stop=False, skip_group_check=True))
                    else:
                        E.wait("s", ("bae", i, h - 1))
                        E.op(lambda e, h=h: e.matmul(
                            p_bl[:, :], s_in["bolhs"][:, (h - 1) * 128:h * 128],
                            s_ba[:, :], start=False, stop=False, skip_group_check=True),
                            ch="t", key=("blm", i, h - 1))
                        E.wait("s", ("dae", i, h - 1))
                        E.op(lambda e, h=h: e.matmul(
                            p_dl[:, :], s_in["dolhs"][:, (h - 1) * 128:h * 128],
                            s_da[:, :], start=(h == 1), stop=False, skip_group_check=True),
                            ch="t", key=("dlm", i, h - 1))
                E.wait("s", ("bae", i, 15))
                E.op(lambda e: e.matmul(
                    p_bl[:, :], s_in["bolhs"][:, 15 * 128:16 * 128],
                    s_ba[:, :], start=False, stop=True, skip_group_check=True),
                    ch="t", key=("blm", i, 15))
                E.wait("s", ("dae", i, 15))
                E.op(lambda e: e.matmul(
                    p_dl[:, :], s_in["dolhs"][:, 15 * 128:16 * 128],
                    s_da[:, :], start=False, stop=True, skip_group_check=True),
                    ch="t", key=("dlm", i, 15))
            prog.body("tensor", p1_ten_bond)

            def p1_vec_e1(E, i=i):
                for h in range(16):
                    E.wait("t", ("bd", i, h))
                    if h >= 2:
                        E.wait("s", ("bae", i, h - 2))
                    E.op(lambda e, h=h: e.tensor_add(
                        s_e1[h % 2][:, :], p_s2[:, :], s_posb[:, :]),
                        ch="v", key=("e1", i, h))
            prog.body("vector", p1_vec_e1)

            def p1_sc_bond(E, i=i):
                for h in range(16):
                    E.wait("v", ("e1", i, h))
                    if h >= 1:
                        E.wait("t", ("blm", i, h - 1))
                    E.op(lambda e, h=h: e.activation(s_ba[:, :], s_e1[h % 2][:, :],
                                                     AF.Gelu),
                         ch="s", key=("bae", i, h))
                    E.wait("t", ("dmg", i, h))
                    if h >= 1:
                        E.wait("t", ("dlm", i, h - 1))
                    E.op(lambda e: e.activation(s_da[:, :], p_s3[:, :], AF.Gelu,
                                                bias=s_in["bdamageR"][:, 0:1]),
                         ch="s", key=("dae", i, h))
                E.wait("t", ("dlm", i, 15))
                E.op(lambda e: e.activation(s_sg[:, :], p_dl[:, :], AF.Sigmoid,
                                            bias=s_in["bdoutR"][:, 0:1]),
                     ch="s", key=("sgm", i))
            prog.body("scalar", p1_sc_bond)

            def p1_vec_logit(E, i=i):
                E.wait("s", ("sgm", i))
                E.wait("t", ("blm", i, 15))
                E.op(lambda e: e.scalar_tensor_tensor(
                    s_lg[:, :], s_sg[:, :], -10.0, p_bl[:, :], ALU.mult, ALU.add),
                    ch="v", key=("lg0", i))
                if i == 0:
                    E.wait("v", ("lg0", i))
                    E.op(lambda e: e.tensor_add(s_lg[:, :], s_lg[:, :],
                                                s_in["clmask"][:, :]),
                         ch="v", key=("lgs", i))
                else:
                    E.wait("v", ("lg0", i))
                    E.op(lambda e: e.tensor_copy(s_lg[:, :], s_lg[:, :]),
                         ch="v", key=("lgs", i))
            prog.body("vector", p1_vec_logit)

            def p1_ten_ltr(E, i=i):
                E.wait("v", ("lgs", i))
                for do in range(4):
                    if do >= 1:
                        E.wait("v", ("wev", i, do - 1))
                    E.op(lambda e, do=do: e.transpose(
                        p_bl[:, 0:128], s_lg[:, do * 128:(do + 1) * 128],
                        s_in["identf"][:, :]),
                        ch="t", key=("ltr", i, do))
            prog.body("tensor", p1_ten_ltr)

            def p1_vec_wall(E, i=i):
                for do in range(4):
                    E.wait("t", ("ltr", i, do))
                    E.op(lambda e, do=do: e.tensor_copy(
                        AP(s_wall, 8 * do, [[512, 128], [32, 16], [1, 8]]),
                        p_bl[:, 0:128]), ch="v", key=("wev", i, do))
            prog.body("vector", p1_vec_wall)

            def p1_sc_exp(E, i=i):
                E.wait("v", ("wev", i, 3))
                E.op(lambda e: e.activation(s_wexp[:, :], s_wall[:, :], AF.Exp),
                     ch="s", key=("wex", i))
            prog.body("scalar", p1_sc_exp)

            def p1_vec_den(E, i=i):
                E.wait("s", ("wex", i))
                for h in range(16):
                    E.op(lambda e, h=h: e.reduce_sum(
                        s_den[:, h:h + 1], s_wexp[:, h * 32:(h + 1) * 32], axis=AX.X),
                        ch="v", key=("den", i, h))
                E.wait("v", ("den", i, 15))
                E.op(lambda e: e.reciprocal(s_rden[:, :], s_den[:, :]),
                     ch="v", key=("rden", i))
            prog.body("vector", p1_vec_den)

            def p1_gp_av(E, i=i):
                E.wait("s", ("wex", i))
                if i >= 1:
                    E.wait("t", ("btr", i - 1, 15, 1))
                E.op(lambda e: e.dma_start(
                    out=AP(bexp.tensor, 97, [[257, 128], [REG, 16], [1, 32]]),
                    in_=AP(s_wexp, 0, [[512, 128], [32, 16], [1, 32]])),
                    ch="d1", key=("shw", i), k=16)
                for h in range(16):
                    E.wait("d1", ("shw", i))
                    if h >= 2:
                        E.wait("t", ("btr", i, h - 2, 1))
                    E.op(lambda e, h=h: e.dma_start(
                        out=s_bt[h % 2][:, :],
                        in_=AP(bexp.tensor, h * REG, [[256, 128], [1, 256]])),
                        ch="d1", key=("rb", i, h), k=16)
            prog.body("gpsimd", p1_gp_av)

            def p1_ten_av(E, i=i):
                for h in range(16):
                    E.wait("d1", ("rb", i, h))
                    for c in range(2):
                        E.wait("v", ("bev", i, h - 1, c) if h >= 1 else
                               (("bev", i - 1, 15, c) if i >= 1 else ("dev", NT - 1, c)))
                        E.op(lambda e, c=c, h=h: e.transpose(
                            p_tr[c % 2][:, :], s_bt[h % 2][:, c * 128:(c + 1) * 128],
                            s_in["ident"][:, :]), ch="t", key=("btr", i, h, c))
                    E.wait("v", ("bev", i, h, 1))
                    E.wait("sl", ("vB", i, h))
                    if h == 0:
                        E.wait("v", ("attno", i - 1, 15) if i >= 1 else ("rden", i))
                        E.wait("v", ("rden", i))
                    nblk = 2 if i >= 1 else 1
                    if i >= 1:
                        E.op(lambda e, h=h: e.matmul(
                            p_s2[:, 0:64], s_bA[:, :], s_vA[h % 2][:, :],
                            start=True, stop=False, skip_group_check=True))
                    E.op(lambda e, first=(i == 0), h=h: e.matmul(
                        p_s2[:, 0:64], s_bB[:, :], s_vB[h % 2][:, :],
                        start=first, stop=True, skip_group_check=True),
                        ch="t", key=("av", i, h))
            prog.body("tensor", p1_ten_av)

            def p1_vec_bev(E, i=i):
                for h in range(16):
                    for c in range(2):
                        E.wait("t", ("btr", i, h, c))
                        E.op(lambda e, c=c: e.tensor_copy(
                            (s_bA if c == 0 else s_bB)[:, :], p_tr[c % 2][:, :]),
                            ch="v", key=("bev", i, h, c))
                    E.wait("t", ("av", i, h))
                    E.op(lambda e, h=h: e.tensor_scalar_mul(
                        s_attno[:, h * 64:(h + 1) * 64], p_s2[:, 0:64],
                        s_rden[:, h:h + 1]), ch="v", key=("attno", i, h))
            prog.body("vector", p1_vec_bev)

            def p1_ten_cproj(E, i=i):
                for c in range(8):
                    E.wait("v", ("attno", i, 15))
                    E.wait("v", ("atev", i, c - 2) if c >= 2 else
                           (("atev", i - 1, 6 + c) if i >= 1 else ("bev", i, 15, c)))
                    E.op(lambda e, c=c: e.transpose(
                        p_tr[c % 2][:, :], s_attno[:, c * 128:(c + 1) * 128],
                        s_in["ident"][:, :]), ch="t", key=("atr", i, c))
                if i >= 1:
                    E.wait("v", ("add", i - 1))
                for half in range(2):
                    E.op(lambda e, half=half: e.matmul(
                        p_cp[:, half * 512:(half + 1) * 512], s_in["ones1"][:, :],
                        s_in["bcproj"][0:1, half * 512:(half + 1) * 512],
                        start=True, stop=False, skip_group_check=True))
                    for c in range(8):
                        E.wait("v", ("atev", i, c))
                        E.op(lambda e, c=c, half=half: e.matmul(
                            p_cp[:, half * 512:(half + 1) * 512],
                            s_attnoT[:, c * 128:(c + 1) * 128],
                            s_in["wcproj"][:, c * 1024 + half * 512:
                                           c * 1024 + (half + 1) * 512],
                            start=False, stop=(c == 7), skip_group_check=True),
                            ch=("t" if (half == 1 and c == 7) else None),
                            key=(("cpm", i) if (half == 1 and c == 7) else None))
            prog.body("tensor", p1_ten_cproj)

            def p1_vec_cproj(E, i=i):
                for c in range(8):
                    E.wait("t", ("atr", i, c))
                    E.op(lambda e, c=c: e.tensor_copy(
                        s_attnoT[:, c * 128:(c + 1) * 128], p_tr[c % 2][:, :]),
                        ch="v", key=("atev", i, c))
                E.wait("t", ("cpm", i))
                E.wait("x", ("x3", i))
                if i >= 2:
                    E.wait("st", ("y", i - 2))
                E.op(lambda e, i=i: e.tensor_add(s_out[i % 2][:, :], s_x[i % 2][:, :],
                                                 p_cp[:, :]), ch="v", key=("add", i))
            prog.body("vector", p1_vec_cproj)

            def p1_gp_store(E, i=i):
                E.wait("v", ("add", i))
                E.op(lambda e, i=i: e.dma_start(
                    out=y[i * 128:(i + 1) * 128, :], in_=s_out[i % 2][:, :]),
                    ch="st", key=("y", i), k=16)
                if i == NT - 1:
                    E.wait("st", ("y", i))
            prog.body("gpsimd", p1_gp_store)




ARENA_BYTES = 196608
TPC = 4128
REGC = 32768


def build_fused(debug_inter=False):
    nc = bass.Bass("TRN2", target_bir_lowering=False)
    aps = {}
    aps["x"] = nc.dram_tensor("x", [T, D], F32, kind="ExternalInput").ap()
    aps["y"] = nc.dram_tensor("y", [T, D], F32, kind="ExternalOutput").ap()
    for n in ("interA", "interB", "interC"):
        aps[n] = nc.dram_tensor(n, [T, D], F32, kind="ExternalOutput").ap()
    aps["xnT_scr"] = nc.dram_tensor("xnT_scr", [D, T], BF16).ap()
    aps["xnT2"] = nc.dram_tensor("xnT2", [D, T], BF16).ap()
    aps["dispT"] = nc.dram_tensor("dispT", [256, TPC], BF16).ap()
    aps["stateT"] = nc.dram_tensor("stateT", [256, TPC], BF16).ap()
    aps["valS"] = nc.dram_tensor("valS", [31 + T + 1, 1024], BF16).ap()
    aps["bexp"] = nc.dram_tensor("bexp", [16 * REGC], BF16).ap()
    aps["posscr"] = nc.dram_tensor("posscr", [512], F32).ap()
    ins_specs = [
        ("A_wqT", [D, D], BF16), ("A_wkT", [D, D], BF16), ("A_wvT", [D, D], BF16),
        ("A_woT", [D, D], BF16), ("A_ident", [128, 128], BF16),
        ("A_masks", [12, 128, 512], BF16), ("A_epsv", [128, 1], F32),
        ("B_wup", [8, D, 5632], BF16), ("B_wdn", [8, 2816, D], BF16),
        ("B_wrt", [128, 64], BF16), ("B_nsc", [128, 8], F32),
        ("B_ident", [128, 128], BF16), ("B_epsv", [128, 1], F32),
        ("B_epsr", [128, 1], F32),
        ("D_wg1", [128, 8 * 256], BF16), ("D_bg1", [128, 2], F32),
        ("D_wg2", [128, 40], BF16), ("D_bg2", [1, 20], F32),
        ("D_we", [20, 128, 8 * 1024], BF16), ("D_be", [20, 1024], F32),
        ("D_ident", [128, 128], BF16), ("D_identf", [128, 128], F32),
        ("D_epsv", [128, 1], F32),
        ("D_ones1", [1, 128], F32),
    ] + [("C_" + n, s, dt) for n, s, dt in C_NAMES]
    for n, s, dt in ins_specs:
        aps[n] = nc.dram_tensor(n, s, dt, kind="ExternalInput").ap()

    ctx = ExitStack()
    with ctx:
        ar = Arena(nc, ctx, ARENA_BYTES)
        pool = {
            "f0": ctx.enter_context(nc.psum_tensor("pool_f0", [128, 512], F32)),
            "f1": ctx.enter_context(nc.psum_tensor("pool_f1", [128, 512], F32)),
            "b6": ctx.enter_context(nc.psum_tensor("pool_b6", [128, 512], F32)),
            "b7": ctx.enter_context(nc.psum_tensor("pool_b7", [128, 512], F32)),
            "w": ctx.enter_context(nc.psum_tensor("pool_w", [128, 1024], F32)),
            "w2": ctx.enter_context(nc.psum_tensor("pool_w2", [128, 1024], F32)),
        }
        pool["f2"] = pool["w2"][:, 0:512]
        pool["f3"] = pool["w2"][:, 512:1024]
        pool["tr"] = [pool["b6"][:, :].bitcast(BF16)[:, 0:128],
                      pool["b7"][:, :].bitcast(BF16)[:, 0:128]]
        phase_chans = {
            "A.": ["const", "x", "v", "s", "t", "scr", "xc", "x3", "st"],
            "B.": ["const", "x", "v", "s", "t", "scr", "wt", "xs", "yst"],
            "C.": ["const", "x", "v", "s", "t", "d0", "d1", "sl", "st"],
            "D.": ["const", "x", "v", "s", "t", "wt", "yst"],
        }
        chans = {"bar": ctx.enter_context(nc.semaphore("c_bar"))}
        for pfx, names in phase_chans.items():
            for n in names:
                chans[pfx + n] = ctx.enter_context(nc.semaphore("c_" + pfx + n))
        prog = Prog()
        ENGS = ("sync", "tensor", "vector", "scalar", "gpsimd")

        def barrier_end(phase):
            prog.pfx = ""
            for eng in ENGS:
                def f(E, phase=phase, eng=eng):
                    E.op(lambda e: e.nop(), ch="bar", key=("bar", phase, eng))
                prog.body(eng, f)

        def barrier_start(phase_prev):
            prog.pfx = ""
            for eng in ENGS:
                def f(E, phase_prev=phase_prev):
                    for eng2 in ENGS:
                        E.wait("bar", ("bar", phase_prev, eng2))
                prog.body(eng, f)

        prog.pfx = "A."
        ar.reset()
        emit_l0_attn(nc, prog, ar, aps, pool)
        barrier_end("A")
        barrier_start("A")
        prog.pfx = "B."
        ar.reset()
        emit_l0_ffn(nc, prog, ar, aps, pool)
        barrier_end("B")
        barrier_start("B")
        prog.pfx = "C."
        ar.reset()
        emit_l1_attn(nc, prog, ar, aps, pool)
        barrier_end("C")
        barrier_start("C")
        prog.pfx = "D."
        ar.reset()
        emit_l1_ffn(nc, prog, ar, aps, pool)

        prog.run_plan()
        block = ctx.enter_context(nc.Block())
        prog.run_emit(nc, block, chans)
    return nc


_CACHE = {}


def kernel(**inputs):
    NC = 8
    x = inputs["L0_attn_x"]
    if "fused" not in _CACHE:
        _CACHE["fused"] = build_fused()
    nc = _CACHE["fused"]
    prep = {}
    for pfx, fn in (("A_", prep_l0_attn), ("B_", prep_l0_ffn),
                    ("C_", prep_l1_attn), ("D_", prep_l1_ffn)):
        for k, v in fn(inputs).items():
            prep[pfx + k] = v
    prep["B_wup"] = prep.pop("B_wup")
    in_maps = [dict(x=np.ascontiguousarray(x[b]), **prep) for b in range(NC)]
    res = run_bass_kernel_spmd(nc, in_maps, core_ids=list(range(NC)))
    return np.stack([res.results[b]["y"] for b in range(NC)])


# version 11
# speedup vs baseline: 1.0696x; 1.0696x over previous
import numpy as np
import ml_dtypes
from contextlib import ExitStack

import concourse.bass as bass
import concourse.mybir as mybir
from concourse.bass_utils import run_bass_kernel_spmd

F32 = mybir.dt.float32
BF16 = mybir.dt.bfloat16
AF = mybir.ActivationFunctionType
ALU = mybir.AluOpType
AX = mybir.AxisListType
AP = bass.AP
bf16 = ml_dtypes.bfloat16

T = 4096
D = 1024
NT = 32
NH = 16
HD = 64
CSHIFT = 30.0
EPS_LN = 1e-5
EPS_RMS = 1e-6



class Prog:

    ENGINES = ("sync", "tensor", "vector", "scalar", "gpsimd")

    def __init__(self):
        self.counts = {}
        self.ticks = {}
        self.mode = "plan"
        self.funcs = []
        self.pfx = ""

    def body(self, engine, fn):
        self.funcs.append((engine, fn, self.pfx))

    def run_plan(self):
        self.mode = "plan"
        self.counts = {}
        for name, fn, pfx in self.funcs:
            fn(_PlanEng(self, name, pfx))

    def run_emit(self, nc, block, chans):
        self.mode = "emit"
        for name, fn, pfx in self.funcs:
            def make(fn=fn, name=name, pfx=pfx):
                def body(eng):
                    fn(_EmitEng(self, name, eng, chans, pfx))
                return body
            getattr(block, name)(make())

    def final(self, ch):
        return self.counts[ch]


class _PlanEng:
    def __init__(self, prog, name, pfx=""):
        self.prog = prog
        self.name = name
        self.pfx = pfx

    def op(self, fn, ch=None, key=None, k=1):
        if ch is not None:
            ch = self.pfx + ch
            c = self.prog.counts.get(ch, 0) + k
            self.prog.counts[ch] = c
            assert (ch, key) not in self.prog.ticks, f"dup tick {ch} {key}"
            self.prog.ticks[(ch, key)] = c

    def wait(self, ch, key=None, value=None):
        pass


class _EmitEng:
    def __init__(self, prog, name, eng, chans, pfx=""):
        self.prog = prog
        self.name = name
        self.eng = eng
        self.chans = chans
        self.pfx = pfx

    def op(self, fn, ch=None, key=None, k=1):
        ins = fn(self.eng)
        if ch is not None:
            ins.then_inc(self.chans[self.pfx + ch], k)

    def wait(self, ch, key=None, value=None):
        ch = self.pfx + ch
        if value is None:
            value = self.prog.ticks[(ch, key)]
        self.eng.wait_ge(self.chans[ch], value)


class Arena:

    def __init__(self, nc, ctx, size_bytes):
        import concourse.mybir as _mb
        self.nc = nc
        self.size = size_bytes
        self.handle = ctx.enter_context(
            nc.sbuf_tensor("arena", [nc.NUM_PARTITIONS, size_bytes], mybir.dt.uint8))
        self.base = nc.lookup_mloc(self.handle).addr
        self.off = 0
        self.n = 0

    def reset(self):
        self.off = 0

    def sb(self, name, shape, dtype=F32):
        per_part = 1
        for s in shape[1:]:
            per_part *= s
        per_part *= mybir.dt.size(dtype)
        addr = self.base + self.off
        self.off = (self.off + per_part + 31) // 32 * 32
        assert self.off <= self.size, f"arena overflow at {name}: {self.off}"
        self.n += 1
        return self.nc.alloc_sbuf_tensor_at(f"{name}_p{self.n}", shape, dtype,
                                            offset=addr)


def tick(prog, ch, key):
    return prog.ticks[(ch, key)]



def _l0_attn_masks():
    a = np.arange(128)
    A = a[:, None]
    B = a[None, :]
    s_far = (A % 64 == B % 64)
    s_d1 = (A - B > 64) | (A % 64 == B % 64)
    s_d0 = (A <= B) & ((B - A < 64) | (A % 64 == B % 64))
    f_far = np.broadcast_to(A % 64 == 63, (128, 128))
    f_d0 = (A <= B) & ((A // 64 == B // 64) | (A % 64 == 63))
    zero = np.zeros((128, 128), bool)

    def combos(far, d1, d0):
        return [
            [far, far, far, far],
            [d1, far, far, far],
            [d0, d1, far, far],
            [zero, d0, d1, far],
            [zero, zero, d0, d1],
            [zero, zero, zero, d0],
        ]
    out = []
    for far, d1, d0 in ((s_far, s_d1, s_d0), (f_far, f_far, f_d0)):
        for c in combos(far, d1, d0):
            out.append(np.concatenate(c, axis=1))
    return np.stack(out).astype(np.float32).astype(bf16)


def _mask_id(kt, qg, headclass):
    base = 4 * qg
    c = 0 if kt <= base - 2 else kt - base + 2
    return headclass * 6 + c


def prep_l0_attn(ins):
    return {
        "wqT": np.ascontiguousarray(ins["L0_attn_Wq"].T).astype(bf16),
        "wkT": np.ascontiguousarray(ins["L0_attn_Wk"].T).astype(bf16),
        "wvT": np.ascontiguousarray(ins["L0_attn_Wv"].T).astype(bf16),
        "woT": np.ascontiguousarray(ins["L0_attn_Wo"].T).astype(bf16),
        "ident": np.eye(128, dtype=np.float32).astype(bf16),
        "masks": _l0_attn_masks(),
        "epsv": np.full((128, 1), EPS_LN, np.float32),
    }



def emit_l0_attn(nc, prog, ar, aps, pool):
    x = aps["x"]
    wqT = aps["A_wqT"]
    wkT = aps["A_wkT"]
    wvT = aps["A_wvT"]
    woT = aps["A_woT"]
    ident = aps["A_ident"]
    masks = aps["A_masks"]
    epsv = aps["A_epsv"]
    y = aps["interA"]
    xnT_scr = aps["xnT_scr"]

    if True:
        sb = ar.sb

        s_masks = sb("s_masks", [128, 12 * 512], BF16)
        s_id = sb("s_id", [128, 128], BF16)
        s_wo = sb("s_wo", [128, 8 * D], BF16)
        s_wh = [sb(f"s_wh{j}", [128, 3 * 512], BF16) for j in range(2)]
        s_eps = sb("s_eps", [128, 1])
        s_x = [sb(f"s_x{j}", [128, D]) for j in range(2)]
        s_sq = sb("s_sq", [128, D])
        s_mean = [sb(f"s_mean{j}", [128, 1]) for j in range(2)]
        s_ss = sb("s_ss", [128, 1])
        s_nvar = sb("s_nvar", [128, 1])
        s_std = sb("s_std", [128, 1])
        s_rstd = [sb(f"s_rstd{j}", [128, 1]) for j in range(2)]
        s_negmb = [sb(f"s_negmb{j}", [128, 1]) for j in range(2)]
        s_xn = [sb(f"s_xn{j}", [128, D], BF16) for j in range(2)]
        s_xnT = [sb(f"s_xnT{j}", [128, D], BF16) for j in range(2)]
        s_xc = [sb(f"s_xc{j}", [128, 8 * 512], BF16) for j in range(2)]
        s_kT = sb("s_kT", [65, T], BF16)
        s_qT = sb("s_qT", [65, T], BF16)
        s_vT = sb("s_vT", [64, T], BF16)
        s_vtok = sb("s_vtok", [128, 32 * 65], BF16)
        s_pex = [sb(f"s_pex{j}", [128, 512], BF16) for j in range(3)]
        s_pt = [sb(f"s_pt{j}", [128, 512], BF16) for j in range(3)]
        s_oun = sb("s_oun", [65, 512])
        s_rden = sb("s_rden", [1, 512])
        s_ones65 = sb("s_ones65", [1, 65])
        o_all = [sb(f"o_all{c}", [128, T], BF16) for c in range(8)]
        s_out = [sb(f"s_out{j}", [128, D]) for j in range(2)]

        p_tr = pool["tr"]
        p_st = [pool["f0"][:, :], pool["f1"][:, :], pool["f2"][:, :]]
        p_av = pool["f3"][0:65, :]
        p_bc = pool["w"][0:65, 0:512]
        p_qkv = p_st[2]

        def kt_list(qg):
            return list(range(min(NT, 4 * qg + 4)))

        def body_const(E):
            E.op(lambda e: e.dma_start(out=s_id[:, :], in_=ident),
                 ch="const", key="c0", k=16)
            E.op(lambda e: e.dma_start(out=s_eps[:, :], in_=epsv),
                 ch="const", key="c1", k=16)
            E.op(lambda e: e.dma_start(
                out=s_masks[:, :],
                in_=AP(masks.tensor, 0, [[512, 128], [128 * 512, 12], [1, 512]])),
                 ch="const", key="c2", k=16)
            E.op(lambda e: e.dma_start(
                out=s_wo[:, :],
                in_=AP(woT.tensor, 0, [[D, 128], [128 * D, 8], [1, D]])),
                 ch="const", key="loads", k=16)
        prog.body("sync", body_const)

        def body_const_g(E):
            E.wait("const", "loads")
            E.op(lambda e: e.memset(s_ones65[:, :], 1.0))
            E.op(lambda e: e.memset(AP(s_vtok, 64, [[2080, 128], [65, 32]]), 1.0))
            E.op(lambda e: e.memset(s_kT[64:65, :], 1.0))
            E.op(lambda e: e.memset(s_qT[64:65, :], -CSHIFT), ch="const", key="memsets")
        prog.body("gpsimd", body_const_g)

        def p1_sync(E):
            for i in range(NT):
                j = i % 2
                if i >= 2:
                    E.wait("s", ("xn", i - 2))
                E.op(lambda e, i=i, j=j: e.dma_start(
                    out=s_x[j][:, :], in_=x[i * 128:(i + 1) * 128, :]),
                    ch="x", key=("x", i), k=16)
        prog.body("sync", p1_sync)

        def p1_scalar(E):
            E.wait("const", "loads")
            for i in range(NT):
                j = i % 2
                E.wait("x", ("x", i))
                if i >= 1:
                    E.wait("v", ("ss", i - 1))
                E.op(lambda e, j=j: e.activation(s_sq[:, :], s_x[j][:, :], AF.Square),
                     ch="s", key=("sq", i))
                E.wait("v", ("nvar", i))
                if i >= 1:
                    E.wait("v", ("rstd", i - 1))
                E.op(lambda e: e.activation(s_std[:, :], s_nvar[:, :], AF.Sqrt,
                                            bias=s_eps[:, 0:1], scale=-1.0),
                     ch="s", key=("std", i))
                E.wait("v", ("negmb", i))
                if i >= 2:
                    E.wait("t", ("tr", 8 * (i - 1) - 1))
                E.op(lambda e, j=j: e.activation(s_xn[j][:, :], s_x[j][:, :], AF.Identity,
                                                 bias=s_negmb[j][:, 0:1],
                                                 scale=s_rstd[j][:, 0:1]),
                     ch="s", key=("xn", i))
        prog.body("scalar", p1_scalar)

        def p1_vector(E):
            for i in range(NT):
                j = i % 2
                E.wait("x", ("x", i))
                E.op(lambda e, j=j: e.reduce_sum(s_mean[j][:, :], s_x[j][:, :], axis=AX.X),
                     ch="v", key=("meansum", i))
                E.wait("v", ("meansum", i))
                E.op(lambda e, j=j: e.tensor_scalar_mul(s_mean[j][:, :], s_mean[j][:, :], 1.0 / D),
                     ch="v", key=("mean", i))
                E.wait("s", ("sq", i))
                E.op(lambda e: e.reduce_sum(s_ss[:, :], s_sq[:, :], axis=AX.X),
                     ch="v", key=("ss", i))
                E.wait("v", ("ss", i))
                E.op(lambda e: e.tensor_scalar_mul(s_ss[:, :], s_ss[:, :], 1.0 / D),
                     ch="v", key=("ssm", i))
                E.wait("v", ("mean", i))
                E.wait("v", ("ssm", i))
                E.op(lambda e, j=j: e.scalar_tensor_tensor(
                    s_nvar[:, :], s_mean[j][:, :], s_mean[j][:, 0:1], s_ss[:, :],
                    ALU.mult, ALU.subtract), ch="v", key=("nvar", i))
                E.wait("s", ("std", i))
                E.op(lambda e, j=j: e.reciprocal(s_rstd[j][:, :], s_std[:, :]),
                     ch="v", key=("rstd", i))
                E.wait("v", ("rstd", i))
                E.op(lambda e, j=j: e.tensor_scalar(
                    s_negmb[j][:, :], s_mean[j][:, :], s_rstd[j][:, 0:1], -1.0,
                    ALU.mult, ALU.mult), ch="v", key=("negmb", i))
                for c in range(8):
                    k = 8 * i + c
                    E.wait("t", ("tr", k))
                    if c == 0 and i >= 2:
                        E.wait("scr", ("w", i - 2))
                    E.op(lambda e, j=j, c=c: e.tensor_copy(
                        s_xnT[j][:, c * 128:(c + 1) * 128], p_tr[c % 2][:, :]),
                        ch="v", key=("ev", k))
        prog.body("vector", p1_vector)

        def p1_tensor(E):
            for i in range(NT):
                j = i % 2
                E.wait("s", ("xn", i))
                for c in range(8):
                    k = 8 * i + c
                    if k >= 2:
                        E.wait("v", ("ev", k - 2))
                    E.op(lambda e, j=j, c=c: e.transpose(
                        p_tr[c % 2][:, :], s_xn[j][:, c * 128:(c + 1) * 128], s_id[:, :]),
                        ch="t", key=("tr", k))
        prog.body("tensor", p1_tensor)

        def p1_gpsimd(E):
            E.wait("const", "memsets")
            for i in range(NT):
                j = i % 2
                E.wait("v", ("ev", 8 * i + 7))
                E.op(lambda e, i=i, j=j: e.dma_start(
                    out=AP(xnT_scr.tensor, i * 128, [[T, 128], [128 * T, 8], [1, 128]]),
                    in_=s_xnT[j][:, :]), ch="scr", key=("w", i), k=16)
        prog.body("gpsimd", p1_gpsimd)

        for h in range(NH):
            def p2_sync(E, h=h):
                jw = h % 2
                if h >= 2:
                    E.wait("t", ("qkv", h - 1, 7, "V"))
                for w_d, poff in ((wkT, 0), (wqT, 512), (wvT, 1024)):
                    E.op(lambda e, w_d=w_d, poff=poff, jw=jw: e.dma_start(
                        out=s_wh[jw][:, poff:poff + 512],
                        in_=AP(w_d.tensor, h * HD, [[D, 128], [128 * D, 8], [1, HD]])),
                        ch="xc", key=("wh", h, poff), k=16)
                for tg in range(8):
                    jb = tg % 2
                    if h == 0 and tg == 0:
                        E.wait("scr", ("w", NT - 1))
                    prev = (h, tg - 2) if tg >= 2 else ((h - 1, 6 + tg) if h >= 1 else None)
                    if prev is not None:
                        E.wait("t", ("qkv", prev[0], prev[1], "V"))
                    E.op(lambda e, tg=tg, jb=jb: e.dma_start(
                        out=s_xc[jb][:, :],
                        in_=AP(xnT_scr.tensor, tg * 512, [[T, 128], [128 * T, 8], [1, 512]])),
                        ch="xc", key=("xc", h, tg), k=16)
            prog.body("sync", p2_sync)

            def p2_tensor_qkv(E, h=h):
                pidx = 0
                for tg in range(8):
                    jb = tg % 2
                    E.wait("xc", ("xc", h, tg))
                    if h > 0 and tg == 0:
                        lq, lk = _last_qgkt(kt_list)
                        E.wait("t", ("av", h - 1, lq, lk))
                        E.wait("s", ("oun", h - 1, 7))
                    if tg == 0:
                        E.wait("xc", ("wh", h, 1024))
                    for proj, poff in (("K", 0), ("Q", 512), ("V", 1024)):
                        prev = _prev_proj3(h, tg, proj)
                        if prev is not None:
                            E.wait("s", ("qkve",) + prev)
                        for c in range(8):
                            last = (c == 7)
                            E.op(lambda e, c=c, poff=poff, jb=jb, jw=h % 2,
                                 b=pidx % 3: e.matmul(
                                p_st[b][0:64, :],
                                s_wh[jw][:, poff + c * HD: poff + (c + 1) * HD],
                                s_xc[jb][:, c * 512:(c + 1) * 512],
                                start=(c == 0), stop=(c == 7), skip_group_check=True),
                                ch=("t" if last else None),
                                key=(("qkv", h, tg, proj) if last else None))
                        pidx += 1
            prog.body("tensor", p2_tensor_qkv)

            def p2_scalar_qkv(E, h=h):
                pidx = 0
                for tg in range(8):
                    for proj in ("K", "Q", "V"):
                        b = pidx % 3
                        pidx += 1
                        E.wait("t", ("qkv", h, tg, proj))
                        if proj == "K":
                            E.op(lambda e, tg=tg, b=b: e.copy(
                                s_kT[0:64, tg * 512:(tg + 1) * 512], p_st[b][0:64, :]),
                                ch="s", key=("qkve", h, tg, "K"))
                        elif proj == "Q":
                            E.op(lambda e, tg=tg, b=b: e.activation(
                                s_qT[0:64, tg * 512:(tg + 1) * 512], p_st[b][0:64, :],
                                AF.Identity, scale=0.125),
                                ch="s", key=("qkve", h, tg, "Q"))
                        else:
                            E.op(lambda e, tg=tg, b=b: e.copy(
                                s_vT[0:64, tg * 512:(tg + 1) * 512], p_st[b][0:64, :]),
                                ch="s", key=("qkve", h, tg, "V"))
            prog.body("scalar", p2_scalar_qkv)

            def p2_tensor_vtr(E, h=h):
                E.wait("s", ("qkve", h, 7, "V"))
                for i in range(NT):
                    if i >= 2:
                        E.wait("v", ("vtok", h, i - 2))
                    if h == 0 and i < 2:
                        E.wait("v", ("ev", 8 * (NT - 1) + 6 + i))
                    elif h >= 1 and i < 2:
                        E.wait("v", ("vtok", h - 1, NT - 2 + i))
                    E.op(lambda e, i=i: e.transpose(
                        p_tr[i % 2][:, 0:64], s_vT[0:64, i * 128:(i + 1) * 128],
                        s_id[0:64, 0:64]), ch="t", key=("vtr", h, i))
            prog.body("tensor", p2_tensor_vtr)

            def p2_vector_vtok(E, h=h):
                for i in range(NT):
                    E.wait("t", ("vtr", h, i))
                    E.op(lambda e, i=i: e.tensor_copy(
                        s_vtok[:, 65 * i:65 * i + 64], p_tr[i % 2][:, 0:64]),
                        ch="v", key=("vtok", h, i))
            prog.body("vector", p2_vector_vtok)

            seq = []
            for qg in range(8):
                for kt in kt_list(qg):
                    seq.append((qg, kt))

            def p2_tensor_att(E, h=h, seq=seq):
                n = len(seq)
                for step in range(n + 2):
                    if step < n:
                        idx = step
                        qg, kt = seq[idx]
                        b = idx % 3
                        if idx >= 3:
                            E.wait("s", ("exp", h) + seq[idx - 3])
                        if idx == 0:
                            E.wait("s", ("qkve", h, 7, "Q"))
                            E.wait("v", ("vtok", h, NT - 1))
                        if idx == 2:
                            E.wait("s", ("qkve", h, 7, "V"))
                        E.op(lambda e, b=b, kt=kt, qg=qg: e.matmul(
                            p_st[b][:, :], s_kT[:, kt * 128:(kt + 1) * 128],
                            s_qT[:, qg * 512:(qg + 1) * 512], start=True, stop=True),
                            ch="t", key=("st", h, qg, kt))
                    if step >= 2:
                        jdx = step - 2
                        qg, kt = seq[jdx]
                        kts = kt_list(qg)
                        n_kt = kts.index(kt)
                        b = jdx % 3
                        E.wait("v", ("pt", h, qg, kt))
                        if n_kt == 0 and (h > 0 or qg > 0):
                            E.wait("s", ("oun",) + _prev_qg(h, qg))
                        E.op(lambda e, b=b, kt=kt, n_kt=n_kt,
                             last=(n_kt == len(kts) - 1): e.matmul(
                            p_av[:, :], s_vtok[:, 65 * kt:65 * (kt + 1)], s_pt[b][:, :],
                            start=(n_kt == 0), stop=last, skip_group_check=True),
                            ch="t", key=("av", h, qg, kt))
                        if kt == kts[-1]:
                            E.wait("v", ("rden", h, qg))
                            if (h, qg) != (0, 0):
                                E.wait("v", ("onorm",) + _prev_qg(h, qg))
                            E.op(lambda e: e.matmul(p_bc[:, :], s_ones65[:, :], s_rden[:, :],
                                                    start=True, stop=True),
                                 ch="t", key=("bc", h, qg))
            prog.body("tensor", p2_tensor_att)

            def p2_scalar_att(E, h=h, seq=seq):
                for idx, (qg, kt) in enumerate(seq):
                    kts = kt_list(qg)
                    b = idx % 3
                    E.wait("t", ("st", h, qg, kt))
                    if idx >= 3:
                        E.wait("v", ("pt", h) + seq[idx - 3])
                    E.op(lambda e, b=b: e.activation(s_pex[b][:, :], p_st[b][:, :], AF.Exp),
                         ch="s", key=("exp", h, qg, kt))
                    if kt == kts[-1]:
                        E.wait("t", ("av", h, qg, kt))
                        if (h, qg) != (0, 0):
                            E.wait("v", ("onorm",) + _prev_qg(h, qg))
                        E.op(lambda e: e.copy(s_oun[:, :], p_av[:, :]),
                             ch="s", key=("oun", h, qg))
            prog.body("scalar", p2_scalar_att)

            def p2_vector_att(E, h=h, seq=seq):
                for idx, (qg, kt) in enumerate(seq):
                    kts = kt_list(qg)
                    b = idx % 3
                    E.wait("s", ("exp", h, qg, kt))
                    if idx >= 3:
                        E.wait("t", ("av", h) + seq[idx - 3])
                    mid = _mask_id(kt, qg, 0 if h < 8 else 1)
                    E.op(lambda e, b=b, mid=mid: e.tensor_mul(
                        s_pt[b][:, :], s_pex[b][:, :],
                        s_masks[:, mid * 512:(mid + 1) * 512]),
                        ch="v", key=("pt", h, qg, kt))
                    if kt == kts[-1]:
                        E.wait("s", ("oun", h, qg))
                        E.op(lambda e: e.reciprocal(s_rden[:, :], s_oun[64:65, :]),
                             ch="v", key=("rden", h, qg))
                        E.wait("t", ("bc", h, qg))
                        E.op(lambda e, h=h, qg=qg: e.tensor_mul(
                            o_all[h // 2][64 * (h % 2):64 * (h % 2) + 64,
                                          qg * 512:(qg + 1) * 512],
                            s_oun[0:64, :], p_bc[0:64, :]),
                            ch="v", key=("onorm", h, qg))
            prog.body("vector", p2_vector_att)

        def p3_sync(E):
            for i in range(NT):
                j = i % 2
                if i >= 2:
                    E.wait("v", ("add", i - 2))
                elif i == 0:
                    E.wait("s", ("xn", NT - 1))
                E.op(lambda e, i=i, j=j: e.dma_start(
                    out=s_x[j][:, :], in_=x[i * 128:(i + 1) * 128, :]),
                    ch="x3", key=("x3", i), k=16)
        prog.body("sync", p3_sync)

        def p3_tensor(E):
            E.wait("v", ("onorm", NH - 1, 7))
            E.wait("s", ("oun", NH - 1, 7))
            for i in range(NT):
                if i >= 1:
                    E.wait("v", ("add", i - 1))
                for half in range(2):
                    for c in range(8):
                        last = (half == 1 and c == 7)
                        E.op(lambda e, i=i, half=half, c=c: e.matmul(
                            p_st[half][:, :],
                            o_all[c][:, i * 128:(i + 1) * 128],
                            s_wo[:, c * D + half * 512: c * D + (half + 1) * 512],
                            start=(c == 0), stop=(c == 7), skip_group_check=True),
                            ch=("t" if last else None),
                            key=(("proj", i) if last else None))
        prog.body("tensor", p3_tensor)

        def p3_vector(E):
            for i in range(NT):
                j = i % 2
                E.wait("t", ("proj", i))
                E.wait("x3", ("x3", i))
                if i >= 2:
                    E.wait("st", ("y", i - 2))
                E.op(lambda e, j=j: e.tensor_add(s_out[j][:, 0:512], s_x[j][:, 0:512],
                                                 p_st[0][:, :]), ch="v", key=("adda", i))
                E.op(lambda e, j=j: e.tensor_add(s_out[j][:, 512:1024], s_x[j][:, 512:1024],
                                                 p_st[1][:, :]), ch="v", key=("add", i))
        prog.body("vector", p3_vector)

        def p3_gpsimd(E):
            for i in range(NT):
                j = i % 2
                E.wait("v", ("add", i))
                E.op(lambda e, i=i, j=j: e.dma_start(
                    out=y[i * 128:(i + 1) * 128, :], in_=s_out[j][:, :]),
                    ch="st", key=("y", i), k=16)
            E.wait("st", ("y", NT - 1))
        prog.body("gpsimd", p3_gpsimd)



def _last_qgkt(kt_list):
    kts = kt_list(7)
    return 7, kts[-1]


def _prev_proj3(h, tg, proj):
    order = ["K", "Q", "V"]
    k = tg * 3 + order.index(proj) - 3
    if k >= 0:
        return (h, k // 3, order[k % 3])
    if h > 0:
        return (h - 1, (24 + k) // 3, order[k % 3])
    return None


def _prev_proj(h, tg, proj):
    order = ["K", "Q", "V"]
    idx = order.index(proj)
    if idx > 0:
        return (h, tg, order[idx - 1])
    if tg > 0:
        return (h, tg - 1, "V")
    if h > 0:
        return (h - 1, 7, "V")
    return None


def _prev_qg(h, qg):
    if qg > 0:
        return (h, qg - 1)
    return (h - 1, 7)




def prep_l0_ffn(ins):
    wup = ins["L0_ffn_w_up"]
    wdn = ins["L0_ffn_w_down"]
    wrt = ins["L0_ffn_w_router"]
    ns = ins["L0_ffn_norm_scale"]
    wupT = np.ascontiguousarray(wup.transpose(0, 2, 1)).astype(bf16)
    wdnT = np.ascontiguousarray(wdn.transpose(0, 2, 1)).astype(bf16)
    wrtC = np.ascontiguousarray(wrt.T.reshape(8, 128, 8).transpose(1, 0, 2)
                                .reshape(128, 64)).astype(bf16)
    nsC = np.ascontiguousarray(ns.reshape(8, 128).T).astype(np.float32)
    return {
        "wup": wupT, "wdn": wdnT, "wrt": wrtC, "nsc": nsC,
        "ident": np.eye(128, dtype=np.float32).astype(bf16),
        "epsv": np.full((128, 1), EPS_LN, np.float32),
        "epsr": np.full((128, 1), EPS_RMS, np.float32),
    }


def emit_l0_ffn(nc, prog, ar, aps, pool):
    HHC = 11
    NE = 8
    NTG = 8
    with_phase2 = True
    dump_xn = False
    x = aps["interA"]
    wup = aps["B_wup"]
    wdn = aps["B_wdn"]
    wrt = aps["B_wrt"]
    nsc = aps["B_nsc"]
    ident = aps["B_ident"]
    epsv = aps["B_epsv"]
    epsr = aps["B_epsr"]
    y = aps["interB"]
    dxn = None
    xnT2 = aps["xnT2"]

    if True:
        sb = ar.sb

        s_id = sb("s_id", [128, 128], BF16)
        s_wrt = sb("s_wrt", [128, 64], BF16)
        s_nsc = sb("s_nsc", [128, 8])
        s_eps = sb("s_eps", [128, 1])
        s_epsr = sb("s_epsr", [128, 1])
        s_x = [sb(f"s_x{j}", [128, D]) for j in range(2)]
        s_ln = [sb(f"s_ln{j}", [128, D]) for j in range(2)]
        s_sq = sb("s_sq", [128, D])
        s_mean = [sb(f"s_mean{j}", [128, 1]) for j in range(2)]
        s_ss = sb("s_ss", [128, 1])
        s_nvar = sb("s_nvar", [128, 1])
        s_std = sb("s_std", [128, 1])
        s_rstd = [sb(f"s_rstd{j}", [128, 1]) for j in range(2)]
        s_negmb = [sb(f"s_negmb{j}", [128, 1]) for j in range(2)]
        s_ms = sb("s_ms", [128, 1])
        s_rstd2 = [sb(f"s_rstd2{j}", [128, 1]) for j in range(2)]
        s_xn = [sb(f"s_xn{j}", [128, D], BF16) for j in range(2)]
        s_xnT = [sb(f"s_xnT{j}", [128, D], BF16) for j in range(2)]
        s_tp8 = sb("s_tp8", [128, 8])
        s_negmax = sb("s_negmax", [128, 1])
        s_pe = sb("s_pe", [128, 8])
        s_pd = sb("s_pd", [128, 1])
        s_pr = sb("s_pr", [128, 1])
        s_msk = sb("s_msk", [128, 8])
        s_gun = sb("s_gun", [128, 8])
        s_g = sb("s_g", [128, NT * 8])
        s_acc = sb("s_acc", [128, 16 * D])
        s_wup = sb("s_wup", [128, 8 * 2816], BF16)
        s_wdn = sb("s_wdn", [128, HHC * D], BF16)
        s_xs = [sb(f"s_xs{j}", [128, 8 * 256], BF16) for j in range(2)]
        s_sil = [sb(f"s_sil{j}", [128, 512], BF16) for j in range(2)]
        s_act = [sb(f"s_act{j}", [128, 512], BF16) for j in range(2)]

        p_tr = pool["tr"]
        pv = [pool["f0"][:, :], pool["b6"][:, :]]
        pg = [pool["f1"][:, :], pool["b7"][:, :]]
        ps_sc = pool["f0"][:, 0:8]
        p_yv = [[pool["w"][:, 0:512], pool["w"][:, 512:1024]],
                [pool["f2"][:, :], pool["f3"][:, :]]]

        def body_const(E):
            E.op(lambda e: e.dma_start(out=s_id[:, :], in_=ident), ch="const", key="c0", k=16)
            E.op(lambda e: e.dma_start(out=s_wrt[:, :], in_=wrt), ch="const", key="c1", k=16)
            E.op(lambda e: e.dma_start(out=s_nsc[:, :], in_=nsc), ch="const", key="c2", k=16)
            E.op(lambda e: e.dma_start(out=s_eps[:, :], in_=epsv), ch="const", key="c3", k=16)
            E.op(lambda e: e.dma_start(out=s_epsr[:, :], in_=epsr), ch="const", key="loads", k=16)
        prog.body("sync", body_const)

        for th in range(2):
            TOFF = th * 2048

            def p1_sync(E, th=th):
                for i in range(16):
                    j = i % 2
                    gi = th * 16 + i
                    if th == 0 and i == 0:
                        E.wait("const", "loads")
                    if i >= 2:
                        E.wait("s", ("xn", th, i - 2))
                    elif th == 1:
                        E.wait("s", ("xn", 0, 14 + i))
                    if th == 1 and i == 0:
                        E.wait("yst", ("y", 0, 15))
                    if th == 1 and i < 2 and not with_phase2:
                        pass
                    E.op(lambda e, i=i, j=j, th=th: e.dma_start(
                        out=s_x[j][:, :],
                        in_=x[(th * 16 + i) * 128:(th * 16 + i + 1) * 128, :]),
                        ch="x", key=("x", th, i), k=16)
            prog.body("sync", p1_sync)

            def p1_scalar(E, th=th):
                for i in range(16):
                    j = i % 2
                    E.wait("x", ("x", th, i))
                    if (th, i) != (0, 0):
                        E.wait("v", ("ssm", *_previ(th, i)))
                    E.op(lambda e, j=j: e.activation(s_sq[:, :], s_x[j][:, :], AF.Square),
                         ch="s", key=("sq", th, i))
                    E.wait("v", ("nvar", th, i))
                    if (th, i) != (0, 0):
                        E.wait("v", ("rstd2", *_previ(th, i)))
                    E.op(lambda e: e.activation(s_std[:, :], s_nvar[:, :], AF.Sqrt,
                                                bias=s_eps[:, 0:1], scale=-1.0),
                         ch="s", key=("std", th, i))
                    E.wait("v", ("negmb", th, i))
                    if (th, i) not in ((0, 0), (0, 1)):
                        E.wait("v", ("acc0", *_previ(th, i, 2)))
                        E.wait("s", ("xn", *_previ(th, i, 2)))
                    E.op(lambda e, j=j: e.activation(s_ln[j][:, :], s_x[j][:, :], AF.Identity,
                                                     bias=s_negmb[j][:, 0:1],
                                                     scale=s_rstd[j][:, 0:1]),
                         ch="s", key=("ln", th, i))
                    E.wait("s", ("ln", th, i))
                    E.op(lambda e, j=j: e.activation(s_sq[:, :], s_ln[j][:, :], AF.Square),
                         ch="s", key=("sq2", th, i))
                    E.wait("v", ("msm", th, i))
                    E.wait("v", ("rstd", th, i))
                    E.op(lambda e: e.activation(s_std[:, :], s_ms[:, :], AF.Sqrt,
                                                bias=s_epsr[:, 0:1]),
                         ch="s", key=("std2", th, i))
                    E.wait("v", ("rstd2", th, i))
                    if (th, i) not in ((0, 0), (0, 1)):
                        E.wait("t", ("tr", *_previ(th, i, 2), 7))
                    E.op(lambda e, j=j: e.activation(s_xn[j][:, :], s_ln[j][:, :], AF.Identity,
                                                     scale=s_rstd2[j][:, 0:1]),
                         ch="s", key=("xn", th, i))
                    E.wait("v", ("negmax", th, i))
                    E.op(lambda e: e.activation(s_pe[:, :], ps_sc, AF.Exp,
                                                bias=s_negmax[:, 0:1]),
                         ch="s", key=("pex", th, i))
            prog.body("scalar", p1_scalar)

            def p1_vector(E, th=th):
                for i in range(16):
                    j = i % 2
                    gi = th * 16 + i
                    E.wait("x", ("x", th, i))
                    E.op(lambda e, j=j: e.reduce_sum(s_mean[j][:, :], s_x[j][:, :], axis=AX.X),
                         ch="v", key=("meansum", th, i))
                    E.wait("v", ("meansum", th, i))
                    E.op(lambda e, j=j: e.tensor_scalar_mul(s_mean[j][:, :], s_mean[j][:, :],
                                                            1.0 / D), ch="v", key=("mean", th, i))
                    E.wait("s", ("sq", th, i))
                    E.op(lambda e: e.reduce_sum(s_ss[:, :], s_sq[:, :], axis=AX.X),
                         ch="v", key=("ss", th, i))
                    E.wait("v", ("ss", th, i))
                    E.op(lambda e: e.tensor_scalar_mul(s_ss[:, :], s_ss[:, :], 1.0 / D),
                         ch="v", key=("ssm", th, i))
                    E.wait("v", ("ssm", th, i))
                    E.op(lambda e, j=j: e.scalar_tensor_tensor(
                        s_nvar[:, :], s_mean[j][:, :], s_mean[j][:, 0:1], s_ss[:, :],
                        ALU.mult, ALU.subtract), ch="v", key=("nvar", th, i))
                    E.wait("s", ("std", th, i))
                    E.op(lambda e, j=j: e.reciprocal(s_rstd[j][:, :], s_std[:, :]),
                         ch="v", key=("rstd", th, i))
                    E.wait("v", ("rstd", th, i))
                    E.op(lambda e, j=j: e.tensor_scalar(
                        s_negmb[j][:, :], s_mean[j][:, :], s_rstd[j][:, 0:1], -1.0,
                        ALU.mult, ALU.mult), ch="v", key=("negmb", th, i))
                    E.wait("s", ("ln", th, i))
                    E.op(lambda e, j=j, i=i: e.tensor_add(
                        s_acc[:, i * D:(i + 1) * D], s_x[j][:, :], s_ln[j][:, :]),
                        ch="v", key=("acc0", th, i))
                    E.wait("s", ("sq2", th, i))
                    E.op(lambda e: e.reduce_sum(s_ms[:, :], s_sq[:, :], axis=AX.X),
                         ch="v", key=("msr", th, i))
                    E.wait("v", ("msr", th, i))
                    E.op(lambda e: e.tensor_scalar_mul(s_ms[:, :], s_ms[:, :], 1.0 / D),
                         ch="v", key=("msm", th, i))
                    E.wait("s", ("std2", th, i))
                    E.op(lambda e, j=j: e.reciprocal(s_rstd2[j][:, :], s_std[:, :]),
                         ch="v", key=("rstd2", th, i))
                    for c in range(8):
                        E.wait("t", ("tr", th, i, c))
                        if (c <= 1) and (th, i) not in ((0, 0), (0, 1)):
                            E.wait("scr", ("w", *_previ(th, i, 2)))
                        E.op(lambda e, j=j, c=c: e.tensor_scalar(
                            s_xnT[j][:, c * 128:(c + 1) * 128], p_tr[c % 2][:, :],
                            s_nsc[:, c:c + 1], None, ALU.mult),
                            ch="v", key=("ev", th, i, c))
                    E.wait("t", ("scm", th, i))
                    E.op(lambda e: e.max(out=s_tp8[:, :], in_=ps_sc),
                         ch="v", key=("tp8", th, i))
                    E.wait("v", ("tp8", th, i))
                    E.op(lambda e: e.tensor_scalar_mul(s_negmax[:, :], s_tp8[:, 0:1], -1.0),
                         ch="v", key=("negmax", th, i))
                    E.wait("s", ("pex", th, i))
                    E.op(lambda e: e.reduce_sum(s_pd[:, :], s_pe[:, :], axis=AX.X),
                         ch="v", key=("pd", th, i))
                    E.wait("v", ("pd", th, i))
                    E.op(lambda e: e.reciprocal(s_pr[:, :], s_pd[:, :]),
                         ch="v", key=("pr", th, i))
                    E.op(lambda e: e.tensor_scalar(s_msk[:, :], ps_sc, s_tp8[:, 1:2],
                                                   None, ALU.is_ge),
                         ch="v", key=("msk", th, i))
                    E.wait("v", ("msk", th, i))
                    E.op(lambda e: e.tensor_mul(s_gun[:, :], s_pe[:, :], s_msk[:, :]),
                         ch="v", key=("gun", th, i))
                    E.wait("v", ("gun", th, i))
                    E.wait("v", ("pr", th, i))
                    E.op(lambda e, gi=gi: e.tensor_scalar(
                        s_g[:, gi * 8:(gi + 1) * 8], s_gun[:, :], s_pr[:, 0:1], None,
                        ALU.mult), ch="v", key=("g", th, i))
            prog.body("vector", p1_vector)

            def p1_tensor(E, th=th):
                for i in range(16):
                    j = i % 2
                    gi = th * 16 + i
                    E.wait("s", ("xn", th, i))
                    for c in range(8):
                        k = 16 * th * 8 + 8 * i + c
                        if (th, i, c) > (0, 0, 1):
                            pc = _prevc(th, i, c, 2)
                            E.wait("v", ("ev",) + pc)
                        E.op(lambda e, j=j, c=c: e.transpose(
                            p_tr[c % 2][:, :], s_xn[j][:, c * 128:(c + 1) * 128], s_id[:, :]),
                            ch="t", key=("tr", th, i, c))
                    if th == 0 and i == 0:
                        E.wait("const", "loads")
                    else:
                        E.wait("v", ("msk", *_previ(th, i)))
                        E.wait("s", ("pex", *_previ(th, i)))
                    if (th, i) == (1, 0) and with_phase2:
                        E.wait("v", ("accend", 0))
                    for c in range(8):
                        E.wait("v", ("ev", th, i, c))
                        E.op(lambda e, j=j, c=c: e.matmul(
                            ps_sc, s_xnT[j][:, c * 128:(c + 1) * 128],
                            s_wrt[:, c * 8:(c + 1) * 8],
                            start=(c == 0), stop=(c == 7), skip_group_check=True),
                            ch=("t" if c == 7 else None),
                            key=(("scm", th, i) if c == 7 else None))
            prog.body("tensor", p1_tensor)

            def p1_gpsimd(E, th=th):
                for i in range(16):
                    j = i % 2
                    gi = th * 16 + i
                    E.wait("v", ("ev", th, i, 7))
                    E.op(lambda e, gi=gi, j=j: e.dma_start(
                        out=AP(xnT2.tensor, gi * 128, [[T, 128], [128 * T, 8], [1, 128]]),
                        in_=s_xnT[j][:, :]), ch="scr", key=("w", th, i), k=16)
            prog.body("gpsimd", p1_gpsimd)

            def hcg_of(ew, tg, hc):
                return (ew * NTG + tg) * HHC + hc

            SP = ((0, 1), (2, 3), (4, 5), (6, 7), (8, 9), (10,))
            NSP = 6

            def gs_of(ew, tg, si):
                return (ew * NTG + tg) * NSP + si

            NEW = NE * 2

            QS = ((0, 2), (3, 5), (6, 8), (9, 10))

            def p2_sync(E, th=th):
                for ew in range(NEW):
                    e_, wh = divmod(ew, 2)
                    ub = e_ * D * 5632 + wh * 1408
                    db = e_ * 2816 * D + wh * 1408 * D
                    for q, (hs, he) in enumerate(QS):
                        he_s = {2: 1, 5: 2, 8: 4, 10: 5}[he]
                        if ew == 0:
                            if q == 0:
                                E.wait("v", ("g", th, 15))
                            if th == 1:
                                E.wait("t", ("gmm", 0, gs_of(NEW - 1, NTG - 1, he_s)))
                                E.wait("t", ("dmm", 0, gs_of(NEW - 1, NTG - 1, he_s)))
                        else:
                            E.wait("t", ("gmm", th, gs_of(ew - 1, NTG - 1, he_s)))
                            E.wait("t", ("dmm", th, gs_of(ew - 1, NTG - 1, he_s)))
                        nc_ = (he + 1 - hs) * 128
                        E.op(lambda e, ub=ub, hs=hs, nc_=nc_: e.dma_start(
                            out=AP(s_wup, hs * 128, [[8 * 2816, 128], [2816, 8], [1, nc_]]),
                            in_=AP(wup.tensor, ub + hs * 128,
                                   [[5632, 128], [128 * 5632, 8], [1, nc_]])),
                            ch="wt", key=("wv", th, ew, q), k=16)
                        E.op(lambda e, ub=ub, hs=hs, nc_=nc_: e.dma_start(
                            out=AP(s_wup, 1408 + hs * 128,
                                   [[8 * 2816, 128], [2816, 8], [1, nc_]]),
                            in_=AP(wup.tensor, ub + 2816 + hs * 128,
                                   [[5632, 128], [128 * 5632, 8], [1, nc_]])),
                            ch="wt", key=("wg", th, ew, q), k=16)
                        nhc = he + 1 - hs
                        E.op(lambda e, db=db, hs=hs, nhc=nhc: e.dma_start(
                            out=AP(s_wdn, hs * D, [[HHC * D, 128], [D, nhc], [1, D]]),
                            in_=AP(wdn.tensor, db + hs * 128 * D,
                                   [[D, 128], [128 * D, nhc], [1, D]])),
                            ch="wt", key=("wd", th, ew, q), k=16)
            if with_phase2:
                prog.body("sync", p2_sync)

            def p2_gpsimd(E, th=th):
                for ew in range(NEW):
                    for tg in range(NTG):
                        jb = tg % 2
                        if ew == 0 and tg == 0:
                            E.wait("scr", ("w", th, 15))
                        if tg >= 2:
                            E.wait("t", ("gmm", th, gs_of(ew, tg - 2, NSP - 1)))
                        elif ew >= 1:
                            E.wait("t", ("gmm", th, gs_of(ew - 1, NTG - 2 + tg, NSP - 1)))
                        E.op(lambda e, tg=tg, jb=jb, th=th: e.dma_start(
                            out=s_xs[jb][:, :],
                            in_=AP(xnT2.tensor, th * 2048 + tg * 256,
                                   [[T, 128], [128 * T, 8], [1, 256]])),
                            ch="xs", key=("xs", th, ew, tg), k=16)
            if with_phase2:
                prog.body("gpsimd", p2_gpsimd)

            def p2_tensor(E, th=th):
                for ew in range(NEW):
                    e_, wh = divmod(ew, 2)
                    for tg in range(NTG):
                        jb = tg % 2
                        E.wait("xs", ("xs", th, ew, tg))
                        def emit_down(E, si, hcs, tg=tg, ew=ew):
                            gs = gs_of(ew, tg, si)
                            if tg == 0:
                                qd = {0: 0, 1: 1, 2: 1, 3: 2, 4: 3, 5: 3}[si]
                                E.wait("wt", ("wd", th, ew, qd))
                            E.wait("v", ("act", th, gs))
                            if si == 0:
                                if tg >= 1:
                                    E.wait("v", ("acc", th, ew, tg - 1))
                                elif ew >= 1:
                                    E.wait("v", ("acc", th, ew - 1, NTG - 1))
                                elif th == 1:
                                    E.wait("v", ("acc", 0, NEW - 1, NTG - 1))
                            for k_, hc in enumerate(hcs):
                                for sub in range(2):
                                    for nh in range(2):
                                        last = (k_ == len(hcs) - 1 and sub == 1 and nh == 1)
                                        E.op(lambda e, sub=sub, nh=nh, hc=hc, si=si,
                                             k_=k_: e.matmul(
                                            p_yv[sub][nh],
                                            s_act[si % 2][:, k_ * 256 + sub * 128:
                                                          k_ * 256 + (sub + 1) * 128],
                                            s_wdn[:, hc * D + nh * 512: hc * D + (nh + 1) * 512],
                                            start=(hc == 0), stop=(hc == HHC - 1),
                                            skip_group_check=True),
                                            ch=("t" if last else None),
                                            key=(("dmm", th, gs) if last else None))

                        pend = None
                        for si, hcs in enumerate(SP):
                            gs = gs_of(ew, tg, si)
                            LASTGS = gs_of(NEW - 1, NTG - 1, NSP - 1)
                            if tg == 0 and si in (0, 1, 3, 4):
                                q = {0: 0, 1: 1, 3: 2, 4: 3}[si]
                                E.wait("wt", ("wv", th, ew, q))
                                E.wait("wt", ("wg", th, ew, q))
                            if gs >= 2:
                                E.wait("v", ("act", th, gs - 2))
                            elif th == 1:
                                E.wait("v", ("act", 0, LASTGS - 1 + gs))
                            if gs == 0:
                                E.wait("v", ("msk", th, 15))
                                E.wait("s", ("pex", th, 15))
                            for k_, hc in enumerate(hcs):
                                for c in range(8):
                                    last = (k_ == len(hcs) - 1 and c == 7)
                                    E.op(lambda e, c=c, hc=hc, jb=jb, si=si, k_=k_: e.matmul(
                                        pv[si % 2][:, k_ * 256:(k_ + 1) * 256],
                                        s_wup[:, c * 2816 + hc * 128: c * 2816 + (hc + 1) * 128],
                                        s_xs[jb][:, c * 256:(c + 1) * 256],
                                        start=(c == 0), stop=(c == 7), skip_group_check=True),
                                        ch=("t" if last else None),
                                        key=(("vmm", th, gs) if last else None))
                            for k_, hc in enumerate(hcs):
                                for c in range(8):
                                    last = (k_ == len(hcs) - 1 and c == 7)
                                    E.op(lambda e, c=c, hc=hc, jb=jb, si=si, k_=k_: e.matmul(
                                        pg[si % 2][:, k_ * 256:(k_ + 1) * 256],
                                        s_wup[:, c * 2816 + 1408 + hc * 128:
                                               c * 2816 + 1408 + (hc + 1) * 128],
                                        s_xs[jb][:, c * 256:(c + 1) * 256],
                                        start=(c == 0), stop=(c == 7), skip_group_check=True),
                                        ch=("t" if last else None),
                                        key=(("gmm", th, gs) if last else None))
                            if pend is not None:
                                emit_down(E, pend, SP[pend])
                            pend = si
                        emit_down(E, pend, SP[pend])
            if with_phase2:
                prog.body("tensor", p2_tensor)

            def p2_scalar(E, th=th):
                for ew in range(NEW):
                    for tg in range(NTG):
                        for si, hcs in enumerate(SP):
                            gs = gs_of(ew, tg, si)
                            LASTGS = gs_of(NEW - 1, NTG - 1, NSP - 1)
                            nc_ = len(hcs) * 256
                            E.wait("t", ("gmm", th, gs))
                            if gs >= 2:
                                E.wait("v", ("act", th, gs - 2))
                            elif th == 1:
                                E.wait("v", ("act", 0, LASTGS - 1 + gs))
                            E.op(lambda e, gs=gs, si=si, nc_=nc_: e.activation(
                                s_sil[si % 2][:, 0:nc_], pg[si % 2][:, 0:nc_], AF.Silu),
                                 ch="s", key=("sil", th, gs))
            if with_phase2:
                prog.body("scalar", p2_scalar)

            def p2_vector(E, th=th):
                for ew in range(NEW):
                    e_, wh = divmod(ew, 2)
                    for tg in range(NTG):
                        for si, hcs in enumerate(SP):
                            gs = gs_of(ew, tg, si)
                            LASTGS = gs_of(NEW - 1, NTG - 1, NSP - 1)
                            nc_ = len(hcs) * 256
                            E.wait("t", ("vmm", th, gs))
                            E.wait("s", ("sil", th, gs))
                            if gs >= 2:
                                E.wait("t", ("dmm", th, gs - 2))
                            elif th == 1:
                                E.wait("t", ("dmm", 0, LASTGS - 1 + gs))
                            E.op(lambda e, gs=gs, si=si, nc_=nc_: e.tensor_mul(
                                s_act[si % 2][:, 0:nc_], pv[si % 2][:, 0:nc_],
                                s_sil[si % 2][:, 0:nc_]),
                                 ch="v", key=("act", th, gs))
                        E.wait("t", ("dmm", th, gs_of(ew, tg, NSP - 1)))
                        for sub in range(2):
                            gtile = th * 16 + tg * 2 + sub
                            ai = (tg * 2 + sub) * D
                            for nh in range(2):
                                E.op(lambda e, sub=sub, ai=ai, gtile=gtile, e_=e_,
                                     nh=nh: e.scalar_tensor_tensor(
                                    s_acc[:, ai + nh * 512:ai + (nh + 1) * 512],
                                    p_yv[sub][nh],
                                    s_g[:, gtile * 8 + e_: gtile * 8 + e_ + 1],
                                    s_acc[:, ai + nh * 512:ai + (nh + 1) * 512],
                                    ALU.mult, ALU.add),
                                    ch="v", key=(("acc", th, ew, tg)
                                                 if (sub == 1 and nh == 1)
                                                 else ("acca", th, ew, tg, sub, nh)))
                E.op(lambda e: e.engine_nop(), ch="v", key=("accend", th))
            if with_phase2:
                prog.body("vector", p2_vector)

            def p_store(E, th=th):
                for i in range(16):
                    tg, sub = divmod(i, 2)
                    if with_phase2:
                        E.wait("v", ("acc", th, NEW - 1, tg))
                    else:
                        E.wait("v", ("acc0", th, i))
                    E.op(lambda e, i=i, th=th: e.dma_start(
                        out=y[(th * 16 + i) * 128:(th * 16 + i + 1) * 128, :],
                        in_=s_acc[:, i * D:(i + 1) * D]),
                        ch="yst", key=("y", th, i), k=16)
                if th == 1:
                    if dump_xn:
                        E.op(lambda e: e.dma_start(out=dxn, in_=xnT2),
                             ch="yst", key="dxn", k=16)
                        E.wait("yst", "dxn")
                    E.wait("yst", ("y", 1, 15))
            prog.body("gpsimd", p_store)



def _previ(th, i, k=1):
    gi = th * 16 + i - k
    return (gi // 16, gi % 16)


def _prevc(th, i, c, k=1):
    g = (th * 16 + i) * 8 + c - k
    return (g // 128, (g % 128) // 8, g % 8)




def prep_l1_ffn(ins):
    wg1 = ins["L1_ffn_wg1"]
    bg1 = ins["L1_ffn_bg1"]
    wg2 = ins["L1_ffn_wg2"]
    bg2 = ins["L1_ffn_bg2"]
    We = ins["L1_ffn_We"]
    be = ins["L1_ffn_be"]
    wg1C = np.ascontiguousarray(
        wg1.reshape(8, 128, 256).transpose(1, 0, 2).reshape(128, 8 * 256)).astype(bf16)
    bg1C = np.ascontiguousarray(bg1.reshape(2, 128).T).astype(np.float32)
    wg2C = np.ascontiguousarray(
        wg2[:, :20].reshape(2, 128, 20).transpose(1, 0, 2).reshape(128, 40)).astype(bf16)
    bg2C = np.ascontiguousarray(bg2[:20].reshape(1, 20)).astype(np.float32)
    WeC = np.ascontiguousarray(
        We.reshape(20, 8, 128, 1024).transpose(0, 2, 1, 3).reshape(20, 128, 8 * 1024)
    ).astype(bf16)
    return {
        "wg1": wg1C, "bg1": bg1C, "wg2": wg2C, "bg2": bg2C,
        "we": WeC, "be": np.ascontiguousarray(be).astype(np.float32),
        "ident": np.eye(128, dtype=np.float32).astype(bf16),
        "identf": np.eye(128, dtype=np.float32),
        "epsv": np.full((128, 1), EPS_LN, np.float32),
        "ones1": np.ones((1, 128), np.float32),
    }


def emit_l1_ffn(nc, prog, ar, aps, pool):
    NE = 20
    x = aps["interC"]
    wg1 = aps["D_wg1"]
    bg1 = aps["D_bg1"]
    wg2 = aps["D_wg2"]
    bg2 = aps["D_bg2"]
    we = aps["D_we"]
    be = aps["D_be"]
    ident = aps["D_ident"]
    identf = aps["D_identf"]
    epsv = aps["D_epsv"]
    ones1 = aps["D_ones1"]
    y = aps["y"]

    if True:
        sb = ar.sb
        s_id = sb("s_id", [128, 128], BF16)
        s_wg1 = sb("s_wg1", [128, 8 * 256], BF16)
        s_bg1 = sb("s_bg1", [128, 2])
        s_wg2 = sb("s_wg2", [128, 40], BF16)
        s_bg2 = sb("s_bg2", [1, 20])
        s_ones1 = sb("s_ones1", [1, 128])
        s_eps = sb("s_eps", [128, 1])
        s_x = [sb(f"s_x{j}", [128, D]) for j in range(2)]
        s_sq = sb("s_sq", [128, D])
        s_mean = [sb(f"s_mean{j}", [128, 1]) for j in range(2)]
        s_ss = sb("s_ss", [128, 1])
        s_nvar = sb("s_nvar", [128, 1])
        s_std = sb("s_std", [128, 1])
        s_rstd = [sb(f"s_rstd{j}", [128, 1]) for j in range(2)]
        s_negmb = [sb(f"s_negmb{j}", [128, 1]) for j in range(2)]
        s_xn = [sb(f"s_xn{j}", [128, D], BF16) for j in range(2)]
        s_xnT = [sb(f"s_xnT{c}", [128, 2048], BF16) for c in range(8)]
        s_h1 = [sb(f"s_h1{c}", [128, 2048], BF16) for c in range(2)]
        s_tp8 = sb("s_tp8", [128, 8])
        s_negmax = sb("s_negmax", [128, 1])
        s_pe = sb("s_pe", [128, 20])
        s_pd = sb("s_pd", [128, 1])
        s_pr = sb("s_pr", [128, 1])
        s_msk = sb("s_msk", [128, 20])
        s_gun = sb("s_gun", [128, 20])
        s_g = sb("s_g", [128, NT * 20])
        s_acc = sb("s_acc", [128, 16 * D])
        s_we = [sb(f"s_we{j}", [128, 8 * D], BF16) for j in range(2)]
        s_identf = sb("s_identf", [128, 128])
        s_beall = sb("s_beall", [20, D])
        s_gT = sb("s_gT", [20, NT * 128])

        p_tr = pool["tr"]
        p_h = pool["f0"][:, :]
        p_lg = pool["f1"][:, 0:20]
        p_yav = [[pool["w"][:, 0:512], pool["w"][:, 512:1024]],
                 [pool["f2"][:, :], pool["f3"][:, :]]]

        def body_const(E):
            for i, (src_ap, dst) in enumerate((
                    (ident, s_id), (wg1, s_wg1), (bg1, s_bg1), (wg2, s_wg2),
                    (bg2, s_bg2), (ones1, s_ones1), (epsv, s_eps),
                    (identf, s_identf), (be, s_beall))):
                E.op(lambda e, src_ap=src_ap, dst=dst: e.dma_start(out=dst[:, :], in_=src_ap),
                     ch="const", key=("c", i), k=16)
        prog.body("sync", body_const)
        NCONST = 9

        for th in range(2):
            def p1_sync(E, th=th):
                for i in range(16):
                    j = i % 2
                    gi = th * 16 + i
                    if th == 0 and i == 0:
                        E.wait("const", ("c", NCONST - 1))
                    if i >= 2:
                        E.wait("v", ("acc0", th, i - 2))
                    elif th == 1:
                        E.wait("v", ("acc0", 0, 14 + i))
                    E.op(lambda e, gi=gi, j=j: e.dma_start(
                        out=s_x[j][:, :], in_=x[gi * 128:(gi + 1) * 128, :]),
                        ch="x", key=("x", th, i), k=16)
            prog.body("sync", p1_sync)

            def p1_scalar(E, th=th):
                for i in range(16):
                    j = i % 2
                    E.wait("x", ("x", th, i))
                    if (th, i) != (0, 0):
                        E.wait("v", ("ssm", *_previ(th, i)))
                    E.op(lambda e, j=j: e.activation(s_sq[:, :], s_x[j][:, :], AF.Square),
                         ch="s", key=("sq", th, i))
                    E.wait("v", ("nvar", th, i))
                    if (th, i) != (0, 0):
                        E.wait("v", ("rstd", *_previ(th, i)))
                    E.op(lambda e: e.activation(s_std[:, :], s_nvar[:, :], AF.Sqrt,
                                                bias=s_eps[:, 0:1], scale=-1.0),
                         ch="s", key=("std", th, i))
                    E.wait("v", ("negmb", th, i))
                    if (th, i) not in ((0, 0), (0, 1)):
                        E.wait("t", ("tr", *_previ(th, i, 2), 7))
                    E.op(lambda e, j=j: e.activation(s_xn[j][:, :], s_x[j][:, :], AF.Identity,
                                                     bias=s_negmb[j][:, 0:1],
                                                     scale=s_rstd[j][:, 0:1]),
                         ch="s", key=("xn", th, i))
                    for hcb in range(2):
                        E.wait("t", ("h1m", th, i, hcb))
                        E.op(lambda e, hcb=hcb, i=i: e.activation(
                            s_h1[hcb][:, i * 128:(i + 1) * 128], p_h[:, 0:128], AF.Gelu,
                            bias=s_bg1[:, hcb:hcb + 1]),
                            ch="s", key=("h1e", th, i, hcb))
                    E.wait("v", ("negmax", th, i))
                    E.op(lambda e: e.activation(s_pe[:, :], p_lg[:, :], AF.Exp,
                                                bias=s_negmax[:, 0:1]),
                         ch="s", key=("pex", th, i))
            prog.body("scalar", p1_scalar)

            def p1_vector(E, th=th):
                for i in range(16):
                    j = i % 2
                    gi = th * 16 + i
                    E.wait("x", ("x", th, i))
                    E.op(lambda e, j=j: e.reduce_sum(s_mean[j][:, :], s_x[j][:, :], axis=AX.X),
                         ch="v", key=("meansum", th, i))
                    E.wait("v", ("meansum", th, i))
                    E.op(lambda e, j=j: e.tensor_scalar_mul(s_mean[j][:, :], s_mean[j][:, :],
                                                            1.0 / D), ch="v", key=("mean", th, i))
                    E.wait("s", ("sq", th, i))
                    E.op(lambda e: e.reduce_sum(s_ss[:, :], s_sq[:, :], axis=AX.X),
                         ch="v", key=("ss", th, i))
                    E.wait("v", ("ss", th, i))
                    E.op(lambda e: e.tensor_scalar_mul(s_ss[:, :], s_ss[:, :], 1.0 / D),
                         ch="v", key=("ssm", th, i))
                    E.wait("v", ("ssm", th, i))
                    E.op(lambda e, j=j: e.scalar_tensor_tensor(
                        s_nvar[:, :], s_mean[j][:, :], s_mean[j][:, 0:1], s_ss[:, :],
                        ALU.mult, ALU.subtract), ch="v", key=("nvar", th, i))
                    E.wait("s", ("std", th, i))
                    E.op(lambda e, j=j: e.reciprocal(s_rstd[j][:, :], s_std[:, :]),
                         ch="v", key=("rstd", th, i))
                    E.wait("v", ("rstd", th, i))
                    E.op(lambda e, j=j: e.tensor_scalar(
                        s_negmb[j][:, :], s_mean[j][:, :], s_rstd[j][:, 0:1], -1.0,
                        ALU.mult, ALU.mult), ch="v", key=("negmb", th, i))
                    for c in range(8):
                        E.wait("t", ("tr", th, i, c))
                        E.op(lambda e, c=c, i=i: e.tensor_copy(
                            s_xnT[c][:, i * 128:(i + 1) * 128], p_tr[c % 2][:, :]),
                            ch="v", key=("ev", th, i, c))
                    E.wait("t", ("lgm", th, i))
                    E.op(lambda e: e.max(out=s_tp8[:, :], in_=p_lg[:, :]),
                         ch="v", key=("tp8", th, i))
                    E.wait("v", ("tp8", th, i))
                    E.op(lambda e: e.tensor_scalar_mul(s_negmax[:, :], s_tp8[:, 0:1], -1.0),
                         ch="v", key=("negmax", th, i))
                    E.wait("s", ("pex", th, i))
                    E.op(lambda e: e.reduce_sum(s_pd[:, :], s_pe[:, :], axis=AX.X),
                         ch="v", key=("pd", th, i))
                    E.wait("v", ("pd", th, i))
                    E.op(lambda e: e.reciprocal(s_pr[:, :], s_pd[:, :]),
                         ch="v", key=("pr", th, i))
                    E.op(lambda e: e.tensor_scalar(s_msk[:, :], p_lg[:, :], s_tp8[:, 1:2],
                                                   None, ALU.is_ge),
                         ch="v", key=("msk", th, i))
                    E.wait("v", ("msk", th, i))
                    E.op(lambda e: e.tensor_mul(s_gun[:, :], s_pe[:, :], s_msk[:, :]),
                         ch="v", key=("gun", th, i))
                    E.wait("v", ("gun", th, i))
                    E.wait("v", ("pr", th, i))
                    E.op(lambda e, gi=gi: e.tensor_scalar(
                        s_g[:, gi * 20:(gi + 1) * 20], s_gun[:, :], s_pr[:, 0:1], None,
                        ALU.mult), ch="v", key=("g", th, i))
                    E.wait("t", ("gtr", th, i))
                    E.op(lambda e, gi=gi: e.tensor_copy(
                        s_gT[:, gi * 128:(gi + 1) * 128], pool["f1"][0:20, 0:128]),
                        ch="v", key=("gTe", th, i))
                    E.wait("t", ("gbm", th, i))
                    if th == 1 and i == 0:
                        E.wait("yst", ("y", 0, 15))
                    E.op(lambda e, i=i, j=j: e.tensor_add(
                        s_acc[:, i * D:i * D + 512], s_x[j][:, 0:512],
                        pool["w"][:, 0:512]), ch="v", key=("acc0a", th, i))
                    E.op(lambda e, i=i, j=j: e.tensor_add(
                        s_acc[:, i * D + 512:(i + 1) * D], s_x[j][:, 512:1024],
                        pool["w"][:, 512:1024]), ch="v", key=("acc0", th, i))
            prog.body("vector", p1_vector)

            def p1_tensor(E, th=th):
                for i in range(16):
                    j = i % 2
                    gi = th * 16 + i
                    E.wait("s", ("xn", th, i))
                    for c in range(8):
                        if (th, i, c) > (0, 0, 1):
                            E.wait("v", ("ev",) + _prevc(th, i, c, 2))
                        E.op(lambda e, j=j, c=c: e.transpose(
                            p_tr[c % 2][:, :], s_xn[j][:, c * 128:(c + 1) * 128], s_id[:, :]),
                            ch="t", key=("tr", th, i, c))
                    for hcb in range(2):
                        if hcb == 0 and (th, i) != (0, 0):
                            E.wait("s", ("h1e", *_previ(th, i), 1))
                        elif hcb == 1:
                            E.wait("s", ("h1e", th, i, 0))
                        for c in range(8):
                            E.wait("v", ("ev", th, i, c))
                            E.op(lambda e, c=c, i=i, hcb=hcb: e.matmul(
                                p_h[:, 0:128],
                                s_wg1[:, c * 256 + hcb * 128: c * 256 + (hcb + 1) * 128],
                                s_xnT[c][:, i * 128:(i + 1) * 128],
                                start=(c == 0), stop=(c == 7), skip_group_check=True),
                                ch=("t" if c == 7 else None),
                                key=(("h1m", th, i, hcb) if c == 7 else None))
                    E.wait("s", ("h1e", th, i, 1))
                    if (th, i) != (0, 0):
                        E.wait("s", ("pex", *_previ(th, i)))
                        E.wait("v", ("msk", *_previ(th, i)))
                        E.wait("v", ("gTe", *_previ(th, i)))
                    E.op(lambda e: e.matmul(p_lg[:, :], s_ones1[:, :], s_bg2[:, :],
                                            start=True, stop=False, skip_group_check=True))
                    for hc in range(2):
                        E.op(lambda e, hc=hc, i=i: e.matmul(
                            p_lg[:, :], s_h1[hc][:, i * 128:(i + 1) * 128],
                            s_wg2[:, hc * 20:(hc + 1) * 20],
                            start=False, stop=(hc == 1), skip_group_check=True),
                            ch=("t" if hc == 1 else None),
                            key=(("lgm", th, i) if hc == 1 else None))
                    E.wait("v", ("g", th, i))
                    E.op(lambda e, gi=gi: e.transpose(
                        pool["f1"][0:20, 0:128], s_g[:, gi * 20:(gi + 1) * 20],
                        s_identf[:, :]), ch="t", key=("gtr", th, i))
                    E.wait("v", ("gTe", th, i))
                    if (th, i) != (0, 0):
                        E.wait("v", ("acc0", *_previ(th, i)))
                    for half in range(2):
                        E.op(lambda e, gi=gi, half=half: e.matmul(
                            pool["w"][:, half * 512:(half + 1) * 512],
                            s_gT[:, gi * 128:(gi + 1) * 128],
                            s_beall[:, half * 512:(half + 1) * 512],
                            start=True, stop=True, skip_group_check=True),
                            ch=("t" if half == 1 else None),
                            key=(("gbm", th, i) if half == 1 else None))
            prog.body("tensor", p1_tensor)

            def p2_sync(E, th=th):
                for e_ in range(NE):
                    jw = e_ % 2
                    if th == 0 and e_ < 2:
                        E.wait("const", ("c", NCONST - 1))
                    else:
                        pe_ = (th, e_ - 2) if e_ >= 2 else (0, NE - 2 + e_)
                        E.wait("t", ("em", pe_[0], pe_[1], 15))
                    E.op(lambda e, e_=e_, jw=jw: e.dma_start(
                        out=s_we[jw][:, :], in_=we[e_, :, :]),
                        ch="wt", key=("we", th, e_), k=16)
            prog.body("sync", p2_sync)

            def p2_tensor(E, th=th):
                for e_ in range(NE):
                    jw = e_ % 2
                    E.wait("wt", ("we", th, e_))
                    if e_ == 0:
                        E.wait("s", ("h1e", th, 15, 1))
                        E.wait("v", ("g", th, 15))
                    for i in range(16):
                        jp = i % 2
                        if e_ > 0 or i >= 2:
                            pk = (th, e_, i - 2) if i >= 2 else (th, e_ - 1, 14 + i)
                            E.wait("v", ("ya",) + pk)
                        elif th == 1:
                            E.wait("v", ("ya", 0, NE - 1, 14 + i))
                        if e_ == 0 and i == 0:
                            E.wait("v", ("acc0", th, 15))
                        for half in range(2):
                            for c in range(8):
                                E.op(lambda e, c=c, i=i, jw=jw, jp=jp, half=half: e.matmul(
                                    p_yav[jp][half],
                                    s_xnT[c][:, i * 128:(i + 1) * 128],
                                    s_we[jw][:, c * D + half * 512: c * D + (half + 1) * 512],
                                    start=(c == 0), stop=(c == 7), skip_group_check=True),
                                    ch=("t" if (half == 1 and c == 7) else None),
                                    key=(("em", th, e_, i) if (half == 1 and c == 7) else None))
            prog.body("tensor", p2_tensor)

            def p2_vector(E, th=th):
                for e_ in range(NE):
                    for i in range(16):
                        jp = i % 2
                        gi = th * 16 + i
                        E.wait("t", ("em", th, e_, i))
                        for nh in range(2):
                            E.op(lambda e, i=i, jp=jp, gi=gi, e_=e_, nh=nh: e.scalar_tensor_tensor(
                                s_acc[:, i * D + nh * 512:i * D + (nh + 1) * 512],
                                p_yav[jp][nh],
                                s_g[:, gi * 20 + e_: gi * 20 + e_ + 1],
                                s_acc[:, i * D + nh * 512:i * D + (nh + 1) * 512],
                                ALU.mult, ALU.add),
                                ch="v", key=(("ya", th, e_, i) if nh == 1
                                             else ("yaa", th, e_, i)))
            prog.body("vector", p2_vector)

            def p_store(E, th=th):
                for i in range(16):
                    E.wait("v", ("ya", th, NE - 1, i))
                    E.op(lambda e, i=i, th=th: e.dma_start(
                        out=y[(th * 16 + i) * 128:(th * 16 + i + 1) * 128, :],
                        in_=s_acc[:, i * D:(i + 1) * D]),
                        ch="yst", key=("y", th, i), k=16)
                if th == 1:
                    E.wait("yst", ("y", 1, 15))
            prog.body("gpsimd", p_store)



C_NAMES = (
    ("wdisp", [128, 8 * 256], BF16), ("bdisp", [1, 256], F32),
    ("wval", [128, 8 * 1024], BF16), ("bval", [1, 1024], F32),
    ("wcproj", [128, 8 * 1024], BF16), ("bcproj", [1, 1024], F32),
    ("bd_state", [128, 128], BF16), ("bd_strain", [128, 128], BF16),
    ("bd_statei", [128, 128], BF16), ("bd_statej", [128, 128], BF16),
    ("bd_damage", [128, 128], BF16), ("sumbd", [128, 16], BF16),
    ("bolhs", [128, 16 * 128], BF16), ("dolhs", [128, 16 * 128], BF16),
    ("bstrainR", [128, 1], F32), ("bdamageR", [128, 1], F32),
    ("bstateR", [128, 1], F32), ("bdoutR", [128, 1], F32),
    ("bboR", [1, 512], F32), ("relembT", [16, 32], F32),
    ("wpos", [16, 16], F32), ("cmstrain", [128, 512], BF16),
    ("clmask", [128, 512], F32), ("invcnt", [16, 128], F32),
    ("ident", [128, 128], BF16), ("identf", [128, 128], F32),
    ("epsv", [128, 1], F32),
    ("ones1", [1, 128], F32))


def _bd8(w):
    out = np.zeros((128, 128), np.float32)
    for ds in range(8):
        out[ds * 16:(ds + 1) * 16, ds * 16:(ds + 1) * 16] = w
    return out.astype(bf16)


def prep_l1_attn(ins):
    g = lambda n: ins["L1_attn_" + n]
    d = {}
    d["wdisp"] = np.ascontiguousarray(
        g("w_disp").reshape(8, 128, 256).transpose(1, 0, 2).reshape(128, 8 * 256)).astype(bf16)
    d["bdisp"] = np.ascontiguousarray(g("b_disp").reshape(1, 256)).astype(np.float32)
    d["wval"] = np.ascontiguousarray(
        g("w_val").reshape(8, 128, 1024).transpose(1, 0, 2).reshape(128, 8 * 1024)).astype(bf16)
    d["bval"] = np.ascontiguousarray(g("b_val").reshape(1, 1024)).astype(np.float32)
    d["wcproj"] = np.ascontiguousarray(
        g("w_cproj").reshape(8, 128, 1024).transpose(1, 0, 2).reshape(128, 8 * 1024)).astype(bf16)
    d["bcproj"] = np.ascontiguousarray(g("b_cproj").reshape(1, 1024)).astype(np.float32)
    d["bd_state"] = _bd8(g("w_state"))
    d["bd_strain"] = _bd8(g("w_strain"))
    d["bd_statei"] = _bd8(g("w_state_i"))
    d["bd_statej"] = _bd8(g("w_state_j"))
    d["bd_damage"] = _bd8(g("w_damage"))
    sumbd = np.zeros((128, 16), np.float32)
    for ds in range(8):
        sumbd[ds * 16:(ds + 1) * 16, :] = g("w_state")
    d["sumbd"] = sumbd.astype(bf16)
    bo = np.zeros((16, 128, 128), np.float32)
    do_ = np.zeros((16, 128, 128), np.float32)
    for h in range(16):
        for ds in range(8):
            for bd in range(16):
                bo[h, ds * 16 + bd, 8 * h + ds] = g("w_bond_out")[bd, 0]
                do_[h, ds * 16 + bd, 8 * h + ds] = g("w_damage_out")[bd, 0]
    d["bolhs"] = np.ascontiguousarray(bo.transpose(1, 0, 2).reshape(128, 16 * 128)).astype(bf16)
    d["dolhs"] = np.ascontiguousarray(do_.transpose(1, 0, 2).reshape(128, 16 * 128)).astype(bf16)
    bstr = np.zeros((128, 1), np.float32)
    bdmg = np.zeros((128, 1), np.float32)
    for ds in range(8):
        bstr[ds * 16:(ds + 1) * 16, 0] = g("b_strain")
        bdmg[ds * 16:(ds + 1) * 16, 0] = g("b_damage")
    d["bstrainR"] = bstr
    d["bdamageR"] = bdmg
    bst = np.zeros((128, 1), np.float32)
    bst[:16, 0] = g("b_state")
    d["bstateR"] = bst
    d["bdoutR"] = np.full((128, 1), g("b_damage_out")[0], np.float32)
    d["bboR"] = np.full((1, 512), g("b_bond_out")[0], np.float32)
    d["relembT"] = np.ascontiguousarray(g("rel_pos_emb").T).astype(np.float32)
    d["wpos"] = np.ascontiguousarray(g("w_pos")).astype(np.float32)
    dd = (np.arange(8)[:, None, None] + 8 * np.arange(4)[None, :, None])
    tt = np.arange(128)[None, None, :]
    valid = (tt - 31 + dd) >= 0
    cmask_strain = np.repeat(valid[:, None, :, :], 16, axis=1).reshape(128, 512)
    d["cmstrain"] = cmask_strain.astype(np.float32).astype(bf16)
    clm = np.where(valid, 0.0, -1e9).astype(np.float32)
    clm_full = np.zeros((128, 512), np.float32)
    for h in range(16):
        for ds in range(8):
            clm_full[8 * h + ds] = clm[ds].reshape(512)
    d["clmask"] = clm_full
    inv = np.minimum(np.arange(128) + 1, 32).astype(np.float32)
    d["invcnt"] = np.broadcast_to(1.0 / inv, (16, 128)).copy()
    d["ident"] = np.eye(128, dtype=np.float32).astype(bf16)
    d["identf"] = np.eye(128, dtype=np.float32)
    d["epsv"] = np.full((128, 1), EPS_LN, np.float32)
    d["ones1"] = np.ones((1, 128), np.float32)
    return d


def emit_l1_attn(nc, prog, ar, aps, pool):
    TP = 4128
    REG = 32768
    NAMES = C_NAMES
    x = aps["interB"]
    ap_in = {n: aps["C_" + n] for n, _, _ in NAMES}
    y = aps["interC"]
    dispT = aps["dispT"]
    stateT = aps["stateT"]
    valS = aps["valS"]
    bexp = aps["bexp"]
    posscr = aps["posscr"]

    if True:
        sb = ar.sb
        s_in = {n: sb("s_" + n, s, dt) for n, s, dt in NAMES}

        s_x = [sb(f"s_x{j}", [128, D]) for j in range(2)]
        s_sq = sb("s_sq", [128, D])
        s_mean = [sb(f"s_mean{j}", [128, 1]) for j in range(2)]
        s_ss = sb("s_ss", [128, 1])
        s_nvar = sb("s_nvar", [128, 1])
        s_std = sb("s_std", [128, 1])
        s_rstd = [sb(f"s_rstd{j}", [128, 1]) for j in range(2)]
        s_negmb = [sb(f"s_negmb{j}", [128, 1]) for j in range(2)]
        s_xn = [sb(f"s_xn{j}", [128, D], BF16) for j in range(2)]
        s_xnT = [sb(f"s_xnT{j}", [128, D], BF16) for j in range(2)]
        s_dv = [sb(f"s_dv{j}", [128, 1024 + 256], BF16) for j in range(2)]
        s_dtr = [sb(f"s_dtr{j}", [128, 256], BF16) for j in range(2)]
        s_zero = sb("s_zero", [128, 1024], BF16)
        s_posj = sb("s_posj", [32, 16])
        s_posb4 = sb("s_posb4", [128, 4])
        s_posb = sb("s_posb", [128, 512])
        s_e1 = [sb(f"s_e1{j}", [128, 512]) for j in range(2)]
        s_dwin = sb("s_dwin", [128, 16 * 160], BF16)
        s_dcen = sb("s_dcen", [128, 16 * 128], BF16)
        s_swin = sb("s_swin", [128, 16 * 160], BF16)
        s_scen = sb("s_scen", [128, 16 * 128], BF16)
        s_strain = sb("s_strain", [128, 16 * 512], BF16)
        s_strainM = sb("s_strainM", [128, 512], BF16)
        s_state = [sb(f"s_state{j}", [16, 128], BF16) for j in range(2)]
        s_ba = sb("s_ba", [128, 512], BF16)
        s_da = sb("s_da", [128, 512], BF16)
        s_sg = sb("s_sg", [128, 512])
        s_lg = sb("s_lg", [128, 512])
        s_wall = sb("s_wall", [128, 512])
        s_wexp = sb("s_wexp", [128, 512], BF16)
        s_den = sb("s_den", [128, 16])
        s_rden = sb("s_rden", [128, 16])
        s_bt = [sb(f"s_bt{j}", [128, 256], BF16) for j in range(2)]
        s_bA = sb("s_bA", [128, 128], BF16)
        s_bB = sb("s_bB", [128, 128], BF16)
        s_vA = [sb(f"s_vA{j}", [128, 64], BF16) for j in range(2)]
        s_vB = [sb(f"s_vB{j}", [128, 64], BF16) for j in range(2)]
        s_attno = sb("s_attno", [128, 1024], BF16)
        s_attnoT = sb("s_attnoT", [128, 1024], BF16)
        s_out = [sb(f"s_out{j}", [128, D]) for j in range(2)]

        p_tr = pool["tr"]
        p_s2 = pool["f0"][:, :]
        p_s3 = pool["f1"][:, :]
        p_bl = pool["f2"][:, :]
        p_dl = pool["f3"][:, :]
        p_cp = pool["w"][:, :]

        def body_const(E):
            for name in s_in:
                E.op(lambda e, name=name: e.dma_start(out=s_in[name][:, :], in_=ap_in[name]),
                     ch="const", key=("c", name), k=16)
        prog.body("sync", body_const)
        LASTC = ("c", NAMES[-1][0])

        def body_zero(E):
            E.op(lambda e: e.memset(s_zero[:, :], 0.0), ch="d0", key="z0")
            E.wait("d0", "z0")
            for si, scr in enumerate((dispT, stateT)):
                for half in range(2):
                    E.op(lambda e, scr=scr, half=half: e.dma_start(
                        out=AP(scr.tensor, half * 128 * TP, [[TP, 128], [1, 31]]),
                        in_=s_zero[:, 0:31]),
                        ch="d0", key=("zg", si, half), k=16)
            E.op(lambda e: e.dma_start(
                out=AP(valS.tensor, 0, [[1024, 31], [1, 1024]]),
                in_=s_zero[0:31, :]), ch="d0", key="zv", k=16)
            for h in range(16):
                E.op(lambda e, h=h: e.dma_start(
                    out=AP(bexp.tensor, h * REG, [[512, 64], [1, 512]]),
                    in_=s_zero[0:64, 0:512]),
                    ch="d0", key=("zb", h), k=16)
        prog.body("gpsimd", body_zero)

        def pos_tensor(E):
            E.wait("const", LASTC)
            E.op(lambda e: e.matmul(p_s2[0:32, 0:16], s_in["relembT"][:, :],
                                    s_in["wpos"][:, :], start=True, stop=True,
                                    skip_group_check=True),
                 ch="t", key="posmm")
        prog.body("tensor", pos_tensor)

        def pos_vector(E):
            E.wait("t", "posmm")
            E.op(lambda e: e.tensor_copy(s_posj[:, :], p_s2[0:32, 0:16]),
                 ch="v", key="posev")
        prog.body("vector", pos_vector)

        def pos_gpsimd(E):
            E.wait("v", "posev")
            E.op(lambda e: e.dma_start(
                out=AP(posscr.tensor, 0, [[16, 32], [1, 16]]), in_=s_posj[:, :]),
                ch="d0", key="posw", k=16)
            E.wait("d0", "posw")
            E.op(lambda e: e.dma_start(
                out=s_posb4[:, :],
                in_=AP(posscr.tensor, 0, [[16, 8], [1, 16], [128, 4]]),
                allow_slow_non_contiguous=True),
                ch="d0", key="posr", k=16)
        prog.body("gpsimd", pos_gpsimd)

        def pos_vector2(E):
            E.wait("d0", "posr")
            for do in range(4):
                E.op(lambda e, do=do: e.tensor_copy(
                    s_posb[:, do * 128:(do + 1) * 128],
                    s_posb4[:, do:do + 1].to_broadcast([128, 128])),
                    ch="v", key=("posbc", do))
            E.wait("v", ("posbc", 3))
            E.op(lambda e: e.tensor_scalar(
                s_posb[:, :], s_posb[:, :], s_in["bstrainR"][:, 0:1], None, ALU.add),
                ch="v", key="posb")
        prog.body("vector", pos_vector2)

        def p0_sync(E):
            for i in range(NT):
                j = i % 2
                if i >= 2:
                    E.wait("s", ("xn", i - 2))
                E.op(lambda e, i=i, j=j: e.dma_start(
                    out=s_x[j][:, :], in_=x[i * 128:(i + 1) * 128, :]),
                    ch="x", key=("x", i), k=16)
        prog.body("sync", p0_sync)

        def p0_scalar(E):
            E.wait("const", LASTC)
            for i in range(NT):
                j = i % 2
                E.wait("x", ("x", i))
                if i >= 1:
                    E.wait("v", ("ss", i - 1))
                E.op(lambda e, j=j: e.activation(s_sq[:, :], s_x[j][:, :], AF.Square),
                     ch="s", key=("sq", i))
                E.wait("v", ("nvar", i))
                if i >= 1:
                    E.wait("v", ("rstd", i - 1))
                E.op(lambda e: e.activation(s_std[:, :], s_nvar[:, :], AF.Sqrt,
                                            bias=s_in["epsv"][:, 0:1], scale=-1.0),
                     ch="s", key=("std", i))
                E.wait("v", ("negmb", i))
                if i >= 2:
                    E.wait("t", ("tr", i - 2, 7))
                E.op(lambda e, j=j: e.activation(s_xn[j][:, :], s_x[j][:, :], AF.Identity,
                                                 bias=s_negmb[j][:, 0:1],
                                                 scale=s_rstd[j][:, 0:1]),
                     ch="s", key=("xn", i))
                E.wait("t", ("vm", i, 0))
                if i >= 2:
                    E.wait("d0", ("vst", i - 2))
                E.op(lambda e, j=j: e.activation(s_dv[j][:, 0:512], p_s2[:, :], AF.Identity),
                     ch="s", key=("ve", i, 0))
                E.wait("t", ("vm", i, 1))
                E.op(lambda e, j=j: e.activation(s_dv[j][:, 512:1024], p_s2[:, :], AF.Identity),
                     ch="s", key=("ve", i, 1))
                E.wait("t", ("dm", i))
                E.op(lambda e, j=j: e.activation(s_dv[j][:, 1024:1280], p_s2[:, 0:256],
                                                 AF.Identity), ch="s", key=("de", i))
        prog.body("scalar", p0_scalar)

        def p0_vector(E):
            for i in range(NT):
                j = i % 2
                E.wait("x", ("x", i))
                E.op(lambda e, j=j: e.reduce_sum(s_mean[j][:, :], s_x[j][:, :], axis=AX.X),
                     ch="v", key=("meansum", i))
                E.wait("v", ("meansum", i))
                E.op(lambda e, j=j: e.tensor_scalar_mul(s_mean[j][:, :], s_mean[j][:, :],
                                                        1.0 / D), ch="v", key=("mean", i))
                E.wait("s", ("sq", i))
                E.op(lambda e: e.reduce_sum(s_ss[:, :], s_sq[:, :], axis=AX.X),
                     ch="v", key=("ssr", i))
                E.wait("v", ("ssr", i))
                E.op(lambda e: e.tensor_scalar_mul(s_ss[:, :], s_ss[:, :], 1.0 / D),
                     ch="v", key=("ss", i))
                E.wait("v", ("ss", i))
                E.op(lambda e, j=j: e.scalar_tensor_tensor(
                    s_nvar[:, :], s_mean[j][:, :], s_mean[j][:, 0:1], s_ss[:, :],
                    ALU.mult, ALU.subtract), ch="v", key=("nvar", i))
                E.wait("s", ("std", i))
                E.op(lambda e, j=j: e.reciprocal(s_rstd[j][:, :], s_std[:, :]),
                     ch="v", key=("rstd", i))
                E.wait("v", ("rstd", i))
                E.op(lambda e, j=j: e.tensor_scalar(
                    s_negmb[j][:, :], s_mean[j][:, :], s_rstd[j][:, 0:1], -1.0,
                    ALU.mult, ALU.mult), ch="v", key=("negmb", i))
                for c in range(8):
                    E.wait("t", ("tr", i, c))
                    E.op(lambda e, j=j, c=c: e.tensor_copy(
                        s_xnT[j][:, c * 128:(c + 1) * 128], p_tr[c % 2][:, :]),
                        ch="v", key=("ev", i, c))
                for c in range(2):
                    E.wait("t", ("dtr", i, c))
                    if c == 0 and i >= 2:
                        E.wait("d0", ("dst", i - 2, 1))
                    E.op(lambda e, j=j, c=c: e.tensor_copy(
                        s_dtr[j][:, c * 128:(c + 1) * 128], p_tr[c % 2][:, :]),
                        ch="v", key=("dev", i, c))
        prog.body("vector", p0_vector)

        def p0_tensor(E):
            for i in range(NT):
                j = i % 2
                E.wait("s", ("xn", i))
                for c in range(8):
                    k = 8 * i + c
                    if k >= 2:
                        kk = k - 2
                        E.wait("v", ("ev", kk // 8, kk % 8))
                    E.op(lambda e, j=j, c=c: e.transpose(
                        p_tr[c % 2][:, :], s_xn[j][:, c * 128:(c + 1) * 128],
                        s_in["ident"][:, :]), ch="t", key=("tr", i, c))
                for half in range(2):
                    if i == 0 and half == 0:
                        E.wait("v", "posev")
                    else:
                        E.wait("s", ("ve", i, half - 1) if half == 1 else ("de", i - 1))
                    E.op(lambda e, half=half: e.matmul(
                        p_s2[:, :], s_in["ones1"][:, :],
                        s_in["bval"][0:1, half * 512:(half + 1) * 512],
                        start=True, stop=False, skip_group_check=True))
                    for c in range(8):
                        E.wait("v", ("ev", i, c))
                        E.op(lambda e, j=j, c=c, half=half: e.matmul(
                            p_s2[:, :], s_xnT[j][:, c * 128:(c + 1) * 128],
                            s_in["wval"][:, c * 1024 + half * 512: c * 1024 + (half + 1) * 512],
                            start=False, stop=(c == 7), skip_group_check=True),
                            ch=("t" if c == 7 else None),
                            key=(("vm", i, half) if c == 7 else None))
                E.wait("s", ("ve", i, 1))
                E.op(lambda e: e.matmul(
                    p_s2[:, 0:256], s_in["ones1"][:, :], s_in["bdisp"][0:1, :],
                    start=True, stop=False, skip_group_check=True))
                for c in range(8):
                    E.op(lambda e, j=j, c=c: e.matmul(
                        p_s2[:, 0:256], s_xnT[j][:, c * 128:(c + 1) * 128],
                        s_in["wdisp"][:, c * 256:(c + 1) * 256],
                        start=False, stop=(c == 7), skip_group_check=True),
                        ch=("t" if c == 7 else None), key=(("dm", i) if c == 7 else None))
                E.wait("s", ("de", i))
                for c in range(2):
                    E.wait("v", ("dev", i, c - 2) if c >= 2 else
                           (("dev", i - 1, c) if i >= 1 else ("ev", i, 7)))
                    E.op(lambda e, j=j, c=c: e.transpose(
                        p_tr[c % 2][:, :], s_dv[j][:, 1024 + c * 128: 1024 + (c + 1) * 128],
                        s_in["ident"][:, :]), ch="t", key=("dtr", i, c))
        prog.body("tensor", p0_tensor)

        def p0_gpsimd(E):
            E.wait("d0", ("zb", 15))
            E.wait("d0", "posr")
            for i in range(NT):
                j = i % 2
                E.wait("s", ("ve", i, 1))
                E.op(lambda e, i=i, j=j: e.dma_start(
                    out=AP(valS.tensor, (31 + i * 128) * 1024, [[1024, 128], [1, 1024]]),
                    in_=s_dv[j][:, 0:1024]), ch="d0", key=("vst", i), k=16)
                for c in range(2):
                    E.wait("v", ("dev", i, c))
                    E.op(lambda e, i=i, j=j, c=c: e.dma_start(
                        out=AP(dispT.tensor, c * 128 * TP + 31 + i * 128,
                               [[TP, 128], [1, 128]]),
                        in_=s_dtr[j][:, c * 128:(c + 1) * 128]),
                        ch="d0", key=("dst", i, c), k=16)
        prog.body("gpsimd", p0_gpsimd)

        for i in range(NT):
            t0 = i * 128

            def p1_sl(E, i=i, t0=t0):
                if i == 0:
                    E.wait("d0", ("dst", NT - 1, 1))
                    E.wait("d0", ("vst", NT - 1))
                for ds in range(8):
                    if i >= 1 and ds == 0:
                        E.wait("v", ("strain", i - 1, 15))
                    E.op(lambda e, ds=ds, t0=t0: e.dma_start(
                        out=AP(s_dwin, ds * 16 * (16 * 160), [[16 * 160, 16], [160, 16],
                                                              [1, 152]]),
                        in_=AP(dispT.tensor, t0 + ds, [[TP, 16], [16 * TP, 16], [1, 152]])),
                        ch="sl", key=("dw", i, ds), k=16)
                    E.op(lambda e, ds=ds, t0=t0: e.dma_start(
                        out=AP(s_dcen, ds * 16 * (16 * 128), [[16 * 128, 16], [128, 16],
                                                              [1, 128]]),
                        in_=AP(dispT.tensor, 31 + t0, [[TP, 16], [16 * TP, 16], [1, 128]])),
                        ch="sl", key=("dc", i, ds), k=16)
                for ds in range(8):
                    if ds == 0:
                        E.wait("d1", ("sst", i, 15))
                    if i >= 1 and ds == 0:
                        E.wait("t", ("bd", i - 1, 15))
                    E.op(lambda e, ds=ds, t0=t0: e.dma_start(
                        out=AP(s_swin, ds * 16 * (16 * 160), [[16 * 160, 16], [160, 16],
                                                              [1, 152]]),
                        in_=AP(stateT.tensor, t0 + ds, [[TP, 16], [16 * TP, 16], [1, 152]])),
                        ch="sl", key=("sw", i, ds), k=16)
                    E.op(lambda e, ds=ds, t0=t0: e.dma_start(
                        out=AP(s_scen, ds * 16 * (16 * 128), [[16 * 128, 16], [128, 16],
                                                              [1, 128]]),
                        in_=AP(stateT.tensor, 31 + t0, [[TP, 16], [16 * TP, 16], [1, 128]])),
                        ch="sl", key=("sc", i, ds), k=16)
                for h in range(16):
                    if h >= 2:
                        E.wait("t", ("av", i, h - 2))
                    elif i >= 1:
                        E.wait("t", ("av", i - 1, 14 + h))
                    if i >= 1:
                        E.op(lambda e, h=h, t0=t0: e.dma_start(
                            out=s_vA[h % 2][:, :],
                            in_=AP(valS.tensor, (31 + t0 - 128) * 1024 + h * 64,
                                   [[1024, 128], [1, 64]])),
                            ch="sl", key=("vA", i, h), k=16)
                    E.op(lambda e, h=h, t0=t0: e.dma_start(
                        out=s_vB[h % 2][:, :],
                        in_=AP(valS.tensor, (31 + t0) * 1024 + h * 64,
                               [[1024, 128], [1, 64]])),
                        ch="sl", key=("vB", i, h), k=16)
                if i == 0:
                    E.wait("s", ("xn", NT - 1))
                elif i >= 2:
                    E.wait("v", ("add", i - 2))
                E.op(lambda e, i=i: e.dma_start(
                    out=s_x[i % 2][:, :], in_=x[i * 128:(i + 1) * 128, :]),
                    ch="x", key=("x3", i), k=16)
            prog.body("sync", p1_sl)

            def p1_vec_strain(E, i=i):
                for h in range(16):
                    if h == 0:
                        E.wait("sl", ("dw", i, 7))
                        E.wait("sl", ("dc", i, 7))
                        if i >= 1:
                            E.wait("t", ("dmg", i - 1, 15))
                    E.op(lambda e, h=h: e.tensor_tensor(
                        s_strain[:, h * 512:(h + 1) * 512],
                        AP(s_dwin, h * 160, [[16 * 160, 128], [8, 4], [1, 128]]),
                        AP(s_dcen, h * 128, [[16 * 128, 128], [0, 4], [1, 128]]),
                        ALU.subtract),
                        ch="v", key=("strain", i, h))
                    if i == 0:
                        E.wait("v", ("strain", i, h))
                        E.op(lambda e, h=h: e.tensor_mul(
                            s_strainM[:, :], s_strain[:, h * 512:(h + 1) * 512],
                            s_in["cmstrain"][:, :]), ch="v", key=("strainM", i, h))
            prog.body("vector", p1_vec_strain)

            def p1_ten_state(E, i=i):
                for h in range(16):
                    E.wait("v", ("strainM", i, h) if i == 0 else ("strain", i, h))
                    if h <= 1 and i >= 1:
                        E.wait("v", ("attno", i - 1, 15))
                    if h >= 2:
                        E.wait("v", ("stfin", i, h - 2))
                    for do in range(4):
                        E.op(lambda e, h=h, do=do, i=i: e.matmul(
                            (p_s2 if h % 2 == 0 else p_s3)[0:16, 0:128],
                            s_in["sumbd"][:, :],
                            (s_strainM[:, do * 128:(do + 1) * 128] if i == 0 else
                             s_strain[:, h * 512 + do * 128: h * 512 + (do + 1) * 128]),
                            start=(do == 0), stop=(do == 3), skip_group_check=True),
                            ch=("t" if do == 3 else None),
                            key=(("stmm", i, h) if do == 3 else None))
            prog.body("tensor", p1_ten_state)

            def p1_vec_state(E, i=i):
                for h in range(16):
                    E.wait("t", ("stmm", i, h))
                    if h >= 2:
                        E.wait("d1", ("sst", i, h - 2))
                    psrc = (p_s2 if h % 2 == 0 else p_s3)[0:16, 0:128]
                    if i == 0:
                        E.op(lambda e, h=h, psrc=psrc: e.tensor_mul(
                            s_state[h % 2][:, :], psrc, s_in["invcnt"][:, :]),
                            ch="v", key=("stf0", i, h))
                        E.wait("v", ("stf0", i, h))
                        E.op(lambda e, h=h: e.tensor_scalar(
                            s_state[h % 2][:, :], s_state[h % 2][:, :],
                            s_in["bstateR"][0:16, 0:1], None, ALU.add),
                            ch="v", key=("stfin", i, h))
                    else:
                        E.op(lambda e, h=h, psrc=psrc: e.tensor_scalar(
                            s_state[h % 2][:, :], psrc, 1.0 / 32,
                            s_in["bstateR"][0:16, 0:1], ALU.mult, ALU.add),
                            ch="v", key=("stfin", i, h))
            prog.body("vector", p1_vec_state)

            def p1_gp_state(E, i=i, t0=t0):
                for h in range(16):
                    E.wait("v", ("stfin", i, h))
                    E.op(lambda e, h=h, t0=t0: e.dma_start(
                        out=AP(stateT.tensor, h * 16 * TP + 31 + t0, [[TP, 16], [1, 128]]),
                        in_=s_state[h % 2][:, :]), ch="d1", key=("sst", i, h), k=16)
            prog.body("gpsimd", p1_gp_state)

            def p1_ten_bond(E, i=i):
                for h in range(16):
                    if h == 0:
                        E.wait("sl", ("sw", i, 7))
                        E.wait("sl", ("sc", i, 7))
                        E.wait("v", ("stfin", i, 15))
                        E.wait("v", "posb")
                        if i >= 1:
                            E.wait("s", ("sgm", i - 1))
                            E.wait("v", ("lgs", i - 1))
                    else:
                        E.wait("v", ("e1", i, h - 1))
                    def _rhs_bond(h, kind):
                        if kind == 0:
                            return s_strain[:, h * 512:(h + 1) * 512]
                        if kind == 1:
                            return AP(s_swin, h * 160, [[16 * 160, 128], [8, 4], [1, 128]])
                        return AP(s_scen, h * 128, [[16 * 128, 128], [0, 4], [1, 128]])
                    for mi, lhs in enumerate(("bd_strain", "bd_statej", "bd_statei")):
                        E.op(lambda e, h=h, lhs=lhs, mi=mi: e.matmul(
                            p_s2[:, :], s_in[lhs][:, :], _rhs_bond(h, mi),
                            start=(mi == 0), stop=(mi == 2), skip_group_check=True),
                            ch=("t" if mi == 2 else None),
                            key=(("bd", i, h) if mi == 2 else None))
                    if h >= 1:
                        E.wait("s", ("dae", i, h - 1))
                    E.op(lambda e, h=h: e.matmul(
                        p_s3[:, :], s_in["bd_damage"][:, :],
                        s_strain[:, h * 512:(h + 1) * 512],
                        start=True, stop=True, skip_group_check=True),
                        ch="t", key=("dmg", i, h))
                    if h == 0:
                        E.op(lambda e: e.matmul(
                            p_bl[:, :], s_in["ones1"][:, :], s_in["bboR"][:, :],
                            start=True, stop=False, skip_group_check=True))
                    else:
                        E.wait("s", ("bae", i, h - 1))
                        E.op(lambda e, h=h: e.matmul(
                            p_bl[:, :], s_in["bolhs"][:, (h - 1) * 128:h * 128],
                            s_ba[:, :], start=False, stop=False, skip_group_check=True),
                            ch="t", key=("blm", i, h - 1))
                        E.wait("s", ("dae", i, h - 1))
                        E.op(lambda e, h=h: e.matmul(
                            p_dl[:, :], s_in["dolhs"][:, (h - 1) * 128:h * 128],
                            s_da[:, :], start=(h == 1), stop=False, skip_group_check=True),
                            ch="t", key=("dlm", i, h - 1))
                E.wait("s", ("bae", i, 15))
                E.op(lambda e: e.matmul(
                    p_bl[:, :], s_in["bolhs"][:, 15 * 128:16 * 128],
                    s_ba[:, :], start=False, stop=True, skip_group_check=True),
                    ch="t", key=("blm", i, 15))
                E.wait("s", ("dae", i, 15))
                E.op(lambda e: e.matmul(
                    p_dl[:, :], s_in["dolhs"][:, 15 * 128:16 * 128],
                    s_da[:, :], start=False, stop=True, skip_group_check=True),
                    ch="t", key=("dlm", i, 15))
            prog.body("tensor", p1_ten_bond)

            def p1_vec_e1(E, i=i):
                for h in range(16):
                    E.wait("t", ("bd", i, h))
                    if h >= 2:
                        E.wait("s", ("bae", i, h - 2))
                    E.op(lambda e, h=h: e.tensor_add(
                        s_e1[h % 2][:, :], p_s2[:, :], s_posb[:, :]),
                        ch="v", key=("e1", i, h))
            prog.body("vector", p1_vec_e1)

            def p1_sc_bond(E, i=i):
                for h in range(16):
                    E.wait("v", ("e1", i, h))
                    if h >= 1:
                        E.wait("t", ("blm", i, h - 1))
                    E.op(lambda e, h=h: e.activation(s_ba[:, :], s_e1[h % 2][:, :],
                                                     AF.Gelu),
                         ch="s", key=("bae", i, h))
                    E.wait("t", ("dmg", i, h))
                    if h >= 1:
                        E.wait("t", ("dlm", i, h - 1))
                    E.op(lambda e: e.activation(s_da[:, :], p_s3[:, :], AF.Gelu,
                                                bias=s_in["bdamageR"][:, 0:1]),
                         ch="s", key=("dae", i, h))
                E.wait("t", ("dlm", i, 15))
                E.op(lambda e: e.activation(s_sg[:, :], p_dl[:, :], AF.Sigmoid,
                                            bias=s_in["bdoutR"][:, 0:1]),
                     ch="s", key=("sgm", i))
            prog.body("scalar", p1_sc_bond)

            def p1_vec_logit(E, i=i):
                E.wait("s", ("sgm", i))
                E.wait("t", ("blm", i, 15))
                E.op(lambda e: e.scalar_tensor_tensor(
                    s_lg[:, :], s_sg[:, :], -10.0, p_bl[:, :], ALU.mult, ALU.add),
                    ch="v", key=("lg0", i))
                if i == 0:
                    E.wait("v", ("lg0", i))
                    E.op(lambda e: e.tensor_add(s_lg[:, :], s_lg[:, :],
                                                s_in["clmask"][:, :]),
                         ch="v", key=("lgs", i))
                else:
                    E.wait("v", ("lg0", i))
                    E.op(lambda e: e.tensor_copy(s_lg[:, :], s_lg[:, :]),
                         ch="v", key=("lgs", i))
            prog.body("vector", p1_vec_logit)

            def p1_ten_ltr(E, i=i):
                E.wait("v", ("lgs", i))
                for do in range(4):
                    if do >= 1:
                        E.wait("v", ("wev", i, do - 1))
                    E.op(lambda e, do=do: e.transpose(
                        p_bl[:, 0:128], s_lg[:, do * 128:(do + 1) * 128],
                        s_in["identf"][:, :]),
                        ch="t", key=("ltr", i, do))
            prog.body("tensor", p1_ten_ltr)

            def p1_vec_wall(E, i=i):
                for do in range(4):
                    E.wait("t", ("ltr", i, do))
                    E.op(lambda e, do=do: e.tensor_copy(
                        AP(s_wall, 8 * do, [[512, 128], [32, 16], [1, 8]]),
                        p_bl[:, 0:128]), ch="v", key=("wev", i, do))
            prog.body("vector", p1_vec_wall)

            def p1_sc_exp(E, i=i):
                E.wait("v", ("wev", i, 3))
                E.op(lambda e: e.activation(s_wexp[:, :], s_wall[:, :], AF.Exp),
                     ch="s", key=("wex", i))
            prog.body("scalar", p1_sc_exp)

            def p1_vec_den(E, i=i):
                E.wait("s", ("wex", i))
                for h in range(16):
                    E.op(lambda e, h=h: e.reduce_sum(
                        s_den[:, h:h + 1], s_wexp[:, h * 32:(h + 1) * 32], axis=AX.X),
                        ch="v", key=("den", i, h))
                E.wait("v", ("den", i, 15))
                E.op(lambda e: e.reciprocal(s_rden[:, :], s_den[:, :]),
                     ch="v", key=("rden", i))
            prog.body("vector", p1_vec_den)

            def p1_gp_av(E, i=i):
                E.wait("s", ("wex", i))
                if i >= 1:
                    E.wait("t", ("btr", i - 1, 15, 1))
                E.op(lambda e: e.dma_start(
                    out=AP(bexp.tensor, 97, [[257, 128], [REG, 16], [1, 32]]),
                    in_=AP(s_wexp, 0, [[512, 128], [32, 16], [1, 32]])),
                    ch="d1", key=("shw", i), k=16)
                for h in range(16):
                    E.wait("d1", ("shw", i))
                    if h >= 2:
                        E.wait("t", ("btr", i, h - 2, 1))
                    E.op(lambda e, h=h: e.dma_start(
                        out=s_bt[h % 2][:, :],
                        in_=AP(bexp.tensor, h * REG, [[256, 128], [1, 256]])),
                        ch="d1", key=("rb", i, h), k=16)
            prog.body("gpsimd", p1_gp_av)

            def p1_ten_av(E, i=i):
                for h in range(16):
                    E.wait("d1", ("rb", i, h))
                    for c in range(2):
                        E.wait("v", ("bev", i, h - 1, c) if h >= 1 else
                               (("bev", i - 1, 15, c) if i >= 1 else ("dev", NT - 1, c)))
                        E.op(lambda e, c=c, h=h: e.transpose(
                            p_tr[c % 2][:, :], s_bt[h % 2][:, c * 128:(c + 1) * 128],
                            s_in["ident"][:, :]), ch="t", key=("btr", i, h, c))
                    E.wait("v", ("bev", i, h, 1))
                    E.wait("sl", ("vB", i, h))
                    if h == 0:
                        E.wait("v", ("attno", i - 1, 15) if i >= 1 else ("rden", i))
                        E.wait("v", ("rden", i))
                    nblk = 2 if i >= 1 else 1
                    if i >= 1:
                        E.op(lambda e, h=h: e.matmul(
                            p_s2[:, 0:64], s_bA[:, :], s_vA[h % 2][:, :],
                            start=True, stop=False, skip_group_check=True))
                    E.op(lambda e, first=(i == 0), h=h: e.matmul(
                        p_s2[:, 0:64], s_bB[:, :], s_vB[h % 2][:, :],
                        start=first, stop=True, skip_group_check=True),
                        ch="t", key=("av", i, h))
            prog.body("tensor", p1_ten_av)

            def p1_vec_bev(E, i=i):
                for h in range(16):
                    for c in range(2):
                        E.wait("t", ("btr", i, h, c))
                        E.op(lambda e, c=c: e.tensor_copy(
                            (s_bA if c == 0 else s_bB)[:, :], p_tr[c % 2][:, :]),
                            ch="v", key=("bev", i, h, c))
                    E.wait("t", ("av", i, h))
                    E.op(lambda e, h=h: e.tensor_scalar_mul(
                        s_attno[:, h * 64:(h + 1) * 64], p_s2[:, 0:64],
                        s_rden[:, h:h + 1]), ch="v", key=("attno", i, h))
            prog.body("vector", p1_vec_bev)

            def p1_ten_cproj(E, i=i):
                for c in range(8):
                    E.wait("v", ("attno", i, 15))
                    E.wait("v", ("atev", i, c - 2) if c >= 2 else
                           (("atev", i - 1, 6 + c) if i >= 1 else ("bev", i, 15, c)))
                    E.op(lambda e, c=c: e.transpose(
                        p_tr[c % 2][:, :], s_attno[:, c * 128:(c + 1) * 128],
                        s_in["ident"][:, :]), ch="t", key=("atr", i, c))
                if i >= 1:
                    E.wait("v", ("add", i - 1))
                for half in range(2):
                    E.op(lambda e, half=half: e.matmul(
                        p_cp[:, half * 512:(half + 1) * 512], s_in["ones1"][:, :],
                        s_in["bcproj"][0:1, half * 512:(half + 1) * 512],
                        start=True, stop=False, skip_group_check=True))
                    for c in range(8):
                        E.wait("v", ("atev", i, c))
                        E.op(lambda e, c=c, half=half: e.matmul(
                            p_cp[:, half * 512:(half + 1) * 512],
                            s_attnoT[:, c * 128:(c + 1) * 128],
                            s_in["wcproj"][:, c * 1024 + half * 512:
                                           c * 1024 + (half + 1) * 512],
                            start=False, stop=(c == 7), skip_group_check=True),
                            ch=("t" if (half == 1 and c == 7) else None),
                            key=(("cpm", i) if (half == 1 and c == 7) else None))
            prog.body("tensor", p1_ten_cproj)

            def p1_vec_cproj(E, i=i):
                for c in range(8):
                    E.wait("t", ("atr", i, c))
                    E.op(lambda e, c=c: e.tensor_copy(
                        s_attnoT[:, c * 128:(c + 1) * 128], p_tr[c % 2][:, :]),
                        ch="v", key=("atev", i, c))
                E.wait("t", ("cpm", i))
                E.wait("x", ("x3", i))
                if i >= 2:
                    E.wait("st", ("y", i - 2))
                E.op(lambda e, i=i: e.tensor_add(s_out[i % 2][:, :], s_x[i % 2][:, :],
                                                 p_cp[:, :]), ch="v", key=("add", i))
            prog.body("vector", p1_vec_cproj)

            def p1_gp_store(E, i=i):
                E.wait("v", ("add", i))
                E.op(lambda e, i=i: e.dma_start(
                    out=y[i * 128:(i + 1) * 128, :], in_=s_out[i % 2][:, :]),
                    ch="st", key=("y", i), k=16)
                if i == NT - 1:
                    E.wait("st", ("y", i))
            prog.body("gpsimd", p1_gp_store)




ARENA_BYTES = 196608
TPC = 4128
REGC = 32768


def build_fused(debug_inter=False):
    nc = bass.Bass("TRN2", target_bir_lowering=False)
    aps = {}
    aps["x"] = nc.dram_tensor("x", [T, D], F32, kind="ExternalInput").ap()
    aps["y"] = nc.dram_tensor("y", [T, D], F32, kind="ExternalOutput").ap()
    for n in ("interA", "interB", "interC"):
        aps[n] = nc.dram_tensor(n, [T, D], F32, kind="ExternalOutput").ap()
    aps["xnT_scr"] = nc.dram_tensor("xnT_scr", [D, T], BF16).ap()
    aps["xnT2"] = nc.dram_tensor("xnT2", [D, T], BF16).ap()
    aps["dispT"] = nc.dram_tensor("dispT", [256, TPC], BF16).ap()
    aps["stateT"] = nc.dram_tensor("stateT", [256, TPC], BF16).ap()
    aps["valS"] = nc.dram_tensor("valS", [31 + T + 1, 1024], BF16).ap()
    aps["bexp"] = nc.dram_tensor("bexp", [16 * REGC], BF16).ap()
    aps["posscr"] = nc.dram_tensor("posscr", [512], F32).ap()
    ins_specs = [
        ("A_wqT", [D, D], BF16), ("A_wkT", [D, D], BF16), ("A_wvT", [D, D], BF16),
        ("A_woT", [D, D], BF16), ("A_ident", [128, 128], BF16),
        ("A_masks", [12, 128, 512], BF16), ("A_epsv", [128, 1], F32),
        ("B_wup", [8, D, 5632], BF16), ("B_wdn", [8, 2816, D], BF16),
        ("B_wrt", [128, 64], BF16), ("B_nsc", [128, 8], F32),
        ("B_ident", [128, 128], BF16), ("B_epsv", [128, 1], F32),
        ("B_epsr", [128, 1], F32),
        ("D_wg1", [128, 8 * 256], BF16), ("D_bg1", [128, 2], F32),
        ("D_wg2", [128, 40], BF16), ("D_bg2", [1, 20], F32),
        ("D_we", [20, 128, 8 * 1024], BF16), ("D_be", [20, 1024], F32),
        ("D_ident", [128, 128], BF16), ("D_identf", [128, 128], F32),
        ("D_epsv", [128, 1], F32),
        ("D_ones1", [1, 128], F32),
    ] + [("C_" + n, s, dt) for n, s, dt in C_NAMES]
    for n, s, dt in ins_specs:
        aps[n] = nc.dram_tensor(n, s, dt, kind="ExternalInput").ap()

    ctx = ExitStack()
    with ctx:
        ar = Arena(nc, ctx, ARENA_BYTES)
        pool = {
            "f0": ctx.enter_context(nc.psum_tensor("pool_f0", [128, 512], F32)),
            "f1": ctx.enter_context(nc.psum_tensor("pool_f1", [128, 512], F32)),
            "f2": ctx.enter_context(nc.psum_tensor("pool_f2", [128, 512], F32)),
            "f3": ctx.enter_context(nc.psum_tensor("pool_f3", [128, 512], F32)),
            "b6": ctx.enter_context(nc.psum_tensor("pool_b6", [128, 512], F32)),
            "b7": ctx.enter_context(nc.psum_tensor("pool_b7", [128, 512], F32)),
            "w": ctx.enter_context(nc.psum_tensor("pool_w", [128, 1024], F32)),
        }
        pool["tr"] = [pool["b6"][:, :].bitcast(BF16)[:, 0:128],
                      pool["b7"][:, :].bitcast(BF16)[:, 0:128]]
        phase_chans = {
            "A.": ["const", "x", "v", "s", "t", "scr", "xc", "x3", "st"],
            "B.": ["const", "x", "v", "s", "t", "scr", "wt", "xs", "yst"],
            "C.": ["const", "x", "v", "s", "t", "d0", "d1", "sl", "st"],
            "D.": ["const", "x", "v", "s", "t", "wt", "yst"],
        }
        chans = {"bar": ctx.enter_context(nc.semaphore("c_bar"))}
        for pfx, names in phase_chans.items():
            for n in names:
                chans[pfx + n] = ctx.enter_context(nc.semaphore("c_" + pfx + n))
        prog = Prog()
        ENGS = ("sync", "tensor", "vector", "scalar", "gpsimd")

        def barrier_end(phase):
            prog.pfx = ""
            for eng in ENGS:
                def f(E, phase=phase, eng=eng):
                    E.op(lambda e: e.nop(), ch="bar", key=("bar", phase, eng))
                prog.body(eng, f)

        def barrier_start(phase_prev):
            prog.pfx = ""
            for eng in ENGS:
                def f(E, phase_prev=phase_prev):
                    for eng2 in ENGS:
                        E.wait("bar", ("bar", phase_prev, eng2))
                prog.body(eng, f)

        prog.pfx = "A."
        ar.reset()
        emit_l0_attn(nc, prog, ar, aps, pool)
        barrier_end("A")
        barrier_start("A")
        prog.pfx = "B."
        ar.reset()
        emit_l0_ffn(nc, prog, ar, aps, pool)
        barrier_end("B")
        barrier_start("B")
        prog.pfx = "C."
        ar.reset()
        emit_l1_attn(nc, prog, ar, aps, pool)
        barrier_end("C")
        barrier_start("C")
        prog.pfx = "D."
        ar.reset()
        emit_l1_ffn(nc, prog, ar, aps, pool)

        prog.run_plan()
        block = ctx.enter_context(nc.Block())
        prog.run_emit(nc, block, chans)
    return nc


_CACHE = {}


def kernel(**inputs):
    NC = 8
    x = inputs["L0_attn_x"]
    if "fused" not in _CACHE:
        _CACHE["fused"] = build_fused()
    nc = _CACHE["fused"]
    prep = {}
    for pfx, fn in (("A_", prep_l0_attn), ("B_", prep_l0_ffn),
                    ("C_", prep_l1_attn), ("D_", prep_l1_ffn)):
        for k, v in fn(inputs).items():
            prep[pfx + k] = v
    prep["B_wup"] = prep.pop("B_wup")
    in_maps = [dict(x=np.ascontiguousarray(x[b]), **prep) for b in range(NC)]
    res = run_bass_kernel_spmd(nc, in_maps, core_ids=list(range(NC)))
    return np.stack([res.results[b]["y"] for b in range(NC)])
